# version 02
# baseline (speedup 1.0000x reference)
import numpy as np
import ml_dtypes
from contextlib import ExitStack
import concourse.bass as bass
import concourse.mybir as mybir
from concourse.bass_utils import run_bass_kernel_spmd
from concourse.aot_env import allow_ici_all_to_all

F32 = mybir.dt.float32
BF16 = mybir.dt.bfloat16
I32 = mybir.dt.int32
U16 = mybir.dt.uint16
AF = mybir.ActivationFunctionType
ALU = mybir.AluOpType
MAGIC = float(np.float32(1.5 * 2 ** 23))
NEG = -1e30

NCORES = 8
S = 8192
TC = 1024
D = 2048
NT = 16
DSL = 256
NST = 16
H = 16
DH = 128
HFF0 = 5632
HFF1 = 8192
EPS = 1e-5

PHASES = 5
DEBUG = False
BF = ml_dtypes.bfloat16
KBb = 512


def tok_idx(c):
    return np.concatenate([np.arange(64 * (8 * jj + c), 64 * (8 * jj + c) + 64)
                           for jj in range(16)])


class Build:
    def __init__(self, nc, ctx):
        self.nc = nc
        self.ctx = ctx
        self.block = ctx.enter_context(nc.Block())
        self.s = {}

    def sem(self, name):
        if name not in self.s:
            self.s[name] = [self.ctx.enter_context(self.nc.semaphore(name)), 0]
        return self.s[name][0]

    def val(self, name):
        self.sem(name)
        return self.s[name][1]

    def bump(self, name, k=1):
        h = self.sem(name)
        self.s[name][1] += k
        return h, self.s[name][1]

    def dma(self, eng, out, in_, q):
        h, v = self.bump(q, 16)
        eng.dma_start(out=out, in_=in_).then_inc(h, 16)
        return v

    def dmat(self, eng, out, in_, q):
        h, v = self.bump(q, 16)
        eng.dma_start_transpose(out=out, in_=in_).then_inc(h, 16)
        return v

    def mark(self, instr, eng):
        h, v = self.bump(eng, 1)
        instr.then_inc(h, 1)
        return v

    def te(self, fn):
        self.block.tensor(fn)

    def ve(self, fn):
        self.block.vector(fn)

    def se(self, fn):
        self.block.scalar(fn)

    def ge(self, fn):
        self.block.gpsimd(fn)

    def sy(self, fn):
        self.block.sync(fn)

    def barrier(self, names=("tensor", "vector", "scalar", "gpsimd", "sync")):
        vals = {n: self.val("bar_" + n) + 1 for n in names}
        for n in names:
            def mk(eng, me=n):
                eng.nop().then_inc(self.sem("bar_" + me), 1)
                for o in names:
                    if o != me:
                        eng.wait_ge(self.sem("bar_" + o), vals[o])
            getattr(self.block, n)(mk)
        for n in names:
            self.s["bar_" + n][1] += 1


def build_program():
    nc = bass.Bass("TRN2", target_bir_lowering=False, debug=False)
    g = {}

    def din(name, shape, dtype=F32):
        g[name] = nc.dram_tensor(name, shape, dtype, kind="ExternalInput").ap()

    def dout(name, shape, dtype=F32):
        g[name] = nc.dram_tensor(name, shape, dtype, kind="ExternalOutput").ap()

    def dscr(name, shape, dtype):
        g[name] = nc.dram_tensor(name, shape, dtype).ap()

    din("xt", [D, TC])
    din("xd", [DSL, S])
    din("wx", [D, 160])
    din("wdt", [128, DSL])
    din("bdt", [128, 2])
    din("asl", [128, 2 * NST])
    din("dsl", [128, 2])
    din("ones", [128, 128])
    din("ident", [128, 128])
    din("consts", [128, 8])
    if PHASES >= 3:
        din("w1t", [D, HFF0])
        din("w2t", [D, HFF0])
        din("w3t", [HFF0, D])
    if PHASES >= 4:
        din("wqt", [D, D])
        din("wkt", [D, D])
        din("wvt", [D, D])
        din("wot", [D, D])
        din("masks", [32, 128, 512])
        din("eh16", [128, 256])
    if PHASES >= 5:
        din("wgt", [D, HFF1])
        din("wut", [D, HFF1])
        din("wdnt", [HFF1, D])
    dout("yt", [D, TC])
    if DEBUG:
        dout("dbg", [128, 3072])

    dscr("g1i", [TC, 256], BF16)
    dscr("g1o", [S, 256], BF16)
    dscr("g1si", [2, TC], F32)
    dscr("g1so", [2 * NCORES, TC], F32)
    dscr("bcc", [32, S], BF16)
    dscr("stc", [2, S], F32)
    dscr("g2i", [S, DSL], BF16)
    dscr("g2o", [S, DSL], BF16)
    if PHASES >= 3:
        dscr("ddacc", [D, TC], F32)
    if PHASES >= 4:
        dscr("g34i", [TC, 2 * D], BF16)
        dscr("g34o", [S, 2 * D], BF16)
        dscr("g5i", [1, 128], F32)
        dscr("g5o", [NCORES, 128], F32)
        dscr("mkb", [1, 16], F32)
        dscr("rsb", [64, 128], F32)
        dscr("sxrow", [4, TC], F32)
        dscr("qd", [D, TC], BF16)

    ctx = ExitStack()
    with ctx:
        b = Build(nc, ctx)
        E = ctx.enter_context

        X = E(nc.sbuf_tensor("X", [128, NT * TC], F32))
        AR = E(nc.sbuf_tensor("AR", [128, 100 * KBb], BF16, is_transpose_dest=True))
        WX = E(nc.sbuf_tensor("WX", [128, NT * 160], BF16))
        WDT = E(nc.sbuf_tensor("WDT", [128, DSL], BF16))
        ONES = E(nc.sbuf_tensor("ONES", [128, 128], BF16))
        IDENT = E(nc.sbuf_tensor("IDENT", [128, 128], BF16))
        BDT = E(nc.sbuf_tensor("BDT", [128, 2], F32))
        ASL = E(nc.sbuf_tensor("ASL", [128, 2 * NST], F32))
        DS = E(nc.sbuf_tensor("DS", [128, 2], F32))
        HENDW = E(nc.sbuf_tensor("HENDW", [128, 32 * 128], F32))
        HBQN = E(nc.sbuf_tensor("HBQN", [128, 2056], F32))
        CONSTS = E(nc.sbuf_tensor("CONSTS", [128, 8], F32))
        CONST_VALS = [EPS, 1.0 / np.sqrt(128.0), 1.0, 0.0, -1.0]
        for ci, cv in enumerate(CONST_VALS):
            nc.const_aps.register(F32, float(cv), CONSTS[:, ci:ci + 1])
            if float(np.float32(cv)) != float(cv):
                nc.const_aps.register(F32, float(np.float32(cv)), CONSTS[:, ci:ci + 1])
        STG = E(nc.sbuf_tensor("STG", [128, 256], BF16, is_transpose_dest=True))
        EH16 = E(nc.sbuf_tensor("EH16", [128, 256], BF16))
        GAM = None
        DBGT = E(nc.sbuf_tensor("DBGT", [128, 3072], F32)) if DEBUG else None

        def dtap(v, col, src):
            if DEBUG:
                v.tensor_copy(out=DBGT[:, col:col + src.shape[-1]], in_=src)
        PS_A = E(nc.psum_tensor("PS_A", [128, 2048], F32))
        PS_B = E(nc.psum_tensor("PS_B", [128, 1024], F32))
        PS_C = E(nc.psum_tensor("PS_C", [128, 1024], F32))

        def arv(off_kb, nelem, dtype=BF16, p0=0, p1=128):
            i0 = off_kb * KBb
            if dtype == BF16:
                return AR[p0:p1, i0:i0 + nelem]
            nbf = nelem * (4 if dtype in (F32, I32) else 2) // 2
            return AR[p0:p1, i0:i0 + nbf].bitcast(dtype)

        XLN = arv(0, 16 * TC)
        MREP = arv(80, TC, F32)
        RREP = arv(84, TC, F32)
        TMPS = arv(88, TC, F32)
        SQT = [arv(92, 512), arv(93, 512)]
        CONA = arv(94, 1024)
        CONB = arv(96, 1024)
        GB = [arv(98, 256), arv(98 + 1, 256)][0:1] + [AR[:, 98 * KBb + 256:98 * KBb + 512]]

        def xln_t(tj, half):
            o = tj * TC + half * 512
            return XLN[:, o:o + 512] if False else XLN[0:128, o:o + 512]

        def p0(sy):
            b.dma(sy, X[:].rearrange("p (t c) -> p t c", t=NT),
                  g["xt"].rearrange("(t p) c -> p t c", p=128), "ld")
            b.dma(sy, BDT[:], g["bdt"], "ld")
            b.dma(sy, DS[:], g["dsl"], "ld")
            b.dma(sy, ASL[:], g["asl"], "ld")
            b.dma(sy, CONSTS[:], g["consts"], "ld")
            sy.wait_ge(b.sem("ld"), b.val("ld"))
        b.sy(p0)

        def p0g(ge):
            b.dma(ge, WX[:].rearrange("p (t c) -> p t c", t=NT),
                  g["wx"].rearrange("(t p) c -> p t c", p=128), "wld")
            b.dma(ge, WDT[:], g["wdt"], "wld")
            b.dma(ge, ONES[:], g["ones"], "wld")
            b.dma(ge, IDENT[:], g["ident"], "wld")
            if PHASES >= 4:
                b.dma(ge, EH16[:], g["eh16"], "wld")
            ge.wait_ge(b.sem("wld"), b.val("wld"))
        b.ge(p0g)
        b.barrier()

        def ln_stats(tiles_bf, half):
            mrep = MREP[:, half * 512:half * 512 + 512]
            rrep = RREP[:, half * 512:half * 512 + 512]
            tmp = TMPS[:, half * 512:half * 512 + 512]
            se_base = b.val("se")
            te_base = b.val("te")
            ve_guard = b.val("ve")
            for tj in range(NT):
                def sq(sc, tj=tj):
                    if tj == 0:
                        sc.wait_ge(b.sem("ve"), ve_guard)
                    if tj >= 2:
                        sc.wait_ge(b.sem("te"), te_base + tj - 1)
                    i = sc.activation(out=SQT[tj % 2], in_=tiles_bf(tj), func=AF.Square)
                    b.mark(i, "se")
                b.se(sq)

                def mm(t, tj=tj):
                    if tj == 0:
                        t.wait_ge(b.sem("ve"), ve_guard)
                    t.wait_ge(b.sem("se"), se_base + tj + 1)
                    t.matmul(PS_B[:, 0:512], ONES[:], tiles_bf(tj),
                             start=(tj == 0), stop=(tj == NT - 1))
                    i = t.matmul(PS_B[:, 512:1024], ONES[:], SQT[tj % 2],
                                 start=(tj == 0), stop=(tj == NT - 1))
                    b.mark(i, "te")
                b.te(mm)
            vte = b.val("te")
            hold = {}

            def vstat(v):
                v.wait_ge(b.sem("te"), vte)
                v.tensor_scalar(out=mrep, in0=PS_B[:, 0:512], scalar1=1.0 / D,
                                scalar2=None, op0=ALU.mult)
                v.tensor_scalar(out=tmp, in0=PS_B[:, 512:1024], scalar1=1.0 / D,
                                scalar2=None, op0=ALU.mult)
                v.tensor_tensor(out=rrep, in0=mrep, in1=mrep, op=ALU.mult)
                i = v.tensor_tensor(out=tmp, in0=tmp, in1=rrep, op=ALU.subtract)
                hold["v1"] = b.mark(i, "ve")
            b.ve(vstat)

            def sstat(sc):
                sc.wait_ge(b.sem("ve"), hold["v1"])
                i = sc.activation(out=tmp, in_=tmp, func=AF.Sqrt, bias=EPS)
                hold["s1"] = b.mark(i, "se")
            b.se(sstat)

            def vrec(v):
                v.wait_ge(b.sem("se"), hold["s1"])
                i = v.reciprocal(out=rrep, in_=tmp)
                hold["v2"] = b.mark(i, "ve")
            b.ve(vrec)
            return mrep, rrep, hold["v2"]

        ln1_stats = []
        for half in range(2):
            def vcast(v, half=half):
                for tj in range(NT):
                    i = v.tensor_copy(out=xln_t(tj, half),
                                      in_=X[:, tj * TC + half * 512:tj * TC + half * 512 + 512])
                b.mark(i, "ve")
            b.ve(vcast)
            mrep, rrep, vrr = ln_stats(lambda tj, h=half: xln_t(tj, h), half)

            def vnorm(v, half=half, mrep=mrep, rrep=rrep, vrr=vrr):
                v.wait_ge(b.sem("ve"), vrr)
                tmp = TMPS[:, half * 512:half * 512 + 512]
                for tj in range(NT):
                    xsl = X[:, tj * TC + half * 512:tj * TC + half * 512 + 512]
                    v.tensor_tensor(out=tmp, in0=xsl, in1=mrep, op=ALU.subtract)
                    i = v.tensor_tensor(out=xln_t(tj, half), in0=tmp, in1=rrep, op=ALU.mult)
                b.mark(i, "ve")
            b.ve(vnorm)
            ln1_stats.append((mrep, rrep))

        def vtap1(v):
            pass
        b.ve(vtap1)
        b.barrier()

        for half in range(2):
            ve_guard = b.val("ve")

            def temm(t, half=half, ve_guard=ve_guard):
                t.wait_ge(b.sem("ve"), ve_guard)
                for tj in range(NT):
                    t.matmul(PS_A[:, 0:512], WX[:, tj * 160:tj * 160 + 128],
                             xln_t(tj, half), start=(tj == 0), stop=(tj == NT - 1))
                    i = t.matmul(PS_C[0:32, 0:512], WX[:, tj * 160 + 128:tj * 160 + 160],
                                 xln_t(tj, half), start=(tj == 0), stop=(tj == NT - 1))
                b.mark(i, "te")
            b.te(temm)
            vte = b.val("te")
            mrep, rrep = ln1_stats[half]

            def vevac(v, half=half, vte=vte):
                v.wait_ge(b.sem("te"), vte)
                v.tensor_copy(out=CONA[:, half * 512:half * 512 + 512], in_=PS_A[:, 0:512])
                i = v.tensor_copy(out=CONB[0:32, half * 512:half * 512 + 512], in_=PS_C[0:32, 0:512])
                b.mark(i, "ve")
            b.ve(vevac)

            def systat(sy, half=half, mrep=mrep, rrep=rrep):
                sy.wait_ge(b.sem("ve"), b.val("ve"))
                b.dma(sy, g["g1si"][0:1, half * 512:half * 512 + 512], mrep[0:1, :], "st")
                b.dma(sy, g["g1si"][1:2, half * 512:half * 512 + 512], rrep[0:1, :], "st")
            b.sy(systat)
        vcon = b.val("ve")

        def syt(sy):
            sy.wait_ge(b.sem("ve"), vcon)
            for tb in range(8):
                b.dmat(sy, STG[:, 0:128], CONA[:, tb * 128:tb * 128 + 128], "tp")
                vt = b.dmat(sy, STG[:, 128:256], CONB[:, tb * 128:tb * 128 + 128], "tp")
                sy.wait_ge(b.sem("tp"), vt)
                vs = b.dma(sy, g["g1i"][tb * 128:tb * 128 + 128, :], STG[:], "st")
                sy.wait_ge(b.sem("st"), vs)
        b.sy(syt)

        vsy_h, vsy = b.bump("sy")

        def symark(sy):
            sy.nop().then_inc(vsy_h, 1)
        b.sy(symark)

        def gcc1b(ge):
            ge.wait_ge(vsy_h, vsy)
            i = ge.collective_compute(
                "AllGather", ALU.bypass, replica_groups=[list(range(NCORES))],
                ins=[g["g1i"].opt()], outs=[g["g1o"].opt()])
            h, _ = b.bump("cc", 1)
            i.then_inc(h, 1)
            i = ge.collective_compute(
                "AllGather", ALU.bypass, replica_groups=[list(range(NCORES))],
                ins=[g["g1si"].opt()], outs=[g["g1so"].opt()])
            h, _ = b.bump("cc", 1)
            i.then_inc(h, 1)
            ge.wait_ge(b.sem("cc"), b.val("cc"))
            stv = g["stc"].rearrange("n (i1 r i0) -> n r i1 i0", r=8, i0=64)
            for r in range(NCORES):
                sv = g["g1so"].rearrange("n (i1 i0) -> n i1 i0", i0=64)
                b.dma(ge, stv[0:1, r], sv[2 * r:2 * r + 1], "wld")
                b.dma(ge, stv[1:2, r], sv[2 * r + 1:2 * r + 2], "wld")
            ge.wait_ge(b.sem("wld"), b.val("wld"))
        b.ge(gcc1b)
        vcc1 = b.val("cc")
        vstc = b.val("wld")

        DTFM = arv(32, 8192)
        BCFM = arv(48, 8192)

        def p2b(sy):
            sy.wait_ge(b.sem("cc"), vcc1)
            for blk in range(64):
                gb = GB[blk % 2]
                if blk >= 2:
                    sy.wait_ge(b.sem("tp"), b.val("tp"))
                vl = b.dma(sy, gb, g["g1o"][blk * 128:blk * 128 + 128, :], "ld")
                sy.wait_ge(b.sem("ld"), vl)
                b.dmat(sy, DTFM[:, blk * 128:blk * 128 + 128], gb[:, 0:128], "tp")
                b.dmat(sy, BCFM[:, blk * 128:blk * 128 + 128], gb[:, 128:256], "tp")
            sy.wait_ge(b.sem("tp"), b.val("tp"))
            bco = g["bcc"].rearrange("n (i1 r i0) -> n r i1 i0", r=8, i0=64)
            bcs = BCFM.rearrange("p (r i1 i0) -> p r i1 i0", r=8, i0=64)
            for r in range(8):
                b.dma(sy, bco[:, r], bcs[0:32, r], "st")
            sy.wait_ge(b.sem("st"), b.val("st"))
            sy.nop().then_inc(b.sem("sy"), 1)
        b.sy(p2b)
        b.s["sy"][1] += 1
        vsy2 = b.val("sy")

        b.barrier()

        QW = 2048
        NQ = 4
        SD = 2052
        DELTAP = arv(0, 2 * NQ * SD)

        def dchunk(tj, q):
            o = (tj * NQ + q) * SD
            return DELTAP[:, o:o + 2050]

        def vpfx(v):
            for tj in range(2):
                for q in range(NQ):
                    o = (tj * NQ + q) * SD
                    i = v.memset(DELTAP[:, o:o + 2], 30000.0)
            b.mark(i, "ve")
        b.ve(vpfx)

        dtv = DTFM.rearrange("p (r i1 i0) -> p r i1 i0", r=8, i0=64)
        se_base = b.val("se")
        ve_base_c = b.val("ve")
        for k in range(32):
            tj2, ch = divmod(k, 16)

            def dmm(t, k=k, tj2=tj2, ch=ch):
                if k >= 2:
                    t.wait_ge(b.sem("se"), se_base + k - 1)
                    t.wait_ge(b.sem("ve"), ve_base_c + k - 1)
                i = t.matmul(PS_B[:, (k % 2) * 512:(k % 2) * 512 + 512],
                             WDT[:, tj2 * 128:tj2 * 128 + 128], dtv[:, :, ch, :],
                             start=True, stop=True)
                b.mark(i, "te")
            b.te(dmm)
            vte = b.val("te")

            def dact(sc, k=k, tj2=tj2, ch=ch, vte=vte):
                sc.wait_ge(b.sem("te"), vte)
                if k >= 2:
                    sc.wait_ge(b.sem("ve"), ve_base_c + k - 1)
                ps = PS_B[:, (k % 2) * 512:(k % 2) * 512 + 512]
                spt = MREP[:, (k % 2) * 512:(k % 2) * 512 + 512]
                sc.activation(out=spt, in_=ps, func=AF.Abs, bias=BDT[:, tj2:tj2 + 1])
                sc.activation(out=spt, in_=spt, func=AF.Exp, scale=-1.0)
                i = sc.activation(out=spt, in_=spt, func=AF.Ln, bias=1.0)
                b.mark(i, "se")
            b.se(dact)
            vse_sp = b.val("se")

            def dve_sp(v, k=k, tj2=tj2, ch=ch, vse_sp=vse_sp):
                v.wait_ge(b.sem("se"), vse_sp)
                ps = PS_B[:, (k % 2) * 512:(k % 2) * 512 + 512]
                spt = MREP[:, (k % 2) * 512:(k % 2) * 512 + 512]
                spt2 = RREP[:, (k % 2) * 512:(k % 2) * 512 + 512]
                v.tensor_scalar(out=spt2, in0=ps, scalar1=BDT[:, tj2:tj2 + 1],
                                scalar2=0.0, op0=ALU.add, op1=ALU.max)
                dout_ = dchunk(tj2, ch // 4)
                off = 2 + (ch % 4) * 512
                i = v.tensor_tensor(out=dout_[:, off:off + 512], in0=spt,
                                    in1=spt2, op=ALU.add)
                b.mark(i, "ve")
            b.ve(dve_sp)

        b.barrier()

        XSQ = arv(33, QW)
        UQP = arv(37, 2050)
        YQ = PS_A[:, 0:QW]
        ADQ = [arv(50, 2050), arv(55, 2050)]
        BXQ = arv(60, 2050)
        XDQ = arv(50, QW, F32)
        HBQ = HBQN[:, 0:2050]
        BMBP = [arv(66, 2050), arv(71, 2050)]
        CBB = [arv(76, QW), arv(80, QW)]
        CSQ = arv(84, 2052, F32)
        ZD = arv(93, 2050)
        MEANB = arv(84, QW, F32)
        RSTDB = arv(66, QW, F32)
        PQ = [arv(33, QW), ZD[:, 2:2050]]

        def vhz(v):
            v.memset(HENDW[:], 0.0)
            v.memset(UQP[:, 0:2], 0.0)
            i = v.memset(ZD[:, 0:2], 0.0)
            b.mark(i, "ve")
        b.ve(vhz)
        b.barrier(("vector", "sync", "scalar"))

        for q in range(NQ):
            for tj in range(2):
                def ldq(sy, q=q, tj=tj):
                    b.dma(sy, XDQ, g["xd"][tj * 128:tj * 128 + 128, q * QW:q * QW + QW], "ld")
                    b.dma(sy, MEANB, g["stc"][0:1, q * QW:q * QW + QW].partition_broadcast(128), "ld")
                    b.dma(sy, RSTDB, g["stc"][1:2, q * QW:q * QW + QW].partition_broadcast(128), "ld")
                b.sy(ldq)
                vld = b.val("ld")

                def vxs(v, tj=tj, q=q, vld=vld):
                    v.wait_ge(b.sem("ld"), vld)
                    v.tensor_tensor(out=XSQ, in0=XDQ, in1=MEANB, op=ALU.subtract)
                    v.tensor_tensor(out=XSQ, in0=XSQ, in1=RSTDB, op=ALU.mult)
                    v.tensor_scalar(out=YQ, in0=XSQ, scalar1=DS[:, tj:tj + 1],
                                    scalar2=None, op0=ALU.mult)
                    v.tensor_tensor(out=UQP[:, 2:2050], in0=dchunk(tj, q)[:, 2:2050],
                                    in1=XSQ, op=ALU.mult)
                    v.tensor_copy(out=ZD[:, 2:2050], in_=dchunk(tj, q)[:, 2:2050])
                    i = v.tensor_tensor_scan(out=CSQ[:, 0:2050], data0=ZD, data1=ZD,
                                             initial=0.0, op0=ALU.add, op1=ALU.bypass)
                    b.mark(i, "ve")
                b.ve(vxs)
                ve_q0 = b.val("ve")

                for n in range(NST):
                    pb = n % 2
                    slot = tj * NST + n

                    def ldbc(sy, q=q, n=n, pb=pb, ve_q0=ve_q0):
                        if n < 2:
                            sy.wait_ge(b.sem("ve"), ve_q0)
                        else:
                            sy.wait_ge(b.sem("ve"), ve_q0 + n - 1)
                        b.dma(sy, BMBP[pb][:, 2:2050],
                              g["bcc"][n:n + 1, q * QW:q * QW + QW].partition_broadcast(128), "bc%d" % pb)
                        b.dma(sy, CBB[pb],
                              g["bcc"][16 + n:17 + n, q * QW:q * QW + QW].partition_broadcast(128), "bc%d" % pb)
                    b.sy(ldbc)
                    vbc = b.val("bc%d" % pb)

                    def sad(sc, tj=tj, n=n, q=q, pb=pb, ve_q0=ve_q0):
                        if n == 0:
                            sc.wait_ge(b.sem("ve"), ve_q0)
                        if n >= 2:
                            sc.wait_ge(b.sem("ve"), ve_q0 + n - 1)
                        sc.activation(out=ADQ[pb], in_=dchunk(tj, q),
                                      func=AF.Exp, scale=ASL[:, tj * NST + n:tj * NST + n + 1])
                        i = sc.activation(out=PQ[pb], in_=CSQ[:, 2:2050],
                                          func=AF.Exp, scale=ASL[:, tj * NST + n:tj * NST + n + 1])
                        b.mark(i, "se")
                    b.se(sad)
                    vse = b.val("se")

                    def vscan(v, tj=tj, n=n, pb=pb, vbc=vbc, vse=vse, q=q, slot=slot):
                        v.wait_ge(b.sem("bc%d" % pb), vbc)
                        v.wait_ge(b.sem("se"), vse)
                        v.tensor_tensor(out=BXQ, in0=UQP, in1=BMBP[pb], op=ALU.mult)
                        v.tensor_tensor_scan(out=HBQ, data0=ADQ[pb], data1=BXQ,
                                             initial=0.0, op0=ALU.mult, op1=ALU.add)
                        hsb = HENDW[:, slot * 128 + 127:slot * 128 + 128].broadcast_to((128, QW))
                        v.tensor_tensor(out=BXQ[:, 2:2050], in0=PQ[pb], in1=hsb, op=ALU.mult)
                        v.tensor_tensor(out=HBQ[:, 2:2050], in0=HBQ[:, 2:2050],
                                        in1=BXQ[:, 2:2050], op=ALU.add)
                        v.tensor_copy(out=HENDW[:, slot * 128:slot * 128 + 128],
                                      in_=HBQ[:, 2050 - 128:2050])
                        v.tensor_tensor(out=HBQ[:, 2:2050], in0=HBQ[:, 2:2050], in1=CBB[pb], op=ALU.mult)
                        i = v.tensor_tensor(out=YQ, in0=YQ, in1=HBQ[:, 2:2050], op=ALU.add)
                        b.mark(i, "ve")
                    b.ve(vscan)

                def vyc(v, q=q, tj=tj):
                    i = v.tensor_copy(out=XSQ, in_=YQ)
                    b.mark(i, "ve")
                b.ve(vyc)
                vym = b.val("ve")

                def syy(sy, q=q, tj=tj, vym=vym):
                    sy.wait_ge(b.sem("ve"), vym)
                    for blk in range(QW // 128):
                        sy.wait_ge(b.sem("st"), b.val("st"))
                        vtt = b.dmat(sy, STG[:, (blk % 2) * 128:(blk % 2) * 128 + 128],
                                     XSQ[:, blk * 128:blk * 128 + 128], "tp")
                        sy.wait_ge(b.sem("tp"), vtt)
                        t128 = q * (QW // 128) + blk
                        for k2 in range(2):
                            g64 = (2 * t128 + k2) % 8
                            jj = (2 * t128 + k2) // 8
                            b.dma(sy, g["g2i"][g64 * 1024 + jj * 64:g64 * 1024 + jj * 64 + 64,
                                               tj * 128:tj * 128 + 128],
                                  STG[k2 * 64:k2 * 64 + 64, (blk % 2) * 128:(blk % 2) * 128 + 128], "st")
                b.sy(syy)
                b.barrier(("vector", "sync", "scalar"))

        h_sy, v_sy = b.bump("sy")

        def symark2(sy):
            sy.wait_ge(b.sem("st"), b.val("st"))
            sy.nop().then_inc(h_sy, 1)
        b.sy(symark2)

        def gcc2b(ge):
            ge.wait_ge(h_sy, v_sy)
            with allow_ici_all_to_all():
                i = ge.collective_compute(
                    "AllToAll", ALU.bypass, replica_groups=[list(range(NCORES))],
                    ins=[g["g2i"].opt()], outs=[g["g2o"].opt()])
            h, _ = b.bump("cc", 1)
            i.then_inc(h, 1)
        b.ge(gcc2b)
        vcc2 = b.val("cc")

        YF = arv(32, 16 * TC)

        def p2e(sy):
            sy.wait_ge(b.sem("cc"), vcc2)
            for k in range(64):
                r, tb = divmod(k, 8)
                gb = GB[k % 2]
                if k >= 2:
                    sy.wait_ge(b.sem("tp"), b.val("tp"))
                vl = b.dma(sy, gb, g["g2o"][r * 1024 + tb * 128:r * 1024 + tb * 128 + 128, :], "ld")
                sy.wait_ge(b.sem("ld"), vl)
                for dd in range(2):
                    gt = 2 * r + dd
                    b.dmat(sy, YF[:, gt * TC + tb * 128:gt * TC + tb * 128 + 128],
                           gb[:, dd * 128:dd * 128 + 128], "tp")
            sy.wait_ge(b.sem("tp"), b.val("tp"))
            sy.nop().then_inc(b.sem("sy"), 1)
        b.sy(p2e)
        b.s["sy"][1] += 1
        vsy3 = b.val("sy")

        def yf_t(tj, half):
            o = tj * TC + half * 512
            return YF[0:128, o:o + 512]


        for half in range(2):
            if half == 0:
                def wv(v):
                    v.wait_ge(b.sem("sy"), vsy3)
                b.ve(wv)

                def ws(sc):
                    sc.wait_ge(b.sem("sy"), vsy3)
                b.se(ws)

                def wt(t):
                    t.wait_ge(b.sem("sy"), vsy3)
                b.te(wt)
            mrep, rrep, vrr = ln_stats(lambda tj, h=half: yf_t(tj, h), half)

            def vres(v, half=half, mrep=mrep, rrep=rrep, vrr=vrr):
                v.wait_ge(b.sem("ve"), vrr)
                tmp = TMPS[:, half * 512:half * 512 + 512]
                for tj in range(NT):
                    xsl = X[:, tj * TC + half * 512:tj * TC + half * 512 + 512]
                    v.tensor_tensor(out=tmp, in0=yf_t(tj, half), in1=mrep, op=ALU.subtract)
                    v.tensor_tensor(out=tmp, in0=tmp, in1=rrep, op=ALU.mult)
                    i = v.tensor_tensor(out=xsl, in0=xsl, in1=tmp, op=ALU.add)
                b.mark(i, "ve")
            b.ve(vres)
        b.barrier()

        if PHASES >= 3:
            WFA = [arv(0, 5632, F32), arv(23, 5632, F32)]
            ABSB = arv(46, 5632)
            EVB = arv(57, 128)
            GSC = [arv(58, 128, F32), arv(59, 128, F32), arv(60, 128, F32)]
            wspec = [("w1t", D, HFF0), ("w2t", D, HFF0), ("w3t", HFF0, D)]
            for wi, (wn, wrows, wcols) in enumerate(wspec):
                nj = wrows // 128
                ncb = wcols // 128
                se_b = b.val("se")
                te_b = b.val("te")
                wv = g[wn].rearrange("(j p) c -> p j c", p=128)
                loadv = []
                for cb in range(ncb):
                    def gl(ge, cb=cb, nj=nj, wv=wv, se_b=se_b):
                        if cb >= 2:
                            ge.wait_ge(b.sem("se"), se_b + cb - 1)
                        v = b.dma(ge, WFA[cb % 2][:, 0:nj * 128].rearrange("p (j c) -> p j c", c=128),
                                  wv[:, :, cb * 128:cb * 128 + 128], "wld")
                        loadv.append(v)
                    b.ge(gl)

                    def sab(sc, cb=cb, nj=nj, te_b=te_b):
                        sc.wait_ge(b.sem("wld"), loadv[cb])
                        if cb >= 1:
                            sc.wait_ge(b.sem("te"), te_b + cb)
                        i = sc.activation(out=ABSB[:, 0:nj * 128], in_=WFA[cb % 2][:, 0:nj * 128],
                                          func=AF.Abs)
                        b.mark(i, "se")
                    b.se(sab)
                    vse = b.val("se")

                    def tmm(t, cb=cb, nj=nj, vse=vse, ncb=ncb):
                        t.wait_ge(b.sem("se"), vse)
                        for j in range(nj):
                            i = t.matmul(PS_B[:, 0:128], ABSB[:, j * 128:j * 128 + 128],
                                         ONES[:, 0:128], start=(cb == 0 and j == 0),
                                         stop=(cb == ncb - 1 and j == nj - 1))
                        b.mark(i, "te")
                    b.te(tmm)
                vte = b.val("te")

                def vev(v, vte=vte):
                    v.wait_ge(b.sem("te"), vte)
                    i = v.tensor_copy(out=EVB, in_=PS_B[:, 0:128])
                    b.mark(i, "ve")
                b.ve(vev)
                vv = b.val("ve")

                def tm2(t, vv=vv):
                    t.wait_ge(b.sem("ve"), vv)
                    i = t.matmul(PS_C[:, 0:128], ONES[:, 0:128], EVB, start=True, stop=True)
                    b.mark(i, "te")
                b.te(tm2)
                vte2 = b.val("te")

                def vg(v, wi=wi, vte2=vte2, numel=wrows * wcols):
                    v.wait_ge(b.sem("te"), vte2)
                    v.tensor_scalar(out=GSC[wi], in0=PS_C[:, 0:128], scalar1=1.0 / numel,
                                    scalar2=1e-8, op0=ALU.mult, op1=ALU.add)
                    i = v.reciprocal(out=GSC[wi], in_=GSC[wi])
                    b.mark(i, "ve")
                    dtap(v, 128 * wi, GSC[wi])
                b.ve(vg)
            b.barrier()

            def xln3_t(tj, half):
                o = tj * TC + half * 512
                return XLN[0:128, o:o + 512]
            for half in range(2):
                def vcast3(v, half=half):
                    for tj in range(NT):
                        i = v.tensor_copy(out=xln3_t(tj, half),
                                          in_=X[:, tj * TC + half * 512:tj * TC + half * 512 + 512])
                    b.mark(i, "ve")
                b.ve(vcast3)
                mrep, rrep, vrr = ln_stats(lambda tj, h=half: xln3_t(tj, h), half)

                def vnorm3(v, half=half, mrep=mrep, rrep=rrep, vrr=vrr):
                    v.wait_ge(b.sem("ve"), vrr)
                    tmp = TMPS[:, half * 512:half * 512 + 512]
                    for tj in range(NT):
                        xsl = X[:, tj * TC + half * 512:tj * TC + half * 512 + 512]
                        v.tensor_tensor(out=tmp, in0=xsl, in1=mrep, op=ALU.subtract)
                        i = v.tensor_tensor(out=xln3_t(tj, half), in0=tmp, in1=rrep, op=ALU.mult)
                    b.mark(i, "ve")
                b.ve(vnorm3)
            b.barrier()

            WF1 = [arv(33, 2048, F32), arv(41, 2048, F32)]
            WF2 = [arv(49, 2048, F32), arv(61, 2048, F32)]
            WQT = arv(69, 2048)
            WQT2 = arv(73, 2048)
            SG = arv(77, 512)
            HIDC = arv(78, 11 * 1024)
            w1v = g["w1t"].rearrange("(j p) c -> p j c", p=128)
            w2v = g["w2t"].rearrange("(j p) c -> p j c", p=128)
            w3v = g["w3t"].rearrange("(j p) c -> p j c", p=128)
            for ch in range(4):
                for oi in range(11):
                    ob = ch * 11 + oi
                    def gl2(ge, ob=ob):
                        if ob >= 2:
                            ge.wait_ge(b.sem("ve"), b.val("ve"))
                        b.dma(ge, WF1[ob % 2][:].rearrange("p (j c) -> p j c", c=128),
                              w1v[:, :, ob * 128:ob * 128 + 128], "wld")
                        b.dma(ge, WF2[ob % 2][:].rearrange("p (j c) -> p j c", c=128),
                              w2v[:, :, ob * 128:ob * 128 + 128], "wld")
                    b.ge(gl2)
                    vl2 = b.val("wld")

                    def vq(v, ob=ob, vl2=vl2):
                        v.wait_ge(b.sem("wld"), vl2)
                        if ob >= 1:
                            v.wait_ge(b.sem("te"), b.val("te"))
                        for wf, gsc, wqt in ((WF1[ob % 2], GSC[0], WQT), (WF2[ob % 2], GSC[1], WQT2)):
                            gb = gsc[:, 0:1].broadcast_to((128, 2048))
                            v.tensor_tensor(out=wf[:], in0=wf[:], in1=gb, op=ALU.mult)
                            v.tensor_scalar(out=wf[:], in0=wf[:], scalar1=MAGIC, scalar2=MAGIC,
                                            op0=ALU.add, op1=ALU.subtract)
                            i = v.tensor_scalar(out=wqt[:], in0=wf[:], scalar1=-1.0, scalar2=1.0,
                                                op0=ALU.max, op1=ALU.min)
                        b.mark(i, "ve")
                    b.ve(vq)
                    vqm = b.val("ve")

                    def tffn(t, ob=ob, vqm=vqm):
                        t.wait_ge(b.sem("ve"), vqm)
                        t.wait_ge(b.sem("se"), b.val("se"))
                        for half in range(2):
                            for i_ in range(NT):
                                t.matmul(PS_A[:, half * 512:half * 512 + 512],
                                         WQT[:, i_ * 128:i_ * 128 + 128],
                                         xln3_t(i_, half), start=(i_ == 0), stop=(i_ == NT - 1))
                            for i_ in range(NT):
                                i = t.matmul(PS_A[:, 1024 + half * 512:1024 + half * 512 + 512],
                                             WQT2[:, i_ * 128:i_ * 128 + 128],
                                             xln3_t(i_, half), start=(i_ == 0), stop=(i_ == NT - 1))
                        b.mark(i, "te")
                    b.te(tffn)
                    vtf = b.val("te")

                    for half in range(2):
                        def sgev(sc, vtf=vtf, half=half):
                            if half == 0:
                                sc.wait_ge(b.sem("te"), vtf)
                            if half == 1:
                                sc.wait_ge(b.sem("ve"), b.val("ve"))
                            i = sc.activation(out=SG[:, 0:512], in_=PS_A[:, half * 512:half * 512 + 512],
                                              func=AF.Sigmoid)
                            b.mark(i, "se")
                        b.se(sgev)
                        vsg = b.val("se")

                        def vhid(v, oi=oi, half=half, vsg=vsg):
                            v.wait_ge(b.sem("se"), vsg)
                            i2 = v.tensor_tensor(out=HIDC[:, oi * 1024 + half * 512:oi * 1024 + half * 512 + 512],
                                                 in0=PS_A[:, 1024 + half * 512:1024 + half * 512 + 512],
                                                 in1=SG[:, 0:512], op=ALU.mult)
                            b.mark(i2, "ve")
                        b.ve(vhid)
                WF3 = [arv(33, 1408, F32), arv(39, 1408, F32)]
                WQT3 = arv(45, 1408)
                ESB = arv(49, 1024, F32)
                ESB = arv(69, 1024, F32)
                for dt in range(16):
                    def gl3(ge, ch=ch, dt=dt):
                        ge.wait_ge(b.sem("ve"), b.val("ve"))
                        b.dma(ge, WF3[dt % 2][:].rearrange("p (j c) -> p j c", c=128),
                              w3v[:, 11 * ch:11 * ch + 11, dt * 128:dt * 128 + 128], "wld")
                    b.ge(gl3)
                    vl3 = b.val("wld")

                    def vq3(v, dt=dt, vl3=vl3):
                        v.wait_ge(b.sem("wld"), vl3)
                        if dt >= 1:
                            v.wait_ge(b.sem("te"), b.val("te"))
                        wf = WF3[dt % 2]
                        gb = GSC[2][:, 0:1].broadcast_to((128, 1408))
                        v.tensor_tensor(out=wf[:], in0=wf[:], in1=gb, op=ALU.mult)
                        v.tensor_scalar(out=wf[:], in0=wf[:], scalar1=MAGIC, scalar2=MAGIC,
                                        op0=ALU.add, op1=ALU.subtract)
                        i = v.tensor_scalar(out=WQT3[:], in0=wf[:], scalar1=-1.0, scalar2=1.0,
                                            op0=ALU.max, op1=ALU.min)
                        b.mark(i, "ve")
                    b.ve(vq3)
                    vq3m = b.val("ve")

                    def tdn(t, dt=dt, vq3m=vq3m):
                        t.wait_ge(b.sem("ve"), vq3m)
                        for half in range(2):
                            for ii in range(11):
                                i = t.matmul(PS_B[:, half * 512:half * 512 + 512],
                                             WQT3[:, ii * 128:ii * 128 + 128],
                                             HIDC[:, ii * 1024 + half * 512:ii * 1024 + half * 512 + 512],
                                             start=(ii == 0), stop=(ii == 10))
                        b.mark(i, "te")
                    b.te(tdn)
                    vtd = b.val("te")

                    def vev3(v, dt=dt, vtd=vtd):
                        v.wait_ge(b.sem("te"), vtd)
                        v.wait_ge(b.sem("ac"), b.val("ac"))
                        i = v.tensor_copy(out=ESB, in_=PS_B[:, 0:1024])
                        b.mark(i, "ve")
                    b.ve(vev3)
                    vem = b.val("ve")

                    def gac(ge, ch=ch, dt=dt, vem=vem):
                        ge.wait_ge(b.sem("ve"), vem)
                        if ch == 0:
                            b.dma(ge, g["ddacc"][dt * 128:dt * 128 + 128, :], ESB, "ac")
                        else:
                            h2, _ = b.bump("ac", 16)
                            ge.dma_start(out=g["ddacc"][dt * 128:dt * 128 + 128, :], in_=ESB,
                                         accum_op=ALU.add).then_inc(h2, 16)
                    b.ge(gac)

                def gacw0(ge):
                    ge.wait_ge(b.sem("ac"), b.val("ac"))
                b.ge(gacw0)
                if ch == 0:
                    def vtapH(v):
                        dtap(v, 384, HIDC[:, 0:512])
                    b.ve(vtapH)
                b.barrier(("vector", "tensor", "scalar", "gpsimd"))

            h_ac, v_ac = b.bump("ge")

            def gacw(ge):
                ge.wait_ge(b.sem("ac"), b.val("ac"))
                YB3 = arv(0, 16 * TC)
                b.dma(ge, YB3.rearrange("p (t c) -> p t c", t=NT),
                      g["ddacc"].rearrange("(t p) c -> p t c", p=128), "wld")
                ge.wait_ge(b.sem("wld"), b.val("wld"))
                ge.nop().then_inc(h_ac, 1)
            b.ge(gacw)
            YB3 = arv(0, 16 * TC)

            def yb3_t(tj, half):
                o = tj * TC + half * 512
                return YB3[0:128, o:o + 512]

            for en in ("vector", "scalar", "tensor"):
                def wv3(eng, en=en):
                    eng.wait_ge(h_ac, v_ac)
                getattr(b.block, en)(wv3)

            def vtapY(v):
                dtap(v, 896, YB3[0:128, 0:512])
            b.ve(vtapY)

            for half in range(2):
                mrep, rrep, vrr = ln_stats(lambda tj, h=half: yb3_t(tj, h), half)

                def vres3(v, half=half, mrep=mrep, rrep=rrep, vrr=vrr):
                    v.wait_ge(b.sem("ve"), vrr)
                    tmp = TMPS[:, half * 512:half * 512 + 512]
                    for tj in range(NT):
                        xsl = X[:, tj * TC + half * 512:tj * TC + half * 512 + 512]
                        v.tensor_tensor(out=tmp, in0=yb3_t(tj, half), in1=mrep, op=ALU.subtract)
                        v.tensor_tensor(out=tmp, in0=tmp, in1=rrep, op=ALU.mult)
                        i = v.tensor_tensor(out=xsl, in0=xsl, in1=tmp, op=ALU.add)
                    b.mark(i, "ve")
                b.ve(vres3)
            b.barrier()

        if PHASES >= 4:
            WFA4 = [arv(0, 2048, F32), arv(8, 2048, F32)]
            ABSB4 = arv(16, 2048)
            EVB4 = arv(20, 128)
            EVL4 = arv(20, 256)[:, 128:256]
            ACC4 = HBQN[:, 0:128]
            ACC4B = HBQN[:, 128:256]
            GSW = [HBQN[:, 1024 + i4 * 128:1024 + i4 * 128 + 128] for i4 in range(4)]
            GIV = [HBQN[:, 1536 + i4 * 128:1536 + i4 * 128 + 128] for i4 in range(4)]
            for wi, wn in enumerate(["wqt", "wkt", "wvt", "wot"]):
                se_b = b.val("se")
                te_b = b.val("te")
                wv4 = g[wn].rearrange("(j p) c -> p j c", p=128)
                loadv4 = []
                for cb in range(16):
                    def gl4(ge, cb=cb, wv4=wv4, se_b=se_b):
                        if cb >= 2:
                            ge.wait_ge(b.sem("se"), se_b + cb - 1)
                        else:
                            ge.wait_ge(b.sem("se"), se_b)
                        v = b.dma(ge, WFA4[cb % 2][:].rearrange("p (j c) -> p j c", c=128),
                                  wv4[:, :, cb * 128:cb * 128 + 128], "wld")
                        loadv4.append(v)
                    b.ge(gl4)

                    def sab4(sc, cb=cb, te_b=te_b):
                        sc.wait_ge(b.sem("wld"), loadv4[cb])
                        if cb >= 1:
                            sc.wait_ge(b.sem("te"), te_b + cb)
                        else:
                            sc.wait_ge(b.sem("te"), te_b)
                        i = sc.activation(out=ABSB4[:], in_=WFA4[cb % 2][:], func=AF.Abs)
                        b.mark(i, "se")
                    b.se(sab4)
                    vse4 = b.val("se")

                    def tmm4(t, cb=cb, vse4=vse4):
                        t.wait_ge(b.sem("se"), vse4)
                        t.wait_ge(b.sem("ve"), b.val("ve"))
                        for j in range(16):
                            i = t.matmul(PS_B[:, 0:128], ABSB4[:, j * 128:j * 128 + 128],
                                         ONES[:, 0:128], start=(j == 0), stop=(j == 15))
                        b.mark(i, "te")
                    b.te(tmm4)
                    vtc4 = b.val("te")

                    def vac4(v, cb=cb, vtc4=vtc4):
                        v.wait_ge(b.sem("te"), vtc4)
                        if cb == 0:
                            i = v.tensor_copy(out=ACC4, in_=PS_B[:, 0:128])
                        else:
                            i = v.tensor_tensor(out=ACC4, in0=ACC4,
                                                in1=PS_B[:, 0:128], op=ALU.add)
                        b.mark(i, "ve")
                    b.ve(vac4)

                def vhl4(v):
                    v.tensor_copy(out=EVB4, in_=ACC4)
                    v.tensor_copy(out=ACC4B, in_=EVB4)
                    v.tensor_tensor(out=ACC4B, in0=ACC4, in1=ACC4B, op=ALU.subtract)
                    i = v.tensor_copy(out=EVL4, in_=ACC4B)
                    b.mark(i, "ve")
                b.ve(vhl4)
                vv4 = b.val("ve")

                def tm24(t, vv4=vv4):
                    t.wait_ge(b.sem("ve"), vv4)
                    t.matmul(PS_C[:, 0:128], ONES[:, 0:128], EVB4, start=True, stop=False)
                    i = t.matmul(PS_C[:, 0:128], ONES[:, 0:128], EVL4, start=False, stop=True)
                    b.mark(i, "te")
                b.te(tm24)
                vte24 = b.val("te")

                def vg4(v, wi=wi, vte24=vte24):
                    v.wait_ge(b.sem("te"), vte24)
                    v.tensor_scalar(out=GIV[wi], in0=PS_C[:, 0:128], scalar1=1.0 / (D * D),
                                    scalar2=1e-5, op0=ALU.mult, op1=ALU.max)
                    i = v.reciprocal(out=GSW[wi], in_=GIV[wi])
                    b.mark(i, "ve")
                b.ve(vg4)
                b.barrier(("vector", "scalar", "tensor", "gpsimd"))
            b.barrier()

            def xcast5_t(tj, half):
                o = tj * TC + half * 512
                return XLN[0:128, o:o + 512]
            ln5 = []
            for half in range(2):
                def vcast5(v, half=half):
                    for tj in range(NT):
                        i = v.tensor_copy(out=xcast5_t(tj, half),
                                          in_=X[:, tj * TC + half * 512:tj * TC + half * 512 + 512])
                    b.mark(i, "ve")
                b.ve(vcast5)
                mrep, rrep, vrr = ln_stats(lambda tj, h=half: xcast5_t(tj, h), half)
                ln5.append((mrep, rrep, vrr))
                AMX = arv(48, 512, F32)
                TT5 = arv(50, 512, F32)
                HI5 = arv(54, 512)
                LO5 = arv(55, 512)
                TH5 = arv(56, 128)
                TL5 = arv(57, 128)
                FR5 = arv(58, 128, F32)

                AMN = arv(52, 512, F32)

                def vamax(v, half=half, mrep=mrep, rrep=rrep, vrr=vrr):
                    v.wait_ge(b.sem("ve"), vrr)
                    for tj in range(NT):
                        xsl = X[:, tj * TC + half * 512:tj * TC + half * 512 + 512]
                        v.tensor_tensor(out=TT5, in0=xsl, in1=mrep, op=ALU.subtract)
                        if tj == 0:
                            v.tensor_copy(out=AMX, in_=TT5)
                            v.tensor_copy(out=AMN, in_=TT5)
                        else:
                            v.tensor_tensor(out=AMX, in0=AMX, in1=TT5, op=ALU.max)
                            v.tensor_tensor(out=AMN, in0=AMN, in1=TT5, op=ALU.min)
                    v.tensor_scalar(out=AMN, in0=AMN, scalar1=-1.0, scalar2=None, op0=ALU.mult)
                    v.tensor_tensor(out=AMX, in0=AMX, in1=AMN, op=ALU.max)
                    v.tensor_tensor(out=AMX, in0=AMX, in1=rrep, op=ALU.mult)
                    for stp in (64, 32):
                        v.tensor_copy(out=TT5[0:stp, :], in_=AMX[stp:2 * stp, :])
                        v.tensor_tensor(out=AMX[0:stp, :], in0=AMX[0:stp, :],
                                        in1=TT5[0:stp, :], op=ALU.max)
                    v.tensor_copy(out=HI5[0:32, :], in_=AMX[0:32, :])
                    v.tensor_copy(out=TT5[0:32, :], in_=HI5[0:32, :])
                    v.tensor_tensor(out=TT5[0:32, :], in0=AMX[0:32, :],
                                    in1=TT5[0:32, :], op=ALU.subtract)
                    i = v.tensor_copy(out=LO5[0:32, :], in_=TT5[0:32, :])
                    b.mark(i, "ve")
                b.ve(vamax)
                vam = b.val("ve")

                def sytr5(sy, vam=vam):
                    sy.wait_ge(b.sem("ve"), vam)
                    for ci in range(4):
                        b.dmat(sy, TH5[:, ci * 32:ci * 32 + 32],
                               HI5[0:32, ci * 128:ci * 128 + 128], "tp")
                        b.dmat(sy, TL5[:, ci * 32:ci * 32 + 32],
                               LO5[0:32, ci * 128:ci * 128 + 128], "tp")
                    sy.wait_ge(b.sem("tp"), b.val("tp"))
                    sy.nop().then_inc(b.sem("sy"), 1)
                b.sy(sytr5)
                b.s["sy"][1] += 1
                vtr5 = b.val("sy")

                def vfr5(v, vtr5=vtr5, half=half):
                    v.wait_ge(b.sem("sy"), vtr5)
                    v.tensor_tensor(out=FR5, in0=TH5[:, 0:128], in1=TL5[:, 0:128], op=ALU.add)
                    v.tensor_scalar(out=FR5, in0=FR5, scalar1=1e-5,
                                    scalar2=None, op0=ALU.max)
                    i = v.reduce_max(out=TT5[:, 0:4],
                                     in_=FR5.rearrange("p (ci c) -> p ci c", ci=4),
                                     axis=mybir.AxisListType.X)
                    if half == 0:
                        dtap(v, 2944, FR5)
                    b.mark(i, "ve")
                b.ve(vfr5)
                vfr5m = b.val("ve")

                def sysx(sy, half=half, vfr5m=vfr5m):
                    sy.wait_ge(b.sem("ve"), vfr5m)
                    vs = None
                    for ci in range(4):
                        cX = half * 512 + ci * 128
                        vs = b.dma(sy, g["sxrow"][0:1, cX:cX + 128]
                                   .rearrange("a (bb c) -> (a bb) c", c=1),
                                   TT5[:, ci:ci + 1], "st")
                    sy.wait_ge(b.sem("st"), vs)
                b.sy(sysx)
            b.barrier(("vector", "sync", "scalar"))

            SXB = arv(44, TC, F32)
            SXIW = [arv(32, TC, F32), arv(36, TC, F32), arv(40, TC, F32)]

            def ldsx(sy):
                v = b.dma(sy, SXB, g["sxrow"][0:1, :].partition_broadcast(128), "ld")
                sy.wait_ge(b.sem("ld"), v)
                sy.nop().then_inc(b.sem("sy"), 1)
            b.sy(ldsx)
            b.s["sy"][1] += 1
            vsx = b.val("sy")

            def vsxc(v):
                v.wait_ge(b.sem("sy"), vsx)
                v.tensor_scalar(out=SXIW[0], in0=SXB, scalar1=1.0 / 127.0,
                                scalar2=None, op0=ALU.mult)
                v.reciprocal(out=SXB, in_=SXIW[0])
                i = v.tensor_copy(out=AMX, in_=AMX)
                dtap(v, 2816, SXB[:, 0:128])
                b.mark(i, "ve")
            b.ve(vsxc)

            for half in range(2):
                mrep, rrep, vrr = ln5[half]

                def vxint(v, half=half, mrep=mrep, rrep=rrep):
                    TT5b = arv(59, 512, F32)
                    v.tensor_tensor(out=TT5b, in0=rrep, in1=SXB[:, half * 512:half * 512 + 512], op=ALU.mult)
                    for tj in range(NT):
                        xsl = X[:, tj * TC + half * 512:tj * TC + half * 512 + 512]
                        TT5c = arv(61, 512, F32)
                        v.tensor_tensor(out=TT5c, in0=xsl, in1=mrep, op=ALU.subtract)
                        v.tensor_tensor(out=TT5c, in0=TT5c, in1=TT5b, op=ALU.mult)
                        v.tensor_scalar(out=TT5c, in0=TT5c, scalar1=MAGIC, scalar2=MAGIC,
                                        op0=ALU.add, op1=ALU.subtract)
                        i = v.tensor_scalar(out=xcast5_t(tj, half), in0=TT5c, scalar1=-128.0,
                                            scalar2=127.0, op0=ALU.max, op1=ALU.min)
                    if half == 0:
                        dtap(v, 0, xcast5_t(0, 0))
                    b.mark(i, "ve")
                b.ve(vxint)
            b.barrier()

            def vsxiw(v):
                gb1 = GIV[1][:, 0:1].broadcast_to((128, TC))
                gb2 = GIV[2][:, 0:1].broadcast_to((128, TC))
                gb0 = GIV[0][:, 0:1].broadcast_to((128, TC))
                v.tensor_tensor(out=SXIW[1], in0=SXIW[0], in1=gb1, op=ALU.mult)
                v.tensor_tensor(out=SXIW[2], in0=SXIW[0], in1=gb2, op=ALU.mult)
                i = v.tensor_tensor(out=SXIW[0], in0=SXIW[0], in1=gb0, op=ALU.mult)
                b.mark(i, "ve")
            b.ve(vsxiw)
            b.barrier(("vector", "sync", "gpsimd", "tensor", "scalar"))

            WF4 = [arv(45, 2048, F32), arv(53, 2048, F32)]
            WQ4 = arv(61, 2048)
            QKV = arv(65, 1024)
            KSQ = arv(67, 1024)
            KN2 = arv(69, 128, F32)
            STG4 = STG
            wviews = [g["wqt"].rearrange("(j p) c -> p j c", p=128),
                      g["wkt"].rearrange("(j p) c -> p j c", p=128),
                      g["wvt"].rearrange("(j p) c -> p j c", p=128)]
            for wi in range(3):
                for ob in range(16):
                    def gl5(ge, wi=wi, ob=ob):
                        ge.wait_ge(b.sem("ve"), b.val("ve"))
                        b.dma(ge, WF4[ob % 2][:].rearrange("p (j c) -> p j c", c=128),
                              wviews[wi][:, :, ob * 128:ob * 128 + 128], "wld")
                    b.ge(gl5)
                    vl5 = b.val("wld")

                    def vq5(v, wi=wi, ob=ob, vl5=vl5):
                        v.wait_ge(b.sem("wld"), vl5)
                        if not (wi == 0 and ob == 0):
                            v.wait_ge(b.sem("te"), b.val("te"))
                        wf = WF4[ob % 2]
                        gb = GSW[wi][:, 0:1].broadcast_to((128, 2048))
                        v.tensor_tensor(out=wf[:], in0=wf[:], in1=gb, op=ALU.mult)
                        v.tensor_scalar(out=wf[:], in0=wf[:], scalar1=MAGIC, scalar2=MAGIC,
                                        op0=ALU.add, op1=ALU.subtract)
                        i = v.tensor_scalar(out=WQ4[:], in0=wf[:], scalar1=-1.0, scalar2=1.0,
                                            op0=ALU.max, op1=ALU.min)
                        b.mark(i, "ve")
                    b.ve(vq5)
                    vqm5 = b.val("ve")

                    def tqkv(t, wi=wi, ob=ob, vqm5=vqm5):
                        t.wait_ge(b.sem("ve"), vqm5)
                        t.wait_ge(b.sem("se"), b.val("se"))
                        for half in range(2):
                            for i_ in range(NT):
                                i = t.matmul(PS_B[:, half * 512:half * 512 + 512],
                                             WQ4[:, i_ * 128:i_ * 128 + 128],
                                             xcast5_t(i_, half), start=(i_ == 0), stop=(i_ == NT - 1))
                        b.mark(i, "te")
                    b.te(tqkv)
                    vtq = b.val("te")

                    def vevq(v, wi=wi, ob=ob, vtq=vtq):
                        v.wait_ge(b.sem("te"), vtq)
                        v.wait_ge(b.sem("st"), b.val("st"))
                        v.wait_ge(b.sem("tp"), b.val("tp"))
                        i = v.tensor_tensor(out=QKV[:, 0:1024],
                                            in0=PS_B[:, 0:1024], in1=SXIW[wi], op=ALU.mult)
                        if wi == 0 and ob == 0:
                            dtap(v, 512, QKV[:, 0:512])
                        b.mark(i, "ve")
                    b.ve(vevq)
                    vem5 = b.val("ve")

                    if wi == 0:
                        def syq(sy, ob=ob, vem5=vem5):
                            sy.wait_ge(b.sem("ve"), vem5)
                            b.dma(sy, g["qd"][ob * 128:ob * 128 + 128, :], QKV[:, 0:1024], "st")
                        b.sy(syq)
                    else:
                        def syk(sy, wi=wi, ob=ob, vem5=vem5):
                            sy.wait_ge(b.sem("ve"), vem5)
                            for tb in range(8):
                                sy.wait_ge(b.sem("st"), b.val("st"))
                                vtt = b.dmat(sy, STG4[:, (tb % 2) * 128:(tb % 2) * 128 + 128],
                                             QKV[:, tb * 128:tb * 128 + 128], "tp")
                                sy.wait_ge(b.sem("tp"), vtt)
                                b.dma(sy, g["g34i"][tb * 128:tb * 128 + 128,
                                                    (wi - 1) * D + ob * 128:(wi - 1) * D + ob * 128 + 128],
                                      STG4[:, (tb % 2) * 128:(tb % 2) * 128 + 128], "st")
                        b.sy(syk)
                    if wi == 1:
                        def skn(sc, ob=ob, vem5=vem5):
                            sc.wait_ge(b.sem("ve"), vem5)
                            i = sc.activation(out=KSQ[:, 0:1024], in_=QKV[:, 0:1024], func=AF.Square)
                            b.mark(i, "se")
                        b.se(skn)
                        vkn = b.val("se")

                        def tkn(t, vkn=vkn, ob=ob):
                            t.wait_ge(b.sem("se"), vkn)
                            t.matmul(PS_C[:, 512:1024], ONES[:], KSQ[:, 0:512], start=True, stop=True)
                            i = t.matmul(PS_C[:, 512:1024], ONES[:], KSQ[:, 512:1024],
                                         start=False, stop=True)
                            b.mark(i, "te")
                        b.te(tkn)
                        vkt = b.val("te")

                        def vkn2(v, ob=ob, vkt=vkt):
                            v.wait_ge(b.sem("te"), vkt)
                            KW = arv(70, 512, F32)
                            v.tensor_copy(out=KW[:, 0:512], in_=PS_C[:, 512:1024])
                            for w2 in (256, 128, 64, 32, 16, 8):
                                v.tensor_tensor(out=KW[:, 0:w2], in0=KW[:, 0:w2],
                                                in1=KW[:, w2:2 * w2], op=ALU.max)
                            i = v.tensor_copy(out=KN2[:, ob * 8:ob * 8 + 8], in_=KW[:, 0:8])
                            b.mark(i, "ve")
                        b.ve(vkn2)
            vknd = b.val("ve")

            def sykn(sy, vknd=vknd):
                sy.wait_ge(b.sem("ve"), vknd)
                vs = b.dma(sy, g["g5i"], KN2[0:1, :], "st")
                sy.wait_ge(b.sem("st"), vs)
                sy.nop().then_inc(b.sem("sy"), 1)
            b.sy(sykn)
            b.s["sy"][1] += 1
            vstg = b.val("sy")

            def gcc4(ge, vstg=vstg):
                ge.wait_ge(b.sem("sy"), vstg)
                i = ge.collective_compute(
                    "AllGather", ALU.bypass, replica_groups=[list(range(NCORES))],
                    ins=[g["g34i"].opt()], outs=[g["g34o"].opt()])
                h2, _ = b.bump("cc", 1)
                i.then_inc(h2, 1)
                i = ge.collective_compute(
                    "AllGather", ALU.bypass, replica_groups=[list(range(NCORES))],
                    ins=[g["g5i"].opt()], outs=[g["g5o"].opt()])
                h2, _ = b.bump("cc", 1)
                i.then_inc(h2, 1)
            b.ge(gcc4)
            vcc4 = b.val("cc")

            KNR = HENDW[:, 656:784]
            KGB = HENDW[:, 640:656]

            def ldknr(sy, vcc4=vcc4):
                sy.wait_ge(b.sem("cc"), vcc4)
                v = b.dma(sy, KNR[0:8, 0:128], g["g5o"], "ld")
                sy.wait_ge(b.sem("ld"), v)
                sy.nop().then_inc(b.sem("sy"), 1)
            b.sy(ldknr)
            b.s["sy"][1] += 1
            vknl = b.val("sy")

            KNRED = HENDW[:, 576:577]

            def vknf(v, vknl=vknl):
                v.wait_ge(b.sem("sy"), vknl)
                i = v.reduce_max(out=KNRED[0:8, 0:1], in_=KNR[0:8, 0:128],
                                 axis=mybir.AxisListType.X)
                b.mark(i, "ve")
            b.ve(vknf)
            vknm = b.val("ve")

            def syknb(sy, vknm=vknm):
                sy.wait_ge(b.sem("ve"), vknm)
                vs = b.dma(sy, g["mkb"][0:1, 0:8].rearrange("a (bb c) -> (a bb) c", c=1),
                           KNRED[0:8, 0:1], "st")
                sy.wait_ge(b.sem("st"), vs)
                v2 = b.dma(sy, KGB[:, 0:8],
                           g["mkb"][0:1, 0:8].partition_broadcast(128), "ld")
                sy.wait_ge(b.sem("ld"), v2)
                sy.nop().then_inc(b.sem("sy"), 1)
            b.sy(syknb)
            b.s["sy"][1] += 1
            b.barrier()
            if DEBUG:
                def vtapG(v):
                    dtap(v, 2688, KGB)
                    for i4 in range(4):
                        v.tensor_copy(out=DBGT[:, 2704 + i4:2705 + i4], in_=GIV[i4][:, 0:1])
                        v.tensor_copy(out=HENDW[:, 2640:3152], in_=HENDW[:, 2640:3152])
                        v.tensor_copy(out=DBGT[:, 2708 + i4:2709 + i4], in_=GSW[i4][:, 0:1])
                        v.tensor_copy(out=HENDW[:, 2640:3152], in_=HENDW[:, 2640:3152])
                b.ve(vtapG)

            QT = arv(0, 16 * 512)
            LBK = [arv(16, 2048), arv(20, 2048)]
            CNT = arv(16, 256)
            KTA = arv(24, 16 * 512)
            VTA = arv(40, 4 * 2048)
            PBUF = arv(56, 4 * 512)
            PTB = arv(60, 2048)
            MSKB = arv(64, 2048)
            OAC = arv(68, 16 * 512, F32)
            RSCUR = HENDW[:, 512:576]
            RS64 = HENDW[:, 576:640]
            CM = HENDW[:, 784:1296]
            CMX = HENDW[:, 1296:1808]
            QSQ2 = HENDW[:, 1808:2320].bitcast(BF16)
            CNB = HENDW[:, 2320:2576].bitcast(BF16)
            CNEGF = HENDW[:, 2576:2640]
            CT1 = HENDW[:, 2640:3152]
            AMO = HENDW[:, 3152:3664]
            SOB = HENDW[:, 784:1296]
            SOIW = HENDW[:, 1296:1808]
            RSI = HENDW[:, 0:512]

            for qb in range(2):
                NKB = 8 * (qb + 1)

                def ldq4(sy, qb=qb):
                    v = b.dma(sy, QT.rearrange("p (t c) -> p t c", t=16),
                              g["qd"][:, qb * 512:qb * 512 + 512]
                              .rearrange("(t p) c -> p t c", p=128), "ld")
                    sy.wait_ge(b.sem("ld"), v)
                    sy.nop().then_inc(b.sem("sy"), 1)
                b.sy(ldq4)
                b.s["sy"][1] += 1
                vq4 = b.val("sy")

                te_q = b.val("te")
                for h in range(H):
                    def sqsq(sc, h=h, vq4=vq4, te_q=te_q):
                        if h == 0:
                            sc.wait_ge(b.sem("sy"), vq4)
                        if h >= 2:
                            sc.wait_ge(b.sem("te"), te_q + h - 1)
                        i = sc.activation(out=QSQ2[:, (h % 2) * 512:(h % 2) * 512 + 512],
                                          in_=QT[:, h * 512:h * 512 + 512], func=AF.Square)
                        b.mark(i, "se")
                    b.se(sqsq)
                    vsq = b.val("se")

                    def tqn(t, h=h, vsq=vsq):
                        t.wait_ge(b.sem("se"), vsq)
                        i = t.matmul(PS_A[0:16, 0:512], EH16[:, h * 16:h * 16 + 16],
                                     QSQ2[:, (h % 2) * 512:(h % 2) * 512 + 512],
                                     start=(h == 0), stop=(h == H - 1))
                        b.mark(i, "te")
                    b.te(tqn)
                vtqn = b.val("te")

                def vcm(v, vtqn=vtqn):
                    v.wait_ge(b.sem("te"), vtqn)
                    v.tensor_copy(out=CM[0:16, :], in_=PS_A[0:16, 0:512])
                    for s in range(8):
                        kb_ = KGB[0:16, s:s + 1].broadcast_to((16, 512))
                        if s == 0:
                            i = v.tensor_tensor(out=CMX[0:16, :], in0=CM[0:16, :],
                                                in1=kb_, op=ALU.mult)
                        else:
                            v.tensor_tensor(out=CT1[0:16, :], in0=CM[0:16, :],
                                            in1=kb_, op=ALU.mult)
                            i = v.tensor_tensor(out=CMX[0:16, :], in0=CMX[0:16, :],
                                                in1=CT1[0:16, :], op=ALU.max)
                    b.mark(i, "ve")
                b.ve(vcm)
                vcmm = b.val("ve")

                def scm(sc, vcmm=vcmm):
                    sc.wait_ge(b.sem("ve"), vcmm)
                    i = sc.activation(out=CMX[0:16, :], in_=CMX[0:16, :], func=AF.Sqrt)
                    b.mark(i, "se")
                b.se(scm)
                vscm = b.val("se")

                def vcneg(v, vscm=vscm):
                    v.wait_ge(b.sem("se"), vscm)
                    i = v.tensor_scalar(out=CNB[0:16, 0:512], in0=CMX[0:16, :],
                                        scalar1=float(-1.015625 / np.sqrt(128.0)),
                                        scalar2=None, op0=ALU.mult)
                    b.mark(i, "ve")
                b.ve(vcneg)
                vcnb = b.val("ve")

                def sytr(sy, vcnb=vcnb):
                    sy.wait_ge(b.sem("ve"), vcnb)
                    for j in range(4):
                        b.dmat(sy, CNT[:, j * 16:j * 16 + 16],
                               CNB[0:16, j * 128:j * 128 + 128], "tp")
                    sy.wait_ge(b.sem("tp"), b.val("tp"))
                    sy.nop().then_inc(b.sem("sy"), 1)
                b.sy(sytr)
                b.s["sy"][1] += 1
                vctr = b.val("sy")

                def vcf(v, vctr=vctr, qb=qb):
                    v.wait_ge(b.sem("sy"), vctr)
                    i = v.tensor_copy(out=CNEGF, in_=CNT[:, 0:64])
                    if qb == 0:
                        dtap(v, 2560, CNEGF)
                    b.mark(i, "ve")
                b.ve(vcf)
                b.barrier()

                AVM = []
                for kb in range(NKB):
                    m, r = kb // 8, kb % 8
                    partial = (m == qb)

                    for s2 in range(4):
                        def ldkv(sy, s2=s2, m=m, r=r, kb=kb):
                            if s2 == 0 and kb > 0:
                                sy.wait_ge(b.sem("te"), b.val("te"))
                            if kb > 0 or s2 >= 2:
                                sy.wait_ge(b.sem("tp"), b.val("tp"))
                            rowb = r * 1024 + 512 * m + 128 * s2
                            vk = b.dma(sy, LBK[s2 % 2], g["g34o"][rowb:rowb + 128, 0:D], "ld")
                            b.dma(sy, VTA[:, s2 * 2048:s2 * 2048 + 2048],
                                  g["g34o"][rowb:rowb + 128, D:2 * D], "ld")
                            sy.wait_ge(b.sem("ld"), vk)
                            for h in range(H):
                                b.dmat(sy, KTA[:, h * 512 + s2 * 128:h * 512 + s2 * 128 + 128],
                                       LBK[s2 % 2][:, h * 128:h * 128 + 128], "tp")
                        b.sy(ldkv)
                    if partial:
                        def ldmsk(ge, r=r):
                            ge.wait_ge(b.sem("te"), b.val("te"))
                            b.dma(ge, MSKB.rearrange("p (j c) -> p j c", j=4),
                                  g["masks"][r * 4:r * 4 + 4].rearrange("j p c -> p j c"),
                                  "wld")
                        b.ge(ldmsk)
                    vload = (b.val("ld"), b.val("tp"), b.val("wld"))
                    if DEBUG and qb == 0 and kb == 0:
                        def vtapK(v, vload=vload):
                            v.wait_ge(b.sem("ld"), vload[0])
                            v.wait_ge(b.sem("tp"), vload[1])
                            dtap(v, 1024, KTA[:, 0:512])
                        b.ve(vtapK)

                    for h in range(H):
                        def tsc(t, h=h, partial=partial, vload=vload):
                            if h == 0:
                                t.wait_ge(b.sem("ld"), vload[0])
                                t.wait_ge(b.sem("tp"), vload[1])
                                if partial:
                                    t.wait_ge(b.sem("wld"), vload[2])
                            t.wait_ge(b.sem("se"), b.val("se"))
                            for j in range(4):
                                i = t.matmul(PS_A[:, j * 512:j * 512 + 512],
                                             QT[:, h * 512 + j * 128:h * 512 + j * 128 + 128],
                                             KTA[:, h * 512:h * 512 + 512],
                                             start=True, stop=(not partial))
                                if partial:
                                    i = t.matmul(PS_A[:, j * 512:j * 512 + 512], IDENT[:],
                                                 MSKB[:, j * 512:j * 512 + 512],
                                                 start=False, stop=True)
                            b.mark(i, "te")
                        b.te(tsc)
                        vts = b.val("te")

                        for j in range(4):
                            def sexp(sc, h=h, j=j, vts=vts, qb=qb, kb=kb):
                                if j == 0:
                                    sc.wait_ge(b.sem("te"), vts)
                                    sc.wait_ge(b.sem("tp"), b.val("tp"))
                                    if h == 0:
                                        sc.wait_ge(b.sem("ve"), b.val("ve"))
                                    if DEBUG and qb == 0 and kb == 0 and h == 1:
                                        sc.wait_ge(b.sem("ve"), b.val("ve"))
                                i = sc.activation(out=PBUF[:, j * 512:j * 512 + 512],
                                                  in_=PS_A[:, j * 512:j * 512 + 512],
                                                  func=AF.Exp,
                                                  scale=float(1.0 / np.sqrt(128.0)),
                                                  bias=CNEGF[:, j * 16 + h:j * 16 + h + 1],
                                                  accum_out=RSCUR[:, h * 4 + j:h * 4 + j + 1])
                                b.mark(i, "se")
                            b.se(sexp)
                        vex = b.val("se")
                        if DEBUG and qb == 0 and kb == 0 and h == 0:
                            def vtapP(v, vex=vex):
                                v.wait_ge(b.sem("se"), vex)
                                dtap(v, 1536, PBUF[:, 0:512])
                                i = v.tensor_copy(out=DBGT[:, 0:1], in_=DBGT[:, 0:1])
                                b.mark(i, "ve")
                            b.ve(vtapP)

                        def sytp(sy, vex=vex):
                            sy.wait_ge(b.sem("se"), vex)
                            for j in range(4):
                                for s2 in range(4):
                                    b.dmat(sy,
                                           PTB[:, s2 * 512 + j * 128:s2 * 512 + j * 128 + 128],
                                           PBUF[:, j * 512 + s2 * 128:j * 512 + s2 * 128 + 128],
                                           "tp")
                        b.sy(sytp)
                        vptp = b.val("tp")

                        def tav(t, h=h, vptp=vptp, n=len(AVM)):
                            t.wait_ge(b.sem("tp"), vptp)
                            if n >= 2:
                                t.wait_ge(b.sem("ve"), AVM[n - 2])
                            pb2 = (h % 2) * 512
                            for s2 in range(4):
                                i = t.matmul(PS_B[:, pb2:pb2 + 512],
                                             VTA[:, s2 * 2048 + h * 128:s2 * 2048 + h * 128 + 128],
                                             PTB[:, s2 * 512:s2 * 512 + 512],
                                             start=(s2 == 0), stop=(s2 == 3))
                            b.mark(i, "te")
                        b.te(tav)
                        vta2 = b.val("te")

                        def vac(v, h=h, kb=kb, vta2=vta2):
                            v.wait_ge(b.sem("te"), vta2)
                            pb2 = (h % 2) * 512
                            if kb == 0:
                                i = v.tensor_copy(out=OAC[:, h * 512:h * 512 + 512],
                                                  in_=PS_B[:, pb2:pb2 + 512])
                            else:
                                i = v.tensor_tensor(out=OAC[:, h * 512:h * 512 + 512],
                                                    in0=OAC[:, h * 512:h * 512 + 512],
                                                    in1=PS_B[:, pb2:pb2 + 512], op=ALU.add)
                            AVM.append(b.mark(i, "ve"))
                        b.ve(vac)

                    def vrsk(v, kb=kb):
                        v.wait_ge(b.sem("se"), b.val("se"))
                        if kb == 0:
                            i = v.tensor_copy(out=RS64, in_=RSCUR)
                        else:
                            i = v.tensor_tensor(out=RS64, in0=RS64, in1=RSCUR, op=ALU.add)
                        b.mark(i, "ve")
                    b.ve(vrsk)

                def vrsf(v, qb=qb):
                    v.wait_ge(b.sem("se"), b.val("se"))
                    if qb == 0:
                        dtap(v, 2624, RS64)
                    v.reciprocal(out=CT1[:, 0:64], in_=RS64)
                    v.memset(CT1[:, 64:128], 0.0)
                    v.tensor_copy(out=CNT[:, 0:128], in_=CT1[:, 0:128])
                    v.tensor_copy(out=CT1[:, 128:256], in_=CNT[:, 0:128])
                    v.tensor_tensor(out=CT1[:, 256:384], in0=CT1[:, 0:128],
                                    in1=CT1[:, 128:256], op=ALU.subtract)
                    i = v.tensor_copy(out=CNT[:, 128:256], in_=CT1[:, 256:384])
                    b.mark(i, "ve")
                b.ve(vrsf)
                vrsm = b.val("ve")

                def syhl(sy, vrsm=vrsm):
                    sy.wait_ge(b.sem("ve"), vrsm)
                    b.dmat(sy, QT[:, 0:128], CNT[:, 0:128], "tp")
                    b.dmat(sy, QT[:, 128:256], CNT[:, 128:256], "tp")
                    sy.wait_ge(b.sem("tp"), b.val("tp"))
                    sy.nop().then_inc(b.sem("sy"), 1)
                b.sy(syhl)
                b.s["sy"][1] += 1
                vhl = b.val("sy")

                def vrst(v, vhl=vhl):
                    v.wait_ge(b.sem("sy"), vhl)
                    i = v.tensor_tensor(out=CT1[0:64, 384:512], in0=QT[0:64, 0:128],
                                        in1=QT[0:64, 128:256], op=ALU.add)
                    b.mark(i, "ve")
                b.ve(vrst)
                vrtm = b.val("ve")

                def syrs(sy, vrtm=vrtm):
                    sy.wait_ge(b.sem("ve"), vrtm)
                    vs = b.dma(sy, g["rsb"], CT1[0:64, 384:512], "st")
                    sy.wait_ge(b.sem("st"), vs)
                    sy.nop().then_inc(b.sem("sy"), 1)
                b.sy(syrs)
                b.s["sy"][1] += 1
                vrsy = b.val("sy")

                for h in range(H):
                    def ldrsi(sy, h=h, vrsy=vrsy):
                        if h == 0:
                            sy.wait_ge(b.sem("sy"), vrsy)
                        sy.wait_ge(b.sem("ve"), b.val("ve"))
                        rsv = g["rsb"].rearrange("(hh j) q -> hh (j q)", hh=16)
                        v2 = b.dma(sy, RSI, rsv[h:h + 1, :].partition_broadcast(128), "ld")
                        sy.wait_ge(b.sem("ld"), v2)
                        sy.nop().then_inc(b.sem("sy"), 1)
                    b.sy(ldrsi)
                    b.s["sy"][1] += 1
                    vrl = b.val("sy")

                    def vnrm(v, h=h, vrl=vrl, qb=qb):
                        v.wait_ge(b.sem("sy"), vrl)
                        i = v.tensor_tensor(out=OAC[:, h * 512:h * 512 + 512],
                                            in0=OAC[:, h * 512:h * 512 + 512],
                                            in1=RSI, op=ALU.mult)
                        if qb == 0 and h == 0:
                            dtap(v, 2048, OAC[:, 0:512])
                        b.mark(i, "ve")
                    b.ve(vnrm)
                vnd = b.val("ve")

                HIQ = arv(16, 512)
                LOQ = arv(17, 512)
                THQ = arv(18, 128)
                TLQ = arv(19, 128)
                FRQ = arv(20, 128, F32)

                AMN4 = HENDW[:, 784:1296]

                def vamax4(v, vnd=vnd):
                    v.wait_ge(b.sem("ve"), vnd)
                    v.tensor_copy(out=AMO, in_=OAC[:, 0:512])
                    v.tensor_copy(out=AMN4, in_=OAC[:, 0:512])
                    for h in range(1, H):
                        v.tensor_tensor(out=AMO, in0=AMO,
                                        in1=OAC[:, h * 512:h * 512 + 512], op=ALU.max)
                        v.tensor_tensor(out=AMN4, in0=AMN4,
                                        in1=OAC[:, h * 512:h * 512 + 512], op=ALU.min)
                    v.tensor_scalar(out=AMN4, in0=AMN4, scalar1=-1.0, scalar2=None, op0=ALU.mult)
                    v.tensor_tensor(out=AMO, in0=AMO, in1=AMN4, op=ALU.max)
                    for stp in (64, 32):
                        v.tensor_copy(out=CT1[0:stp, :], in_=AMO[stp:2 * stp, :])
                        v.tensor_tensor(out=AMO[0:stp, :], in0=AMO[0:stp, :],
                                        in1=CT1[0:stp, :], op=ALU.max)
                    v.tensor_copy(out=HIQ[0:32, :], in_=AMO[0:32, :])
                    v.tensor_copy(out=CT1[0:32, :], in_=HIQ[0:32, :])
                    v.tensor_tensor(out=CT1[0:32, :], in0=AMO[0:32, :],
                                    in1=CT1[0:32, :], op=ALU.subtract)
                    i = v.tensor_copy(out=LOQ[0:32, :], in_=CT1[0:32, :])
                    b.mark(i, "ve")
                b.ve(vamax4)
                vam4 = b.val("ve")

                def sytr4(sy, vam4=vam4):
                    sy.wait_ge(b.sem("ve"), vam4)
                    for ci in range(4):
                        b.dmat(sy, THQ[:, ci * 32:ci * 32 + 32],
                               HIQ[0:32, ci * 128:ci * 128 + 128], "tp")
                        b.dmat(sy, TLQ[:, ci * 32:ci * 32 + 32],
                               LOQ[0:32, ci * 128:ci * 128 + 128], "tp")
                    sy.wait_ge(b.sem("tp"), b.val("tp"))
                    sy.nop().then_inc(b.sem("sy"), 1)
                b.sy(sytr4)
                b.s["sy"][1] += 1
                vtr4 = b.val("sy")

                def vfr4(v, vtr4=vtr4, qb=qb):
                    v.wait_ge(b.sem("sy"), vtr4)
                    v.tensor_tensor(out=FRQ, in0=THQ[:, 0:128], in1=TLQ[:, 0:128], op=ALU.add)
                    v.tensor_scalar(out=FRQ, in0=FRQ, scalar1=1e-5,
                                    scalar2=None, op0=ALU.max)
                    i = v.reduce_max(out=CT1[:, 0:4],
                                     in_=FRQ.rearrange("p (ci c) -> p ci c", ci=4),
                                     axis=mybir.AxisListType.X)
                    if qb == 0:
                        dtap(v, 2720, FRQ[:, 0:32])
                    b.mark(i, "ve")
                b.ve(vfr4)
                vfr4m = b.val("ve")

                def syso(sy, qb=qb, vfr4m=vfr4m):
                    sy.wait_ge(b.sem("ve"), vfr4m)
                    vs = None
                    for ci in range(4):
                        cX = qb * 512 + ci * 128
                        vs = b.dma(sy, g["sxrow"][2:3, cX:cX + 128]
                                   .rearrange("a (bb c) -> (a bb) c", c=1),
                                   CT1[:, ci:ci + 1], "st")
                    sy.wait_ge(b.sem("st"), vs)
                    v3 = b.dma(sy, SOB,
                               g["sxrow"][2:3, qb * 512:qb * 512 + 512].partition_broadcast(128), "ld")
                    sy.wait_ge(b.sem("ld"), v3)
                    sy.nop().then_inc(b.sem("sy"), 1)
                b.sy(syso)
                b.s["sy"][1] += 1
                vso = b.val("sy")

                def vsoi(v, vso=vso, qb=qb):
                    v.wait_ge(b.sem("sy"), vso)
                    v.tensor_scalar(out=SOIW, in0=SOB, scalar1=1.0 / 127.0,
                                    scalar2=None, op0=ALU.mult)
                    v.reciprocal(out=SOB, in_=SOIW)
                    gb = GIV[3][:, 0:1].broadcast_to((128, 512))
                    i = v.tensor_tensor(out=SOIW, in0=SOIW, in1=gb, op=ALU.mult)
                    if qb == 0:
                        dtap(v, 2712, SOB[:, 0:4])
                        dtap(v, 2716, SOIW[:, 0:4])
                    b.mark(i, "ve")
                b.ve(vsoi)

                for h in range(H):
                    def vxi2(v, h=h, qb=qb):
                        v.tensor_tensor(out=CT1, in0=OAC[:, h * 512:h * 512 + 512],
                                        in1=SOB, op=ALU.mult)
                        v.tensor_scalar(out=CT1, in0=CT1, scalar1=MAGIC, scalar2=MAGIC,
                                        op0=ALU.add, op1=ALU.subtract)
                        i = v.tensor_scalar(out=QT[:, h * 512:h * 512 + 512], in0=CT1,
                                            scalar1=-128.0, scalar2=127.0,
                                            op0=ALU.max, op1=ALU.min)
                        if qb == 0 and h == 0:
                            dtap(v, 2752, QT[:, 0:64])
                        b.mark(i, "ve")
                    b.ve(vxi2)

                wov = g["wot"].rearrange("(j p) c -> p j c", p=128)
                for ob in range(16):
                    def glo(ge, ob=ob):
                        ge.wait_ge(b.sem("ve"), b.val("ve"))
                        b.dma(ge, WF4[ob % 2][:].rearrange("p (j c) -> p j c", c=128),
                              wov[:, :, ob * 128:ob * 128 + 128], "wld")
                    b.ge(glo)
                    vlo = b.val("wld")

                    def vqo(v, ob=ob, vlo=vlo):
                        v.wait_ge(b.sem("wld"), vlo)
                        if ob >= 1:
                            v.wait_ge(b.sem("te"), b.val("te"))
                        wf = WF4[ob % 2]
                        gb = GSW[3][:, 0:1].broadcast_to((128, 2048))
                        v.tensor_tensor(out=wf[:], in0=wf[:], in1=gb, op=ALU.mult)
                        v.tensor_scalar(out=wf[:], in0=wf[:], scalar1=MAGIC, scalar2=MAGIC,
                                        op0=ALU.add, op1=ALU.subtract)
                        i = v.tensor_scalar(out=WQ4[:], in0=wf[:], scalar1=-1.0, scalar2=1.0,
                                            op0=ALU.max, op1=ALU.min)
                        b.mark(i, "ve")
                    b.ve(vqo)
                    vqom = b.val("ve")

                    def tout(t, ob=ob, vqom=vqom):
                        t.wait_ge(b.sem("ve"), vqom)
                        for i_ in range(NT):
                            i = t.matmul(PS_B[:, (ob % 2) * 512:(ob % 2) * 512 + 512],
                                         WQ4[:, i_ * 128:i_ * 128 + 128],
                                         QT[:, i_ * 512:i_ * 512 + 512],
                                         start=(i_ == 0), stop=(i_ == NT - 1))
                        b.mark(i, "te")
                    b.te(tout)
                    vto = b.val("te")

                    def vevo(v, ob=ob, vto=vto):
                        v.wait_ge(b.sem("te"), vto)
                        v.wait_ge(b.sem("st"), b.val("st"))
                        i = v.tensor_tensor(out=CT1,
                                            in0=PS_B[:, (ob % 2) * 512:(ob % 2) * 512 + 512],
                                            in1=SOIW, op=ALU.mult)
                        b.mark(i, "ve")
                    b.ve(vevo)
                    vevom = b.val("ve")

                    def syo(sy, ob=ob, qb=qb, vevom=vevom):
                        sy.wait_ge(b.sem("ve"), vevom)
                        b.dma(sy, g["ddacc"][ob * 128:ob * 128 + 128, qb * 512:qb * 512 + 512],
                              CT1, "st")
                    b.sy(syo)
                b.barrier()

            h_a4, v_a4 = b.bump("ge")

            def gacw4(ge):
                ge.wait_ge(b.sem("st"), b.val("st"))
                YB4 = arv(0, 16 * TC)
                b.dma(ge, YB4.rearrange("p (t c) -> p t c", t=NT),
                      g["ddacc"].rearrange("(t p) c -> p t c", p=128), "wld")
                ge.wait_ge(b.sem("wld"), b.val("wld"))
                ge.nop().then_inc(h_a4, 1)
            b.ge(gacw4)
            YB4 = arv(0, 16 * TC)

            def yb4_t(tj, half):
                o = tj * TC + half * 512
                return YB4[0:128, o:o + 512]

            for en in ("vector", "scalar", "tensor"):
                def wv4b(eng, en=en):
                    eng.wait_ge(h_a4, v_a4)
                getattr(b.block, en)(wv4b)

            for half in range(2):
                mrep, rrep, vrr = ln_stats(lambda tj, h=half: yb4_t(tj, h), half)

                def vres4(v, half=half, mrep=mrep, rrep=rrep, vrr=vrr):
                    v.wait_ge(b.sem("ve"), vrr)
                    tmp = TMPS[:, half * 512:half * 512 + 512]
                    for tj in range(NT):
                        xsl = X[:, tj * TC + half * 512:tj * TC + half * 512 + 512]
                        v.tensor_tensor(out=tmp, in0=yb4_t(tj, half), in1=mrep, op=ALU.subtract)
                        v.tensor_tensor(out=tmp, in0=tmp, in1=rrep, op=ALU.mult)
                        i = v.tensor_tensor(out=xsl, in0=xsl, in1=tmp, op=ALU.add)
                    b.mark(i, "ve")
                b.ve(vres4)
            b.barrier()

        if PHASES >= 5:
            XL5 = arv(0, 16 * TC)

            def xl5_t(tj, half):
                o = tj * TC + half * 512
                return XL5[0:128, o:o + 512]

            for half in range(2):
                def vcast8(v, half=half):
                    for tj in range(NT):
                        i = v.tensor_copy(out=xl5_t(tj, half),
                                          in_=X[:, tj * TC + half * 512:tj * TC + half * 512 + 512])
                    b.mark(i, "ve")
                b.ve(vcast8)
                mrep, rrep, vrr = ln_stats(lambda tj, h=half: xl5_t(tj, h), half)

                def vnorm8(v, half=half, mrep=mrep, rrep=rrep, vrr=vrr):
                    v.wait_ge(b.sem("ve"), vrr)
                    tmp = TMPS[:, half * 512:half * 512 + 512]
                    for tj in range(NT):
                        xsl = X[:, tj * TC + half * 512:tj * TC + half * 512 + 512]
                        v.tensor_tensor(out=tmp, in0=xsl, in1=mrep, op=ALU.subtract)
                        i = v.tensor_tensor(out=xl5_t(tj, half), in0=tmp, in1=rrep, op=ALU.mult)
                    b.mark(i, "ve")
                b.ve(vnorm8)
            b.barrier()

            WG5 = [arv(32, 2048), arv(36, 2048)]
            WU5 = [arv(40, 2048), arv(44, 2048)]
            SG5 = arv(48, 512, F32)
            TS5 = arv(50, 512, F32)
            HID5 = arv(52, 16 * 1024)
            WD5 = [arv(84, 2048), arv(88, 2048)]
            ESB5 = arv(92, 1024, F32)
            wgv = g["wgt"].rearrange("(j p) c -> p j c", p=128)
            wuv = g["wut"].rearrange("(j p) c -> p j c", p=128)
            wdv = g["wdnt"].rearrange("(j p) c -> p j c", p=128)
            tg_marks = []
            for ch in range(4):
                for oi in range(16):
                    ob = ch * 16 + oi

                    def gl8(ge, ob=ob, n=len(tg_marks)):
                        if n >= 2:
                            ge.wait_ge(b.sem("te"), tg_marks[n - 2])
                        b.dma(ge, WG5[ob % 2][:].rearrange("p (j c) -> p j c", c=128),
                              wgv[:, :, ob * 128:ob * 128 + 128], "wld")
                        b.dma(ge, WU5[ob % 2][:].rearrange("p (j c) -> p j c", c=128),
                              wuv[:, :, ob * 128:ob * 128 + 128], "wld")
                    b.ge(gl8)
                    vl8 = b.val("wld")

                    def tgu(t, ob=ob, vl8=vl8):
                        t.wait_ge(b.sem("wld"), vl8)
                        t.wait_ge(b.sem("se"), b.val("se"))
                        t.wait_ge(b.sem("ve"), b.val("ve"))
                        for half in range(2):
                            for i_ in range(NT):
                                t.matmul(PS_A[:, half * 512:half * 512 + 512],
                                         WG5[ob % 2][:, i_ * 128:i_ * 128 + 128],
                                         xl5_t(i_, half), start=(i_ == 0), stop=(i_ == NT - 1))
                            for i_ in range(NT):
                                i = t.matmul(PS_A[:, 1024 + half * 512:1024 + half * 512 + 512],
                                             WU5[ob % 2][:, i_ * 128:i_ * 128 + 128],
                                             xl5_t(i_, half), start=(i_ == 0), stop=(i_ == NT - 1))
                        tg_marks.append(b.mark(i, "te"))
                    b.te(tgu)
                    vtg = b.val("te")

                    for half in range(2):
                        def ssig(sc, vtg=vtg, half=half):
                            if half == 0:
                                sc.wait_ge(b.sem("te"), vtg)
                            if half == 1:
                                sc.wait_ge(b.sem("ve"), b.val("ve"))
                            i = sc.activation(out=SG5, in_=PS_A[:, half * 512:half * 512 + 512],
                                              func=AF.Sigmoid)
                            b.mark(i, "se")
                        b.se(ssig)
                        vsg8 = b.val("se")

                        def vhid8(v, oi=oi, half=half, vsg8=vsg8):
                            v.wait_ge(b.sem("se"), vsg8)
                            v.tensor_tensor(out=TS5, in0=PS_A[:, half * 512:half * 512 + 512],
                                            in1=SG5, op=ALU.mult)
                            i = v.tensor_tensor(
                                out=HID5[:, oi * 1024 + half * 512:oi * 1024 + half * 512 + 512],
                                in0=PS_A[:, 1024 + half * 512:1024 + half * 512 + 512],
                                in1=TS5, op=ALU.mult)
                            b.mark(i, "ve")
                        b.ve(vhid8)

                for dt in range(16):
                    def gl8d(ge, ch=ch, dt=dt):
                        ge.wait_ge(b.sem("te"), b.val("te"))
                        b.dma(ge, WD5[dt % 2][:].rearrange("p (j c) -> p j c", c=128),
                              wdv[:, ch * 16:ch * 16 + 16, dt * 128:dt * 128 + 128], "wld")
                    b.ge(gl8d)
                    vl8d = b.val("wld")

                    def tdn8(t, dt=dt, vl8d=vl8d):
                        t.wait_ge(b.sem("wld"), vl8d)
                        t.wait_ge(b.sem("ve"), b.val("ve"))
                        for half in range(2):
                            for ii in range(16):
                                i = t.matmul(PS_B[:, half * 512:half * 512 + 512],
                                             WD5[dt % 2][:, ii * 128:ii * 128 + 128],
                                             HID5[:, ii * 1024 + half * 512:ii * 1024 + half * 512 + 512],
                                             start=(ii == 0), stop=(ii == 15))
                        b.mark(i, "te")
                    b.te(tdn8)
                    vtd8 = b.val("te")

                    def vev8(v, vtd8=vtd8):
                        v.wait_ge(b.sem("te"), vtd8)
                        v.wait_ge(b.sem("ac"), b.val("ac"))
                        i = v.tensor_copy(out=ESB5, in_=PS_B[:, 0:1024])
                        b.mark(i, "ve")
                    b.ve(vev8)
                    vem8 = b.val("ve")

                    def gac8(ge, ch=ch, dt=dt, vem8=vem8):
                        ge.wait_ge(b.sem("ve"), vem8)
                        if ch == 0:
                            b.dma(ge, g["ddacc"][dt * 128:dt * 128 + 128, :], ESB5, "ac")
                        else:
                            h2, _ = b.bump("ac", 16)
                            ge.dma_start(out=g["ddacc"][dt * 128:dt * 128 + 128, :], in_=ESB5,
                                         accum_op=ALU.add).then_inc(h2, 16)
                    b.ge(gac8)

                def gacw8(ge):
                    ge.wait_ge(b.sem("ac"), b.val("ac"))
                b.ge(gacw8)
                b.barrier(("vector", "tensor", "scalar", "gpsimd"))

            h_a5, v_a5 = b.bump("ge")

            def gacw5(ge):
                ge.wait_ge(b.sem("ac"), b.val("ac"))
                YB5 = arv(0, 16 * TC)
                b.dma(ge, YB5.rearrange("p (t c) -> p t c", t=NT),
                      g["ddacc"].rearrange("(t p) c -> p t c", p=128), "wld")
                ge.wait_ge(b.sem("wld"), b.val("wld"))
                ge.nop().then_inc(h_a5, 1)
            b.ge(gacw5)
            YB5 = arv(0, 16 * TC)

            def yb5_t(tj, half):
                o = tj * TC + half * 512
                return YB5[0:128, o:o + 512]

            for en in ("vector", "scalar", "tensor"):
                def wv5b(eng, en=en):
                    eng.wait_ge(h_a5, v_a5)
                getattr(b.block, en)(wv5b)

            for half in range(2):
                mrep, rrep, vrr = ln_stats(lambda tj, h=half: yb5_t(tj, h), half)

                def vres5(v, half=half, mrep=mrep, rrep=rrep, vrr=vrr):
                    v.wait_ge(b.sem("ve"), vrr)
                    tmp = TMPS[:, half * 512:half * 512 + 512]
                    for tj in range(NT):
                        xsl = X[:, tj * TC + half * 512:tj * TC + half * 512 + 512]
                        v.tensor_tensor(out=tmp, in0=yb5_t(tj, half), in1=mrep, op=ALU.subtract)
                        v.tensor_tensor(out=tmp, in0=tmp, in1=rrep, op=ALU.mult)
                        i = v.tensor_tensor(out=xsl, in0=xsl, in1=tmp, op=ALU.add)
                    b.mark(i, "ve")
                b.ve(vres5)
            b.barrier()
        def pout(sy):
            sy.wait_ge(b.sem("ve"), b.val("ve"))
            v = b.dma(sy, g["yt"].rearrange("(t p) c -> p t c", p=128),
                      X[:].rearrange("p (t c) -> p t c", t=NT), "st")
            if DEBUG:
                v = b.dma(sy, g["dbg"], DBGT[:], "st")
            sy.wait_ge(b.sem("st"), v)
        b.sy(pout)

    return nc


_NC_CACHE = {}


def kernel(**inputs):
    x = np.asarray(inputs["L0_attn_x"][0])
    xT = np.ascontiguousarray(x.T)
    eye = np.eye(128, dtype=np.float32)
    ones = np.ones((128, 128), dtype=np.float32)

    consts = np.zeros((128, 8), dtype=np.float32)
    for ci, cv in enumerate([EPS, 1.0 / np.sqrt(128.0), 1.0, 0.0]):
        consts[:, ci] = np.float32(cv)
    shared = {
        "wx": np.ascontiguousarray(np.asarray(inputs["L0_attn_W_xproj"]).T),
        "ones": ones, "ident": eye, "consts": consts,
    }
    if PHASES >= 3:
        shared["w1t"] = np.ascontiguousarray(np.asarray(inputs["L0_ffn_w1"]).T)
        shared["w2t"] = np.ascontiguousarray(np.asarray(inputs["L0_ffn_w2"]).T)
        shared["w3t"] = np.ascontiguousarray(np.asarray(inputs["L0_ffn_w3"]).T)
    if PHASES >= 4:
        shared["wqt"] = np.ascontiguousarray(np.asarray(inputs["L1_attn_w_q"]).T)
        shared["wkt"] = np.ascontiguousarray(np.asarray(inputs["L1_attn_w_k"]).T)
        shared["wvt"] = np.ascontiguousarray(np.asarray(inputs["L1_attn_w_v"]).T)
        shared["wot"] = np.ascontiguousarray(np.asarray(inputs["L1_attn_w_o"]).T)
        eh = np.zeros((128, 256), dtype=np.float32)
        for hh in range(16):
            eh[:, hh * 16 + hh] = 1.0
        shared["eh16"] = eh
    if PHASES >= 5:
        shared["wgt"] = np.ascontiguousarray(np.asarray(inputs["L1_ffn_w_gate"]).T)
        shared["wut"] = np.ascontiguousarray(np.asarray(inputs["L1_ffn_w_up"]).T)
        shared["wdnt"] = np.ascontiguousarray(np.asarray(inputs["L1_ffn_w_down"]).T)
    wdtT = np.ascontiguousarray(np.asarray(inputs["L0_attn_W_dt"]).T)

    in_maps = []
    idxs = []
    for c in range(NCORES):
        ix = tok_idx(c)
        idxs.append(ix)
        m = dict(shared)
        m["xt"] = np.ascontiguousarray(xT[:, ix])
        m["xd"] = np.ascontiguousarray(xT[DSL * c:DSL * c + DSL, :])
        m["wdt"] = np.ascontiguousarray(wdtT[:, DSL * c:DSL * c + DSL])
        m["bdt"] = np.ascontiguousarray(
            np.asarray(inputs["L0_attn_b_dt"])[DSL * c:DSL * c + DSL].reshape(2, 128).T)
        m["asl"] = np.ascontiguousarray(
            np.asarray(inputs["L0_attn_A"])[DSL * c:DSL * c + DSL].reshape(2, 128, NST)
            .transpose(1, 0, 2).reshape(128, 2 * NST))
        m["dsl"] = np.ascontiguousarray(
            np.asarray(inputs["L0_attn_D"])[DSL * c:DSL * c + DSL].reshape(2, 128).T)
        if PHASES >= 4:
            mk = np.full((8, 4, 128, 512), NEG, dtype=np.float32)
            o = np.arange(64)
            diag = np.where(o[None, :] <= o[:, None], 0.0, NEG).astype(np.float32)
            for r in range(8):
                for j in range(4):
                    for hh in range(2):
                        qb_ = 2 * j + hh
                        rows = slice(hh * 64, hh * 64 + 64)
                        mk[r, j, rows, 0:64 * qb_] = 0.0
                        if r < c:
                            mk[r, j, rows, 64 * qb_:64 * qb_ + 64] = 0.0
                        elif r == c:
                            mk[r, j, rows, 64 * qb_:64 * qb_ + 64] = diag
            m["masks"] = np.ascontiguousarray(mk.reshape(32, 128, 512))
        in_maps.append(m)

    if PHASES not in _NC_CACHE:
        _NC_CACHE[PHASES] = build_program()
    nc = _NC_CACHE[PHASES]
    res = run_bass_kernel_spmd(nc, in_maps, core_ids=list(range(NCORES)))

    out = np.zeros((S, D), dtype=np.float32)
    for c in range(NCORES):
        out[idxs[c], :] = res.results[c]["yt"].T
    return out[None]


# version 03
# speedup vs baseline: 1.5332x; 1.5332x over previous
import numpy as np
import ml_dtypes
from contextlib import ExitStack
import concourse.bass as bass
import concourse.mybir as mybir
from concourse.bass_utils import run_bass_kernel_spmd
from concourse.aot_env import allow_ici_all_to_all

F32 = mybir.dt.float32
BF16 = mybir.dt.bfloat16
I32 = mybir.dt.int32
U16 = mybir.dt.uint16
AF = mybir.ActivationFunctionType
ALU = mybir.AluOpType
MAGIC = float(np.float32(1.5 * 2 ** 23))
NEG = -1e30

NCORES = 8
S = 8192
TC = 1024
D = 2048
NT = 16
DSL = 256
NST = 16
H = 16
DH = 128
HFF0 = 5632
HFF1 = 8192
EPS = 1e-5

PHASES = 5
DEBUG = False
BF = ml_dtypes.bfloat16
KBb = 512


def tok_idx(c):
    return np.concatenate([np.arange(64 * (8 * jj + c), 64 * (8 * jj + c) + 64)
                           for jj in range(16)])


class Build:
    def __init__(self, nc, ctx):
        self.nc = nc
        self.ctx = ctx
        self.block = ctx.enter_context(nc.Block())
        self.s = {}

    def sem(self, name):
        if name not in self.s:
            self.s[name] = [self.ctx.enter_context(self.nc.semaphore(name)), 0]
        return self.s[name][0]

    def val(self, name):
        self.sem(name)
        return self.s[name][1]

    def bump(self, name, k=1):
        h = self.sem(name)
        self.s[name][1] += k
        return h, self.s[name][1]

    def dma(self, eng, out, in_, q):
        h, v = self.bump(q, 16)
        eng.dma_start(out=out, in_=in_).then_inc(h, 16)
        return v

    def dmat(self, eng, out, in_, q):
        h, v = self.bump(q, 16)
        eng.dma_start_transpose(out=out, in_=in_).then_inc(h, 16)
        return v

    def mark(self, instr, eng):
        h, v = self.bump(eng, 1)
        instr.then_inc(h, 1)
        return v

    def te(self, fn):
        self.block.tensor(fn)

    def ve(self, fn):
        self.block.vector(fn)

    def se(self, fn):
        self.block.scalar(fn)

    def ge(self, fn):
        self.block.gpsimd(fn)

    def sy(self, fn):
        self.block.sync(fn)

    def barrier(self, names=("tensor", "vector", "scalar", "gpsimd", "sync")):
        vals = {n: self.val("bar_" + n) + 1 for n in names}
        for n in names:
            def mk(eng, me=n):
                eng.nop().then_inc(self.sem("bar_" + me), 1)
                for o in names:
                    if o != me:
                        eng.wait_ge(self.sem("bar_" + o), vals[o])
            getattr(self.block, n)(mk)
        for n in names:
            self.s["bar_" + n][1] += 1


def build_program():
    nc = bass.Bass("TRN2", target_bir_lowering=False, debug=False)
    g = {}

    def din(name, shape, dtype=F32):
        g[name] = nc.dram_tensor(name, shape, dtype, kind="ExternalInput").ap()

    def dout(name, shape, dtype=F32):
        g[name] = nc.dram_tensor(name, shape, dtype, kind="ExternalOutput").ap()

    def dscr(name, shape, dtype):
        g[name] = nc.dram_tensor(name, shape, dtype).ap()

    din("xt", [D, TC])
    din("xd", [DSL, S])
    din("wx", [D, 160])
    din("wdt", [128, DSL])
    din("bdt", [128, 2])
    din("asl", [128, 2 * NST])
    din("dsl", [128, 2])
    din("ones", [128, 128])
    din("ident", [128, 128])
    din("consts", [128, 8])
    if PHASES >= 3:
        din("w1t", [D, HFF0])
        din("w2t", [D, HFF0])
        din("w3t", [HFF0, D])
    if PHASES >= 4:
        din("wqt", [D, D])
        din("wkt", [D, D])
        din("wvt", [D, D])
        din("wot", [D, D])
        din("masks", [32, 128, 512])
        din("eh16", [128, 256])
    if PHASES >= 5:
        din("wgt", [D, HFF1])
        din("wut", [D, HFF1])
        din("wdnt", [HFF1, D])
    dout("yt", [D, TC])
    if DEBUG:
        dout("dbg", [128, 3072])

    dscr("g1i", [TC, 256], BF16)
    dscr("g1o", [S, 256], BF16)
    dscr("g1si", [2, TC], F32)
    dscr("g1so", [2 * NCORES, TC], F32)
    dscr("bcc", [32, S], BF16)
    dscr("stc", [2, S], F32)
    dscr("g2i", [S, DSL], BF16)
    dscr("g2o", [S, DSL], BF16)
    if PHASES >= 3:
        dscr("ddacc", [D, TC], F32)
    if PHASES >= 4:
        dscr("gki", [D, TC], BF16)
        dscr("gko", [NCORES * D, TC], BF16)
        dscr("gvi", [TC, D], BF16)
        dscr("gvo", [S, D], BF16)
        dscr("g5i", [1, 128], F32)
        dscr("g5o", [NCORES, 128], F32)
        dscr("mkb", [1, 32], F32)
        dscr("rsb", [16, 512], F32)
        dscr("sxrow", [4, TC], F32)
        dscr("qd", [D, TC], BF16)

    ctx = ExitStack()
    with ctx:
        b = Build(nc, ctx)
        E = ctx.enter_context

        X = E(nc.sbuf_tensor("X", [128, NT * TC], F32))
        AR = E(nc.sbuf_tensor("AR", [128, 100 * KBb], BF16, is_transpose_dest=True))
        WX = E(nc.sbuf_tensor("WX", [128, NT * 160], BF16))
        WDT = E(nc.sbuf_tensor("WDT", [128, DSL], BF16))
        ONES = E(nc.sbuf_tensor("ONES", [128, 128], BF16))
        IDENT = E(nc.sbuf_tensor("IDENT", [128, 128], BF16))
        BDT = E(nc.sbuf_tensor("BDT", [128, 2], F32))
        ASL = E(nc.sbuf_tensor("ASL", [128, 2 * NST], F32))
        DS = E(nc.sbuf_tensor("DS", [128, 2], F32))
        HENDW = E(nc.sbuf_tensor("HENDW", [128, 32 * 128], F32))
        HBQN = E(nc.sbuf_tensor("HBQN", [128, 2056], F32))
        CONSTS = E(nc.sbuf_tensor("CONSTS", [128, 8], F32))
        CONST_VALS = [EPS, 1.0 / np.sqrt(128.0), 1.0, 0.0, -1.0]
        for ci, cv in enumerate(CONST_VALS):
            nc.const_aps.register(F32, float(cv), CONSTS[:, ci:ci + 1])
            if float(np.float32(cv)) != float(cv):
                nc.const_aps.register(F32, float(np.float32(cv)), CONSTS[:, ci:ci + 1])
        STG = E(nc.sbuf_tensor("STG", [128, 256], BF16, is_transpose_dest=True))
        EH16 = E(nc.sbuf_tensor("EH16", [128, 256], BF16))
        GAM = None
        DBGT = E(nc.sbuf_tensor("DBGT", [128, 3072], F32)) if DEBUG else None

        def dtap(v, col, src):
            if DEBUG:
                v.tensor_copy(out=DBGT[:, col:col + src.shape[-1]], in_=src)
        PS_A = E(nc.psum_tensor("PS_A", [128, 2048], F32))
        PS_B = E(nc.psum_tensor("PS_B", [128, 1024], F32))
        PS_C = E(nc.psum_tensor("PS_C", [128, 1024], F32))

        def arv(off_kb, nelem, dtype=BF16, p0=0, p1=128):
            i0 = off_kb * KBb
            if dtype == BF16:
                return AR[p0:p1, i0:i0 + nelem]
            nbf = nelem * (4 if dtype in (F32, I32) else 2) // 2
            return AR[p0:p1, i0:i0 + nbf].bitcast(dtype)

        XLN = arv(0, 16 * TC)
        MREP = arv(80, TC, F32)
        RREP = arv(84, TC, F32)
        TMPS = arv(88, TC, F32)
        SQT = [arv(92, 512), arv(93, 512)]
        CONA = arv(94, 1024)
        CONB = arv(96, 1024)
        GB = [arv(98, 256), arv(98 + 1, 256)][0:1] + [AR[:, 98 * KBb + 256:98 * KBb + 512]]

        def xln_t(tj, half):
            o = tj * TC + half * 512
            return XLN[:, o:o + 512] if False else XLN[0:128, o:o + 512]

        def p0(sy):
            b.dma(sy, X[:].rearrange("p (t c) -> p t c", t=NT),
                  g["xt"].rearrange("(t p) c -> p t c", p=128), "ld")
            b.dma(sy, BDT[:], g["bdt"], "ld")
            b.dma(sy, DS[:], g["dsl"], "ld")
            b.dma(sy, ASL[:], g["asl"], "ld")
            b.dma(sy, CONSTS[:], g["consts"], "ld")
            sy.wait_ge(b.sem("ld"), b.val("ld"))
        b.sy(p0)

        def p0g(ge):
            b.dma(ge, WX[:].rearrange("p (t c) -> p t c", t=NT),
                  g["wx"].rearrange("(t p) c -> p t c", p=128), "wld")
            b.dma(ge, WDT[:], g["wdt"], "wld")
            b.dma(ge, ONES[:], g["ones"], "wld")
            b.dma(ge, IDENT[:], g["ident"], "wld")
            if PHASES >= 4:
                b.dma(ge, EH16[:], g["eh16"], "wld")
            ge.wait_ge(b.sem("wld"), b.val("wld"))
        b.ge(p0g)
        b.barrier()

        def ln_stats(tiles_bf, half):
            mrep = MREP[:, half * 512:half * 512 + 512]
            rrep = RREP[:, half * 512:half * 512 + 512]
            tmp = TMPS[:, half * 512:half * 512 + 512]
            se_base = b.val("se")
            te_base = b.val("te")
            ve_guard = b.val("ve")
            for tj in range(NT):
                def sq(sc, tj=tj):
                    if tj == 0:
                        sc.wait_ge(b.sem("ve"), ve_guard)
                    if tj >= 2:
                        sc.wait_ge(b.sem("te"), te_base + tj - 1)
                    i = sc.activation(out=SQT[tj % 2], in_=tiles_bf(tj), func=AF.Square)
                    b.mark(i, "se")
                b.se(sq)

                def mm(t, tj=tj):
                    if tj == 0:
                        t.wait_ge(b.sem("ve"), ve_guard)
                    t.wait_ge(b.sem("se"), se_base + tj + 1)
                    t.matmul(PS_B[:, 0:512], ONES[:], tiles_bf(tj),
                             start=(tj == 0), stop=(tj == NT - 1))
                    i = t.matmul(PS_B[:, 512:1024], ONES[:], SQT[tj % 2],
                                 start=(tj == 0), stop=(tj == NT - 1))
                    b.mark(i, "te")
                b.te(mm)
            vte = b.val("te")
            hold = {}

            def vstat(v):
                v.wait_ge(b.sem("te"), vte)
                v.tensor_scalar(out=mrep, in0=PS_B[:, 0:512], scalar1=1.0 / D,
                                scalar2=None, op0=ALU.mult)
                v.tensor_scalar(out=tmp, in0=PS_B[:, 512:1024], scalar1=1.0 / D,
                                scalar2=None, op0=ALU.mult)
                v.tensor_tensor(out=rrep, in0=mrep, in1=mrep, op=ALU.mult)
                i = v.tensor_tensor(out=tmp, in0=tmp, in1=rrep, op=ALU.subtract)
                hold["v1"] = b.mark(i, "ve")
            b.ve(vstat)

            def sstat(sc):
                sc.wait_ge(b.sem("ve"), hold["v1"])
                i = sc.activation(out=tmp, in_=tmp, func=AF.Sqrt, bias=EPS)
                hold["s1"] = b.mark(i, "se")
            b.se(sstat)

            def vrec(v):
                v.wait_ge(b.sem("se"), hold["s1"])
                i = v.reciprocal(out=rrep, in_=tmp)
                hold["v2"] = b.mark(i, "ve")
            b.ve(vrec)
            return mrep, rrep, hold["v2"]

        ln1_stats = []
        for half in range(2):
            def vcast(v, half=half):
                for tj in range(NT):
                    i = v.tensor_copy(out=xln_t(tj, half),
                                      in_=X[:, tj * TC + half * 512:tj * TC + half * 512 + 512])
                b.mark(i, "ve")
            b.ve(vcast)
            mrep, rrep, vrr = ln_stats(lambda tj, h=half: xln_t(tj, h), half)

            def vnorm(v, half=half, mrep=mrep, rrep=rrep, vrr=vrr):
                v.wait_ge(b.sem("ve"), vrr)
                tmp = TMPS[:, half * 512:half * 512 + 512]
                for tj in range(NT):
                    xsl = X[:, tj * TC + half * 512:tj * TC + half * 512 + 512]
                    v.tensor_tensor(out=tmp, in0=xsl, in1=mrep, op=ALU.subtract)
                    i = v.tensor_tensor(out=xln_t(tj, half), in0=tmp, in1=rrep, op=ALU.mult)
                b.mark(i, "ve")
            b.ve(vnorm)
            ln1_stats.append((mrep, rrep))

        def vtap1(v):
            pass
        b.ve(vtap1)
        b.barrier()

        for half in range(2):
            ve_guard = b.val("ve")

            def temm(t, half=half, ve_guard=ve_guard):
                t.wait_ge(b.sem("ve"), ve_guard)
                for tj in range(NT):
                    t.matmul(PS_A[:, 0:512], WX[:, tj * 160:tj * 160 + 128],
                             xln_t(tj, half), start=(tj == 0), stop=(tj == NT - 1))
                    i = t.matmul(PS_C[0:32, 0:512], WX[:, tj * 160 + 128:tj * 160 + 160],
                                 xln_t(tj, half), start=(tj == 0), stop=(tj == NT - 1))
                b.mark(i, "te")
            b.te(temm)
            vte = b.val("te")
            mrep, rrep = ln1_stats[half]

            def vevac(v, half=half, vte=vte):
                v.wait_ge(b.sem("te"), vte)
                v.tensor_copy(out=CONA[:, half * 512:half * 512 + 512], in_=PS_A[:, 0:512])
                i = v.tensor_copy(out=CONB[0:32, half * 512:half * 512 + 512], in_=PS_C[0:32, 0:512])
                b.mark(i, "ve")
            b.ve(vevac)

            def systat(sy, half=half, mrep=mrep, rrep=rrep):
                sy.wait_ge(b.sem("ve"), b.val("ve"))
                b.dma(sy, g["g1si"][0:1, half * 512:half * 512 + 512], mrep[0:1, :], "st")
                b.dma(sy, g["g1si"][1:2, half * 512:half * 512 + 512], rrep[0:1, :], "st")
            b.sy(systat)
        vcon = b.val("ve")

        def syt(sy):
            sy.wait_ge(b.sem("ve"), vcon)
            for tb in range(8):
                b.dmat(sy, STG[:, 0:128], CONA[:, tb * 128:tb * 128 + 128], "tp")
                vt = b.dmat(sy, STG[:, 128:256], CONB[:, tb * 128:tb * 128 + 128], "tp")
                sy.wait_ge(b.sem("tp"), vt)
                vs = b.dma(sy, g["g1i"][tb * 128:tb * 128 + 128, :], STG[:], "st")
                sy.wait_ge(b.sem("st"), vs)
        b.sy(syt)

        vsy_h, vsy = b.bump("sy")

        def symark(sy):
            sy.nop().then_inc(vsy_h, 1)
        b.sy(symark)

        def gcc1b(ge):
            ge.wait_ge(vsy_h, vsy)
            i = ge.collective_compute(
                "AllGather", ALU.bypass, replica_groups=[list(range(NCORES))],
                ins=[g["g1i"].opt()], outs=[g["g1o"].opt()])
            h, _ = b.bump("cc", 1)
            i.then_inc(h, 1)
            i = ge.collective_compute(
                "AllGather", ALU.bypass, replica_groups=[list(range(NCORES))],
                ins=[g["g1si"].opt()], outs=[g["g1so"].opt()])
            h, _ = b.bump("cc", 1)
            i.then_inc(h, 1)
            ge.wait_ge(b.sem("cc"), b.val("cc"))
            stv = g["stc"].rearrange("n (i1 r i0) -> n r i1 i0", r=8, i0=64)
            for r in range(NCORES):
                sv = g["g1so"].rearrange("n (i1 i0) -> n i1 i0", i0=64)
                b.dma(ge, stv[0:1, r], sv[2 * r:2 * r + 1], "wld")
                b.dma(ge, stv[1:2, r], sv[2 * r + 1:2 * r + 2], "wld")
            ge.wait_ge(b.sem("wld"), b.val("wld"))
        b.ge(gcc1b)
        vcc1 = b.val("cc")
        vstc = b.val("wld")

        DTFM = arv(32, 8192)
        BCFM = arv(48, 8192)

        def p2b(sy):
            sy.wait_ge(b.sem("cc"), vcc1)
            for blk in range(64):
                gb = GB[blk % 2]
                if blk >= 2:
                    sy.wait_ge(b.sem("tp"), b.val("tp"))
                vl = b.dma(sy, gb, g["g1o"][blk * 128:blk * 128 + 128, :], "ld")
                sy.wait_ge(b.sem("ld"), vl)
                b.dmat(sy, DTFM[:, blk * 128:blk * 128 + 128], gb[:, 0:128], "tp")
                b.dmat(sy, BCFM[:, blk * 128:blk * 128 + 128], gb[:, 128:256], "tp")
            sy.wait_ge(b.sem("tp"), b.val("tp"))
            bco = g["bcc"].rearrange("n (i1 r i0) -> n r i1 i0", r=8, i0=64)
            bcs = BCFM.rearrange("p (r i1 i0) -> p r i1 i0", r=8, i0=64)
            for r in range(8):
                b.dma(sy, bco[:, r], bcs[0:32, r], "st")
            sy.wait_ge(b.sem("st"), b.val("st"))
            sy.nop().then_inc(b.sem("sy"), 1)
        b.sy(p2b)
        b.s["sy"][1] += 1
        vsy2 = b.val("sy")

        b.barrier()

        QW = 2048
        NQ = 4
        SD = 2052
        DELTAP = arv(0, 2 * NQ * SD)

        def dchunk(tj, q):
            o = (tj * NQ + q) * SD
            return DELTAP[:, o:o + 2050]

        def vpfx(v):
            for tj in range(2):
                for q in range(NQ):
                    o = (tj * NQ + q) * SD
                    i = v.memset(DELTAP[:, o:o + 2], 30000.0)
            b.mark(i, "ve")
        b.ve(vpfx)

        dtv = DTFM.rearrange("p (r i1 i0) -> p r i1 i0", r=8, i0=64)
        se_base = b.val("se")
        ve_base_c = b.val("ve")
        for k in range(32):
            tj2, ch = divmod(k, 16)

            def dmm(t, k=k, tj2=tj2, ch=ch):
                if k >= 2:
                    t.wait_ge(b.sem("se"), se_base + k - 1)
                    t.wait_ge(b.sem("ve"), ve_base_c + k - 1)
                i = t.matmul(PS_B[:, (k % 2) * 512:(k % 2) * 512 + 512],
                             WDT[:, tj2 * 128:tj2 * 128 + 128], dtv[:, :, ch, :],
                             start=True, stop=True)
                b.mark(i, "te")
            b.te(dmm)
            vte = b.val("te")

            def dact(sc, k=k, tj2=tj2, ch=ch, vte=vte):
                sc.wait_ge(b.sem("te"), vte)
                if k >= 2:
                    sc.wait_ge(b.sem("ve"), ve_base_c + k - 1)
                ps = PS_B[:, (k % 2) * 512:(k % 2) * 512 + 512]
                spt = MREP[:, (k % 2) * 512:(k % 2) * 512 + 512]
                sc.activation(out=spt, in_=ps, func=AF.Abs, bias=BDT[:, tj2:tj2 + 1])
                sc.activation(out=spt, in_=spt, func=AF.Exp, scale=-1.0)
                i = sc.activation(out=spt, in_=spt, func=AF.Ln, bias=1.0)
                b.mark(i, "se")
            b.se(dact)
            vse_sp = b.val("se")

            def dve_sp(v, k=k, tj2=tj2, ch=ch, vse_sp=vse_sp):
                v.wait_ge(b.sem("se"), vse_sp)
                ps = PS_B[:, (k % 2) * 512:(k % 2) * 512 + 512]
                spt = MREP[:, (k % 2) * 512:(k % 2) * 512 + 512]
                spt2 = RREP[:, (k % 2) * 512:(k % 2) * 512 + 512]
                v.tensor_scalar(out=spt2, in0=ps, scalar1=BDT[:, tj2:tj2 + 1],
                                scalar2=0.0, op0=ALU.add, op1=ALU.max)
                dout_ = dchunk(tj2, ch // 4)
                off = 2 + (ch % 4) * 512
                i = v.tensor_tensor(out=dout_[:, off:off + 512], in0=spt,
                                    in1=spt2, op=ALU.add)
                b.mark(i, "ve")
            b.ve(dve_sp)

        b.barrier()

        XSQ = arv(33, QW)
        UQP = arv(37, 2050)
        YQ = PS_A[:, 0:QW]
        ADQ = [arv(50, 2050), arv(55, 2050)]
        BXQ = arv(60, 2050)
        XDQ = arv(50, QW, F32)
        HBQ = HBQN[:, 0:2050]
        BMBP = [arv(66, 2050), arv(71, 2050)]
        CBB = [arv(76, QW), arv(80, QW)]
        CSQ = arv(84, 2052, F32)
        ZD = arv(93, 2050)
        MEANB = arv(84, QW, F32)
        RSTDB = arv(66, QW, F32)
        PQ = [arv(33, QW), ZD[:, 2:2050]]

        def vhz(v):
            v.memset(HENDW[:], 0.0)
            v.memset(UQP[:, 0:2], 0.0)
            i = v.memset(ZD[:, 0:2], 0.0)
            b.mark(i, "ve")
        b.ve(vhz)
        b.barrier(("vector", "sync", "scalar"))

        for q in range(NQ):
            for tj in range(2):
                def ldq(sy, q=q, tj=tj):
                    b.dma(sy, XDQ, g["xd"][tj * 128:tj * 128 + 128, q * QW:q * QW + QW], "ld")
                    b.dma(sy, MEANB, g["stc"][0:1, q * QW:q * QW + QW].partition_broadcast(128), "ld")
                    b.dma(sy, RSTDB, g["stc"][1:2, q * QW:q * QW + QW].partition_broadcast(128), "ld")
                b.sy(ldq)
                vld = b.val("ld")

                def vxs(v, tj=tj, q=q, vld=vld):
                    v.wait_ge(b.sem("ld"), vld)
                    v.tensor_tensor(out=XSQ, in0=XDQ, in1=MEANB, op=ALU.subtract)
                    v.tensor_tensor(out=XSQ, in0=XSQ, in1=RSTDB, op=ALU.mult)
                    v.tensor_scalar(out=YQ, in0=XSQ, scalar1=DS[:, tj:tj + 1],
                                    scalar2=None, op0=ALU.mult)
                    v.tensor_tensor(out=UQP[:, 2:2050], in0=dchunk(tj, q)[:, 2:2050],
                                    in1=XSQ, op=ALU.mult)
                    v.tensor_copy(out=ZD[:, 2:2050], in_=dchunk(tj, q)[:, 2:2050])
                    i = v.tensor_tensor_scan(out=CSQ[:, 0:2050], data0=ZD, data1=ZD,
                                             initial=0.0, op0=ALU.add, op1=ALU.bypass)
                    b.mark(i, "ve")
                b.ve(vxs)
                ve_q0 = b.val("ve")

                for n in range(NST):
                    pb = n % 2
                    slot = tj * NST + n

                    def ldbc(sy, q=q, n=n, pb=pb, ve_q0=ve_q0):
                        if n < 2:
                            sy.wait_ge(b.sem("ve"), ve_q0)
                        else:
                            sy.wait_ge(b.sem("ve"), ve_q0 + n - 1)
                        b.dma(sy, BMBP[pb][:, 2:2050],
                              g["bcc"][n:n + 1, q * QW:q * QW + QW].partition_broadcast(128), "bc%d" % pb)
                        b.dma(sy, CBB[pb],
                              g["bcc"][16 + n:17 + n, q * QW:q * QW + QW].partition_broadcast(128), "bc%d" % pb)
                    b.sy(ldbc)
                    vbc = b.val("bc%d" % pb)

                    def sad(sc, tj=tj, n=n, q=q, pb=pb, ve_q0=ve_q0):
                        if n == 0:
                            sc.wait_ge(b.sem("ve"), ve_q0)
                        if n >= 2:
                            sc.wait_ge(b.sem("ve"), ve_q0 + n - 1)
                        sc.activation(out=ADQ[pb], in_=dchunk(tj, q),
                                      func=AF.Exp, scale=ASL[:, tj * NST + n:tj * NST + n + 1])
                        i = sc.activation(out=PQ[pb], in_=CSQ[:, 2:2050],
                                          func=AF.Exp, scale=ASL[:, tj * NST + n:tj * NST + n + 1])
                        b.mark(i, "se")
                    b.se(sad)
                    vse = b.val("se")

                    def vscan(v, tj=tj, n=n, pb=pb, vbc=vbc, vse=vse, q=q, slot=slot):
                        v.wait_ge(b.sem("bc%d" % pb), vbc)
                        v.wait_ge(b.sem("se"), vse)
                        v.tensor_tensor(out=BXQ, in0=UQP, in1=BMBP[pb], op=ALU.mult)
                        v.tensor_tensor_scan(out=HBQ, data0=ADQ[pb], data1=BXQ,
                                             initial=0.0, op0=ALU.mult, op1=ALU.add)
                        hsb = HENDW[:, slot * 128 + 127:slot * 128 + 128].broadcast_to((128, QW))
                        v.tensor_tensor(out=BXQ[:, 2:2050], in0=PQ[pb], in1=hsb, op=ALU.mult)
                        v.tensor_tensor(out=HBQ[:, 2:2050], in0=HBQ[:, 2:2050],
                                        in1=BXQ[:, 2:2050], op=ALU.add)
                        v.tensor_copy(out=HENDW[:, slot * 128:slot * 128 + 128],
                                      in_=HBQ[:, 2050 - 128:2050])
                        v.tensor_tensor(out=HBQ[:, 2:2050], in0=HBQ[:, 2:2050], in1=CBB[pb], op=ALU.mult)
                        i = v.tensor_tensor(out=YQ, in0=YQ, in1=HBQ[:, 2:2050], op=ALU.add)
                        b.mark(i, "ve")
                    b.ve(vscan)

                def vyc(v, q=q, tj=tj):
                    i = v.tensor_copy(out=XSQ, in_=YQ)
                    b.mark(i, "ve")
                b.ve(vyc)
                vym = b.val("ve")

                def syy(sy, q=q, tj=tj, vym=vym):
                    sy.wait_ge(b.sem("ve"), vym)
                    for blk in range(QW // 128):
                        sy.wait_ge(b.sem("st"), b.val("st"))
                        vtt = b.dmat(sy, STG[:, (blk % 2) * 128:(blk % 2) * 128 + 128],
                                     XSQ[:, blk * 128:blk * 128 + 128], "tp")
                        sy.wait_ge(b.sem("tp"), vtt)
                        t128 = q * (QW // 128) + blk
                        for k2 in range(2):
                            g64 = (2 * t128 + k2) % 8
                            jj = (2 * t128 + k2) // 8
                            b.dma(sy, g["g2i"][g64 * 1024 + jj * 64:g64 * 1024 + jj * 64 + 64,
                                               tj * 128:tj * 128 + 128],
                                  STG[k2 * 64:k2 * 64 + 64, (blk % 2) * 128:(blk % 2) * 128 + 128], "st")
                b.sy(syy)
                b.barrier(("vector", "sync", "scalar"))

        h_sy, v_sy = b.bump("sy")

        def symark2(sy):
            sy.wait_ge(b.sem("st"), b.val("st"))
            sy.nop().then_inc(h_sy, 1)
        b.sy(symark2)

        def gcc2b(ge):
            ge.wait_ge(h_sy, v_sy)
            with allow_ici_all_to_all():
                i = ge.collective_compute(
                    "AllToAll", ALU.bypass, replica_groups=[list(range(NCORES))],
                    ins=[g["g2i"].opt()], outs=[g["g2o"].opt()])
            h, _ = b.bump("cc", 1)
            i.then_inc(h, 1)
        b.ge(gcc2b)
        vcc2 = b.val("cc")

        YF = arv(32, 16 * TC)

        def p2e(sy):
            sy.wait_ge(b.sem("cc"), vcc2)
            for k in range(64):
                r, tb = divmod(k, 8)
                gb = GB[k % 2]
                if k >= 2:
                    sy.wait_ge(b.sem("tp"), b.val("tp"))
                vl = b.dma(sy, gb, g["g2o"][r * 1024 + tb * 128:r * 1024 + tb * 128 + 128, :], "ld")
                sy.wait_ge(b.sem("ld"), vl)
                for dd in range(2):
                    gt = 2 * r + dd
                    b.dmat(sy, YF[:, gt * TC + tb * 128:gt * TC + tb * 128 + 128],
                           gb[:, dd * 128:dd * 128 + 128], "tp")
            sy.wait_ge(b.sem("tp"), b.val("tp"))
            sy.nop().then_inc(b.sem("sy"), 1)
        b.sy(p2e)
        b.s["sy"][1] += 1
        vsy3 = b.val("sy")

        def yf_t(tj, half):
            o = tj * TC + half * 512
            return YF[0:128, o:o + 512]


        for half in range(2):
            if half == 0:
                def wv(v):
                    v.wait_ge(b.sem("sy"), vsy3)
                b.ve(wv)

                def ws(sc):
                    sc.wait_ge(b.sem("sy"), vsy3)
                b.se(ws)

                def wt(t):
                    t.wait_ge(b.sem("sy"), vsy3)
                b.te(wt)
            mrep, rrep, vrr = ln_stats(lambda tj, h=half: yf_t(tj, h), half)

            def vres(v, half=half, mrep=mrep, rrep=rrep, vrr=vrr):
                v.wait_ge(b.sem("ve"), vrr)
                tmp = TMPS[:, half * 512:half * 512 + 512]
                for tj in range(NT):
                    xsl = X[:, tj * TC + half * 512:tj * TC + half * 512 + 512]
                    v.tensor_tensor(out=tmp, in0=yf_t(tj, half), in1=mrep, op=ALU.subtract)
                    v.tensor_tensor(out=tmp, in0=tmp, in1=rrep, op=ALU.mult)
                    i = v.tensor_tensor(out=xsl, in0=xsl, in1=tmp, op=ALU.add)
                b.mark(i, "ve")
            b.ve(vres)
        b.barrier()

        if PHASES >= 3:
            WFA = [arv(0, 5632, F32), arv(23, 5632, F32)]
            ABSB = arv(46, 5632)
            EVB = arv(57, 128)
            GSC = [arv(58, 128, F32), arv(59, 128, F32), arv(60, 128, F32)]
            wspec = [("w1t", D, HFF0), ("w2t", D, HFF0), ("w3t", HFF0, D)]
            for wi, (wn, wrows, wcols) in enumerate(wspec):
                nj = wrows // 128
                ncb = wcols // 128
                se_b = b.val("se")
                te_b = b.val("te")
                wv = g[wn].rearrange("(j p) c -> p j c", p=128)
                loadv = []
                for cb in range(ncb):
                    def gl(ge, cb=cb, nj=nj, wv=wv, se_b=se_b):
                        if cb >= 2:
                            ge.wait_ge(b.sem("se"), se_b + cb - 1)
                        v = b.dma(ge, WFA[cb % 2][:, 0:nj * 128].rearrange("p (j c) -> p j c", c=128),
                                  wv[:, :, cb * 128:cb * 128 + 128], "wld")
                        loadv.append(v)
                    b.ge(gl)

                    def sab(sc, cb=cb, nj=nj, te_b=te_b):
                        sc.wait_ge(b.sem("wld"), loadv[cb])
                        if cb >= 1:
                            sc.wait_ge(b.sem("te"), te_b + cb)
                        i = sc.activation(out=ABSB[:, 0:nj * 128], in_=WFA[cb % 2][:, 0:nj * 128],
                                          func=AF.Abs)
                        b.mark(i, "se")
                    b.se(sab)
                    vse = b.val("se")

                    def tmm(t, cb=cb, nj=nj, vse=vse, ncb=ncb):
                        t.wait_ge(b.sem("se"), vse)
                        for j in range(nj):
                            i = t.matmul(PS_B[:, 0:128], ABSB[:, j * 128:j * 128 + 128],
                                         ONES[:, 0:128], start=(cb == 0 and j == 0),
                                         stop=(cb == ncb - 1 and j == nj - 1))
                        b.mark(i, "te")
                    b.te(tmm)
                vte = b.val("te")

                def vev(v, vte=vte):
                    v.wait_ge(b.sem("te"), vte)
                    i = v.tensor_copy(out=EVB, in_=PS_B[:, 0:128])
                    b.mark(i, "ve")
                b.ve(vev)
                vv = b.val("ve")

                def tm2(t, vv=vv):
                    t.wait_ge(b.sem("ve"), vv)
                    i = t.matmul(PS_C[:, 0:128], ONES[:, 0:128], EVB, start=True, stop=True)
                    b.mark(i, "te")
                b.te(tm2)
                vte2 = b.val("te")

                def vg(v, wi=wi, vte2=vte2, numel=wrows * wcols):
                    v.wait_ge(b.sem("te"), vte2)
                    v.tensor_scalar(out=GSC[wi], in0=PS_C[:, 0:128], scalar1=1.0 / numel,
                                    scalar2=1e-8, op0=ALU.mult, op1=ALU.add)
                    i = v.reciprocal(out=GSC[wi], in_=GSC[wi])
                    b.mark(i, "ve")
                    dtap(v, 128 * wi, GSC[wi])
                b.ve(vg)
            b.barrier()

            def xln3_t(tj, half):
                o = tj * TC + half * 512
                return XLN[0:128, o:o + 512]
            for half in range(2):
                def vcast3(v, half=half):
                    for tj in range(NT):
                        i = v.tensor_copy(out=xln3_t(tj, half),
                                          in_=X[:, tj * TC + half * 512:tj * TC + half * 512 + 512])
                    b.mark(i, "ve")
                b.ve(vcast3)
                mrep, rrep, vrr = ln_stats(lambda tj, h=half: xln3_t(tj, h), half)

                def vnorm3(v, half=half, mrep=mrep, rrep=rrep, vrr=vrr):
                    v.wait_ge(b.sem("ve"), vrr)
                    tmp = TMPS[:, half * 512:half * 512 + 512]
                    for tj in range(NT):
                        xsl = X[:, tj * TC + half * 512:tj * TC + half * 512 + 512]
                        v.tensor_tensor(out=tmp, in0=xsl, in1=mrep, op=ALU.subtract)
                        i = v.tensor_tensor(out=xln3_t(tj, half), in0=tmp, in1=rrep, op=ALU.mult)
                    b.mark(i, "ve")
                b.ve(vnorm3)
            b.barrier()

            WF1 = [arv(33, 2048, F32), arv(41, 2048, F32)]
            WF2 = [arv(49, 2048, F32), arv(61, 2048, F32)]
            WQT = arv(69, 2048)
            WQT2 = arv(73, 2048)
            SG = arv(77, 512)
            HIDC = arv(78, 11 * 1024)
            w1v = g["w1t"].rearrange("(j p) c -> p j c", p=128)
            w2v = g["w2t"].rearrange("(j p) c -> p j c", p=128)
            w3v = g["w3t"].rearrange("(j p) c -> p j c", p=128)
            for ch in range(4):
                for oi in range(11):
                    ob = ch * 11 + oi
                    def gl2(ge, ob=ob):
                        if ob >= 2:
                            ge.wait_ge(b.sem("ve"), b.val("ve"))
                        b.dma(ge, WF1[ob % 2][:].rearrange("p (j c) -> p j c", c=128),
                              w1v[:, :, ob * 128:ob * 128 + 128], "wld")
                        b.dma(ge, WF2[ob % 2][:].rearrange("p (j c) -> p j c", c=128),
                              w2v[:, :, ob * 128:ob * 128 + 128], "wld")
                    b.ge(gl2)
                    vl2 = b.val("wld")

                    def vq(v, ob=ob, vl2=vl2):
                        v.wait_ge(b.sem("wld"), vl2)
                        if ob >= 1:
                            v.wait_ge(b.sem("te"), b.val("te"))
                        for wf, gsc, wqt in ((WF1[ob % 2], GSC[0], WQT), (WF2[ob % 2], GSC[1], WQT2)):
                            gb = gsc[:, 0:1].broadcast_to((128, 2048))
                            v.tensor_tensor(out=wf[:], in0=wf[:], in1=gb, op=ALU.mult)
                            v.tensor_scalar(out=wf[:], in0=wf[:], scalar1=MAGIC, scalar2=MAGIC,
                                            op0=ALU.add, op1=ALU.subtract)
                            i = v.tensor_scalar(out=wqt[:], in0=wf[:], scalar1=-1.0, scalar2=1.0,
                                                op0=ALU.max, op1=ALU.min)
                        b.mark(i, "ve")
                    b.ve(vq)
                    vqm = b.val("ve")

                    def tffn(t, ob=ob, vqm=vqm):
                        t.wait_ge(b.sem("ve"), vqm)
                        t.wait_ge(b.sem("se"), b.val("se"))
                        for half in range(2):
                            for i_ in range(NT):
                                t.matmul(PS_A[:, half * 512:half * 512 + 512],
                                         WQT[:, i_ * 128:i_ * 128 + 128],
                                         xln3_t(i_, half), start=(i_ == 0), stop=(i_ == NT - 1))
                            for i_ in range(NT):
                                i = t.matmul(PS_A[:, 1024 + half * 512:1024 + half * 512 + 512],
                                             WQT2[:, i_ * 128:i_ * 128 + 128],
                                             xln3_t(i_, half), start=(i_ == 0), stop=(i_ == NT - 1))
                        b.mark(i, "te")
                    b.te(tffn)
                    vtf = b.val("te")

                    for half in range(2):
                        def sgev(sc, vtf=vtf, half=half):
                            if half == 0:
                                sc.wait_ge(b.sem("te"), vtf)
                            if half == 1:
                                sc.wait_ge(b.sem("ve"), b.val("ve"))
                            i = sc.activation(out=SG[:, 0:512], in_=PS_A[:, half * 512:half * 512 + 512],
                                              func=AF.Sigmoid)
                            b.mark(i, "se")
                        b.se(sgev)
                        vsg = b.val("se")

                        def vhid(v, oi=oi, half=half, vsg=vsg):
                            v.wait_ge(b.sem("se"), vsg)
                            i2 = v.tensor_tensor(out=HIDC[:, oi * 1024 + half * 512:oi * 1024 + half * 512 + 512],
                                                 in0=PS_A[:, 1024 + half * 512:1024 + half * 512 + 512],
                                                 in1=SG[:, 0:512], op=ALU.mult)
                            b.mark(i2, "ve")
                        b.ve(vhid)
                WF3 = [arv(33, 1408, F32), arv(39, 1408, F32)]
                WQT3 = arv(45, 1408)
                ESB = arv(49, 1024, F32)
                ESB = arv(69, 1024, F32)
                for dt in range(16):
                    def gl3(ge, ch=ch, dt=dt):
                        ge.wait_ge(b.sem("ve"), b.val("ve"))
                        b.dma(ge, WF3[dt % 2][:].rearrange("p (j c) -> p j c", c=128),
                              w3v[:, 11 * ch:11 * ch + 11, dt * 128:dt * 128 + 128], "wld")
                    b.ge(gl3)
                    vl3 = b.val("wld")

                    def vq3(v, dt=dt, vl3=vl3):
                        v.wait_ge(b.sem("wld"), vl3)
                        if dt >= 1:
                            v.wait_ge(b.sem("te"), b.val("te"))
                        wf = WF3[dt % 2]
                        gb = GSC[2][:, 0:1].broadcast_to((128, 1408))
                        v.tensor_tensor(out=wf[:], in0=wf[:], in1=gb, op=ALU.mult)
                        v.tensor_scalar(out=wf[:], in0=wf[:], scalar1=MAGIC, scalar2=MAGIC,
                                        op0=ALU.add, op1=ALU.subtract)
                        i = v.tensor_scalar(out=WQT3[:], in0=wf[:], scalar1=-1.0, scalar2=1.0,
                                            op0=ALU.max, op1=ALU.min)
                        b.mark(i, "ve")
                    b.ve(vq3)
                    vq3m = b.val("ve")

                    def tdn(t, dt=dt, vq3m=vq3m):
                        t.wait_ge(b.sem("ve"), vq3m)
                        for half in range(2):
                            for ii in range(11):
                                i = t.matmul(PS_B[:, half * 512:half * 512 + 512],
                                             WQT3[:, ii * 128:ii * 128 + 128],
                                             HIDC[:, ii * 1024 + half * 512:ii * 1024 + half * 512 + 512],
                                             start=(ii == 0), stop=(ii == 10))
                        b.mark(i, "te")
                    b.te(tdn)
                    vtd = b.val("te")

                    def vev3(v, dt=dt, vtd=vtd):
                        v.wait_ge(b.sem("te"), vtd)
                        v.wait_ge(b.sem("ac"), b.val("ac"))
                        i = v.tensor_copy(out=ESB, in_=PS_B[:, 0:1024])
                        b.mark(i, "ve")
                    b.ve(vev3)
                    vem = b.val("ve")

                    def gac(ge, ch=ch, dt=dt, vem=vem):
                        ge.wait_ge(b.sem("ve"), vem)
                        if ch == 0:
                            b.dma(ge, g["ddacc"][dt * 128:dt * 128 + 128, :], ESB, "ac")
                        else:
                            h2, _ = b.bump("ac", 16)
                            ge.dma_start(out=g["ddacc"][dt * 128:dt * 128 + 128, :], in_=ESB,
                                         accum_op=ALU.add).then_inc(h2, 16)
                    b.ge(gac)

                def gacw0(ge):
                    ge.wait_ge(b.sem("ac"), b.val("ac"))
                b.ge(gacw0)
                if ch == 0:
                    def vtapH(v):
                        dtap(v, 384, HIDC[:, 0:512])
                    b.ve(vtapH)
                b.barrier(("vector", "tensor", "scalar", "gpsimd"))

            h_ac, v_ac = b.bump("ge")

            def gacw(ge):
                ge.wait_ge(b.sem("ac"), b.val("ac"))
                YB3 = arv(0, 16 * TC)
                b.dma(ge, YB3.rearrange("p (t c) -> p t c", t=NT),
                      g["ddacc"].rearrange("(t p) c -> p t c", p=128), "wld")
                ge.wait_ge(b.sem("wld"), b.val("wld"))
                ge.nop().then_inc(h_ac, 1)
            b.ge(gacw)
            YB3 = arv(0, 16 * TC)

            def yb3_t(tj, half):
                o = tj * TC + half * 512
                return YB3[0:128, o:o + 512]

            for en in ("vector", "scalar", "tensor"):
                def wv3(eng, en=en):
                    eng.wait_ge(h_ac, v_ac)
                getattr(b.block, en)(wv3)

            def vtapY(v):
                dtap(v, 896, YB3[0:128, 0:512])
            b.ve(vtapY)

            for half in range(2):
                mrep, rrep, vrr = ln_stats(lambda tj, h=half: yb3_t(tj, h), half)

                def vres3(v, half=half, mrep=mrep, rrep=rrep, vrr=vrr):
                    v.wait_ge(b.sem("ve"), vrr)
                    tmp = TMPS[:, half * 512:half * 512 + 512]
                    for tj in range(NT):
                        xsl = X[:, tj * TC + half * 512:tj * TC + half * 512 + 512]
                        v.tensor_tensor(out=tmp, in0=yb3_t(tj, half), in1=mrep, op=ALU.subtract)
                        v.tensor_tensor(out=tmp, in0=tmp, in1=rrep, op=ALU.mult)
                        i = v.tensor_tensor(out=xsl, in0=xsl, in1=tmp, op=ALU.add)
                    b.mark(i, "ve")
                b.ve(vres3)
            b.barrier()

        if PHASES >= 4:
            WFA4 = [arv(0, 2048, F32), arv(8, 2048, F32)]
            ABSB4 = arv(16, 2048)
            EVB4 = arv(20, 128)
            EVL4 = arv(20, 256)[:, 128:256]
            ACC4 = HBQN[:, 0:128]
            ACC4B = HBQN[:, 128:256]
            GSW = [HBQN[:, 1024 + i4 * 128:1024 + i4 * 128 + 128] for i4 in range(4)]
            GIV = [HBQN[:, 1536 + i4 * 128:1536 + i4 * 128 + 128] for i4 in range(4)]
            for wi, wn in enumerate(["wqt", "wkt", "wvt", "wot"]):
                se_b = b.val("se")
                te_b = b.val("te")
                wv4 = g[wn].rearrange("(j p) c -> p j c", p=128)
                loadv4 = []
                for cb in range(16):
                    def gl4(ge, cb=cb, wv4=wv4, se_b=se_b):
                        if cb >= 2:
                            ge.wait_ge(b.sem("se"), se_b + cb - 1)
                        else:
                            ge.wait_ge(b.sem("se"), se_b)
                        v = b.dma(ge, WFA4[cb % 2][:].rearrange("p (j c) -> p j c", c=128),
                                  wv4[:, :, cb * 128:cb * 128 + 128], "wld")
                        loadv4.append(v)
                    b.ge(gl4)

                    def sab4(sc, cb=cb, te_b=te_b):
                        sc.wait_ge(b.sem("wld"), loadv4[cb])
                        if cb >= 1:
                            sc.wait_ge(b.sem("te"), te_b + cb)
                        else:
                            sc.wait_ge(b.sem("te"), te_b)
                        i = sc.activation(out=ABSB4[:], in_=WFA4[cb % 2][:], func=AF.Abs)
                        b.mark(i, "se")
                    b.se(sab4)
                    vse4 = b.val("se")

                    def tmm4(t, cb=cb, vse4=vse4):
                        t.wait_ge(b.sem("se"), vse4)
                        t.wait_ge(b.sem("ve"), b.val("ve"))
                        for j in range(16):
                            i = t.matmul(PS_B[:, 0:128], ABSB4[:, j * 128:j * 128 + 128],
                                         ONES[:, 0:128], start=(j == 0), stop=(j == 15))
                        b.mark(i, "te")
                    b.te(tmm4)
                    vtc4 = b.val("te")

                    def vac4(v, cb=cb, vtc4=vtc4):
                        v.wait_ge(b.sem("te"), vtc4)
                        if cb == 0:
                            i = v.tensor_copy(out=ACC4, in_=PS_B[:, 0:128])
                        else:
                            i = v.tensor_tensor(out=ACC4, in0=ACC4,
                                                in1=PS_B[:, 0:128], op=ALU.add)
                        b.mark(i, "ve")
                    b.ve(vac4)

                def vhl4(v):
                    v.tensor_copy(out=EVB4, in_=ACC4)
                    v.tensor_copy(out=ACC4B, in_=EVB4)
                    v.tensor_tensor(out=ACC4B, in0=ACC4, in1=ACC4B, op=ALU.subtract)
                    i = v.tensor_copy(out=EVL4, in_=ACC4B)
                    b.mark(i, "ve")
                b.ve(vhl4)
                vv4 = b.val("ve")

                def tm24(t, vv4=vv4):
                    t.wait_ge(b.sem("ve"), vv4)
                    t.matmul(PS_C[:, 0:128], ONES[:, 0:128], EVB4, start=True, stop=False)
                    i = t.matmul(PS_C[:, 0:128], ONES[:, 0:128], EVL4, start=False, stop=True)
                    b.mark(i, "te")
                b.te(tm24)
                vte24 = b.val("te")

                def vg4(v, wi=wi, vte24=vte24):
                    v.wait_ge(b.sem("te"), vte24)
                    v.tensor_scalar(out=GIV[wi], in0=PS_C[:, 0:128], scalar1=1.0 / (D * D),
                                    scalar2=1e-5, op0=ALU.mult, op1=ALU.max)
                    i = v.reciprocal(out=GSW[wi], in_=GIV[wi])
                    b.mark(i, "ve")
                b.ve(vg4)
                b.barrier(("vector", "scalar", "tensor", "gpsimd"))
            b.barrier()

            def xcast5_t(tj, half):
                o = tj * TC + half * 512
                return XLN[0:128, o:o + 512]
            ln5 = []
            for half in range(2):
                def vcast5(v, half=half):
                    for tj in range(NT):
                        i = v.tensor_copy(out=xcast5_t(tj, half),
                                          in_=X[:, tj * TC + half * 512:tj * TC + half * 512 + 512])
                    b.mark(i, "ve")
                b.ve(vcast5)
                mrep, rrep, vrr = ln_stats(lambda tj, h=half: xcast5_t(tj, h), half)
                ln5.append((mrep, rrep, vrr))
                AMX = arv(48, 512, F32)
                TT5 = arv(50, 512, F32)
                HI5 = arv(54, 512)
                LO5 = arv(55, 512)
                TH5 = arv(56, 128)
                TL5 = arv(57, 128)
                FR5 = arv(58, 128, F32)

                AMN = arv(52, 512, F32)

                def vamax(v, half=half, mrep=mrep, rrep=rrep, vrr=vrr):
                    v.wait_ge(b.sem("ve"), vrr)
                    for tj in range(NT):
                        xsl = X[:, tj * TC + half * 512:tj * TC + half * 512 + 512]
                        v.tensor_tensor(out=TT5, in0=xsl, in1=mrep, op=ALU.subtract)
                        if tj == 0:
                            v.tensor_copy(out=AMX, in_=TT5)
                            v.tensor_copy(out=AMN, in_=TT5)
                        else:
                            v.tensor_tensor(out=AMX, in0=AMX, in1=TT5, op=ALU.max)
                            v.tensor_tensor(out=AMN, in0=AMN, in1=TT5, op=ALU.min)
                    v.tensor_scalar(out=AMN, in0=AMN, scalar1=-1.0, scalar2=None, op0=ALU.mult)
                    v.tensor_tensor(out=AMX, in0=AMX, in1=AMN, op=ALU.max)
                    v.tensor_tensor(out=AMX, in0=AMX, in1=rrep, op=ALU.mult)
                    for stp in (64, 32):
                        v.tensor_copy(out=TT5[0:stp, :], in_=AMX[stp:2 * stp, :])
                        v.tensor_tensor(out=AMX[0:stp, :], in0=AMX[0:stp, :],
                                        in1=TT5[0:stp, :], op=ALU.max)
                    v.tensor_copy(out=HI5[0:32, :], in_=AMX[0:32, :])
                    v.tensor_copy(out=TT5[0:32, :], in_=HI5[0:32, :])
                    v.tensor_tensor(out=TT5[0:32, :], in0=AMX[0:32, :],
                                    in1=TT5[0:32, :], op=ALU.subtract)
                    i = v.tensor_copy(out=LO5[0:32, :], in_=TT5[0:32, :])
                    b.mark(i, "ve")
                b.ve(vamax)
                vam = b.val("ve")

                def sytr5(sy, vam=vam):
                    sy.wait_ge(b.sem("ve"), vam)
                    for ci in range(4):
                        b.dmat(sy, TH5[:, ci * 32:ci * 32 + 32],
                               HI5[0:32, ci * 128:ci * 128 + 128], "tp")
                        b.dmat(sy, TL5[:, ci * 32:ci * 32 + 32],
                               LO5[0:32, ci * 128:ci * 128 + 128], "tp")
                    sy.wait_ge(b.sem("tp"), b.val("tp"))
                    sy.nop().then_inc(b.sem("sy"), 1)
                b.sy(sytr5)
                b.s["sy"][1] += 1
                vtr5 = b.val("sy")

                def vfr5(v, vtr5=vtr5, half=half):
                    v.wait_ge(b.sem("sy"), vtr5)
                    v.tensor_tensor(out=FR5, in0=TH5[:, 0:128], in1=TL5[:, 0:128], op=ALU.add)
                    v.tensor_scalar(out=FR5, in0=FR5, scalar1=1e-5,
                                    scalar2=None, op0=ALU.max)
                    i = v.reduce_max(out=TT5[:, 0:4],
                                     in_=FR5.rearrange("p (ci c) -> p ci c", ci=4),
                                     axis=mybir.AxisListType.X)
                    if half == 0:
                        dtap(v, 2944, FR5)
                    b.mark(i, "ve")
                b.ve(vfr5)
                vfr5m = b.val("ve")

                def sysx(sy, half=half, vfr5m=vfr5m):
                    sy.wait_ge(b.sem("ve"), vfr5m)
                    vs = None
                    for ci in range(4):
                        cX = half * 512 + ci * 128
                        vs = b.dma(sy, g["sxrow"][0:1, cX:cX + 128]
                                   .rearrange("a (bb c) -> (a bb) c", c=1),
                                   TT5[:, ci:ci + 1], "st")
                    sy.wait_ge(b.sem("st"), vs)
                b.sy(sysx)
            b.barrier(("vector", "sync", "scalar"))

            SXB = arv(44, TC, F32)
            SXIW = [arv(32, TC, F32), arv(36, TC, F32), arv(40, TC, F32)]

            def ldsx(sy):
                v = b.dma(sy, SXB, g["sxrow"][0:1, :].partition_broadcast(128), "ld")
                sy.wait_ge(b.sem("ld"), v)
                sy.nop().then_inc(b.sem("sy"), 1)
            b.sy(ldsx)
            b.s["sy"][1] += 1
            vsx = b.val("sy")

            def vsxc(v):
                v.wait_ge(b.sem("sy"), vsx)
                v.tensor_scalar(out=SXIW[0], in0=SXB, scalar1=1.0 / 127.0,
                                scalar2=None, op0=ALU.mult)
                v.reciprocal(out=SXB, in_=SXIW[0])
                i = v.tensor_copy(out=AMX, in_=AMX)
                dtap(v, 2816, SXB[:, 0:128])
                b.mark(i, "ve")
            b.ve(vsxc)

            for half in range(2):
                mrep, rrep, vrr = ln5[half]

                def vxint(v, half=half, mrep=mrep, rrep=rrep):
                    TT5b = arv(59, 512, F32)
                    v.tensor_tensor(out=TT5b, in0=rrep, in1=SXB[:, half * 512:half * 512 + 512], op=ALU.mult)
                    for tj in range(NT):
                        xsl = X[:, tj * TC + half * 512:tj * TC + half * 512 + 512]
                        TT5c = arv(61, 512, F32)
                        v.tensor_tensor(out=TT5c, in0=xsl, in1=mrep, op=ALU.subtract)
                        v.tensor_tensor(out=TT5c, in0=TT5c, in1=TT5b, op=ALU.mult)
                        v.tensor_scalar(out=TT5c, in0=TT5c, scalar1=MAGIC, scalar2=MAGIC,
                                        op0=ALU.add, op1=ALU.subtract)
                        i = v.tensor_scalar(out=xcast5_t(tj, half), in0=TT5c, scalar1=-128.0,
                                            scalar2=127.0, op0=ALU.max, op1=ALU.min)
                    if half == 0:
                        dtap(v, 0, xcast5_t(0, 0))
                    b.mark(i, "ve")
                b.ve(vxint)
            b.barrier()

            def vsxiw(v):
                gb1 = GIV[1][:, 0:1].broadcast_to((128, TC))
                gb2 = GIV[2][:, 0:1].broadcast_to((128, TC))
                gb0 = GIV[0][:, 0:1].broadcast_to((128, TC))
                v.tensor_tensor(out=SXIW[1], in0=SXIW[0], in1=gb1, op=ALU.mult)
                v.tensor_tensor(out=SXIW[2], in0=SXIW[0], in1=gb2, op=ALU.mult)
                i = v.tensor_tensor(out=SXIW[0], in0=SXIW[0], in1=gb0, op=ALU.mult)
                b.mark(i, "ve")
            b.ve(vsxiw)
            b.barrier(("vector", "sync", "gpsimd", "tensor", "scalar"))

            WF4 = [arv(45, 2048, F32), arv(53, 2048, F32)]
            WQ4 = arv(61, 2048)
            QKV = arv(65, 1024)
            KSQ = arv(67, 1024)
            KN2 = arv(69, 128, F32)
            STG4 = STG
            wviews = [g["wqt"].rearrange("(j p) c -> p j c", p=128),
                      g["wkt"].rearrange("(j p) c -> p j c", p=128),
                      g["wvt"].rearrange("(j p) c -> p j c", p=128)]
            for wi in range(3):
                for ob in range(16):
                    def gl5(ge, wi=wi, ob=ob):
                        ge.wait_ge(b.sem("ve"), b.val("ve"))
                        b.dma(ge, WF4[ob % 2][:].rearrange("p (j c) -> p j c", c=128),
                              wviews[wi][:, :, ob * 128:ob * 128 + 128], "wld")
                    b.ge(gl5)
                    vl5 = b.val("wld")

                    def vq5(v, wi=wi, ob=ob, vl5=vl5):
                        v.wait_ge(b.sem("wld"), vl5)
                        if not (wi == 0 and ob == 0):
                            v.wait_ge(b.sem("te"), b.val("te"))
                        wf = WF4[ob % 2]
                        gb = GSW[wi][:, 0:1].broadcast_to((128, 2048))
                        v.tensor_tensor(out=wf[:], in0=wf[:], in1=gb, op=ALU.mult)
                        v.tensor_scalar(out=wf[:], in0=wf[:], scalar1=MAGIC, scalar2=MAGIC,
                                        op0=ALU.add, op1=ALU.subtract)
                        i = v.tensor_scalar(out=WQ4[:], in0=wf[:], scalar1=-1.0, scalar2=1.0,
                                            op0=ALU.max, op1=ALU.min)
                        b.mark(i, "ve")
                    b.ve(vq5)
                    vqm5 = b.val("ve")

                    def tqkv(t, wi=wi, ob=ob, vqm5=vqm5):
                        t.wait_ge(b.sem("ve"), vqm5)
                        t.wait_ge(b.sem("se"), b.val("se"))
                        for half in range(2):
                            for i_ in range(NT):
                                i = t.matmul(PS_B[:, half * 512:half * 512 + 512],
                                             WQ4[:, i_ * 128:i_ * 128 + 128],
                                             xcast5_t(i_, half), start=(i_ == 0), stop=(i_ == NT - 1))
                        b.mark(i, "te")
                    b.te(tqkv)
                    vtq = b.val("te")

                    def vevq(v, wi=wi, ob=ob, vtq=vtq):
                        v.wait_ge(b.sem("te"), vtq)
                        v.wait_ge(b.sem("st"), b.val("st"))
                        v.wait_ge(b.sem("tp"), b.val("tp"))
                        i = v.tensor_tensor(out=QKV[:, 0:1024],
                                            in0=PS_B[:, 0:1024], in1=SXIW[wi], op=ALU.mult)
                        if wi == 0 and ob == 0:
                            dtap(v, 512, QKV[:, 0:512])
                        b.mark(i, "ve")
                    b.ve(vevq)
                    vem5 = b.val("ve")

                    if wi == 0:
                        def syq(sy, ob=ob, vem5=vem5):
                            sy.wait_ge(b.sem("ve"), vem5)
                            b.dma(sy, g["qd"][ob * 128:ob * 128 + 128, :], QKV[:, 0:1024], "st")
                        b.sy(syq)
                    elif wi == 1:
                        def syk(sy, ob=ob, vem5=vem5):
                            sy.wait_ge(b.sem("ve"), vem5)
                            b.dma(sy, g["gki"][ob * 128:ob * 128 + 128, :], QKV[:, 0:1024], "st")
                        b.sy(syk)
                    else:
                        def syv(sy, ob=ob, vem5=vem5):
                            sy.wait_ge(b.sem("ve"), vem5)
                            for tb in range(8):
                                sy.wait_ge(b.sem("st"), b.val("st"))
                                vtt = b.dmat(sy, STG4[:, (tb % 2) * 128:(tb % 2) * 128 + 128],
                                             QKV[:, tb * 128:tb * 128 + 128], "tp")
                                sy.wait_ge(b.sem("tp"), vtt)
                                b.dma(sy, g["gvi"][tb * 128:tb * 128 + 128,
                                                   ob * 128:ob * 128 + 128],
                                      STG4[:, (tb % 2) * 128:(tb % 2) * 128 + 128], "st")
                        b.sy(syv)
                    if wi == 1:
                        def skn(sc, ob=ob, vem5=vem5):
                            sc.wait_ge(b.sem("ve"), vem5)
                            i = sc.activation(out=KSQ[:, 0:1024], in_=QKV[:, 0:1024], func=AF.Square)
                            b.mark(i, "se")
                        b.se(skn)
                        vkn = b.val("se")

                        def tkn(t, vkn=vkn, ob=ob):
                            t.wait_ge(b.sem("se"), vkn)
                            t.matmul(PS_C[:, 512:1024], ONES[:], KSQ[:, 0:512], start=True, stop=True)
                            i = t.matmul(PS_C[:, 512:1024], ONES[:], KSQ[:, 512:1024],
                                         start=False, stop=True)
                            b.mark(i, "te")
                        b.te(tkn)
                        vkt = b.val("te")

                        def vkn2(v, ob=ob, vkt=vkt):
                            v.wait_ge(b.sem("te"), vkt)
                            KW = arv(70, 512, F32)
                            v.tensor_copy(out=KW[:, 0:512], in_=PS_C[:, 512:1024])
                            for w2 in (256, 128, 64, 32, 16, 8):
                                v.tensor_tensor(out=KW[:, 0:w2], in0=KW[:, 0:w2],
                                                in1=KW[:, w2:2 * w2], op=ALU.max)
                            i = v.tensor_copy(out=KN2[:, ob * 8:ob * 8 + 8], in_=KW[:, 0:8])
                            b.mark(i, "ve")
                        b.ve(vkn2)
            vknd = b.val("ve")

            def sykn(sy, vknd=vknd):
                sy.wait_ge(b.sem("ve"), vknd)
                vs = b.dma(sy, g["g5i"], KN2[0:1, :], "st")
                sy.wait_ge(b.sem("st"), vs)
                sy.nop().then_inc(b.sem("sy"), 1)
            b.sy(sykn)
            b.s["sy"][1] += 1
            vstg = b.val("sy")

            def gcc4(ge, vstg=vstg):
                ge.wait_ge(b.sem("sy"), vstg)
                for nmi, nmo in (("gki", "gko"), ("gvi", "gvo"), ("g5i", "g5o")):
                    i = ge.collective_compute(
                        "AllGather", ALU.bypass, replica_groups=[list(range(NCORES))],
                        ins=[g[nmi].opt()], outs=[g[nmo].opt()])
                    h2, _ = b.bump("cc", 1)
                    i.then_inc(h2, 1)
            b.ge(gcc4)
            vcc4 = b.val("cc")

            KNR = HENDW[:, 656:784]
            KGB = HENDW[:, 640:656]

            def ldknr(sy, vcc4=vcc4):
                sy.wait_ge(b.sem("cc"), vcc4)
                v = b.dma(sy, KNR[0:8, 0:128], g["g5o"], "ld")
                sy.wait_ge(b.sem("ld"), v)
                sy.nop().then_inc(b.sem("sy"), 1)
            b.sy(ldknr)
            b.s["sy"][1] += 1
            vknl = b.val("sy")

            KNRED = HENDW[:, 576:577]

            def vknf(v, vknl=vknl):
                v.wait_ge(b.sem("sy"), vknl)
                i = v.reduce_max(out=KNRED[0:8, 0:1], in_=KNR[0:8, 0:128],
                                 axis=mybir.AxisListType.X)
                b.mark(i, "ve")
            b.ve(vknf)
            vknm = b.val("ve")

            def syknb(sy, vknm=vknm):
                sy.wait_ge(b.sem("ve"), vknm)
                vs = b.dma(sy, g["mkb"][0:1, 0:8].rearrange("a (bb c) -> (a bb) c", c=1),
                           KNRED[0:8, 0:1], "st")
                sy.wait_ge(b.sem("st"), vs)
                v2 = b.dma(sy, KGB[:, 0:8],
                           g["mkb"][0:1, 0:8].partition_broadcast(128), "ld")
                sy.wait_ge(b.sem("ld"), v2)
                sy.nop().then_inc(b.sem("sy"), 1)
            b.sy(syknb)
            b.s["sy"][1] += 1
            b.barrier()
            if DEBUG:
                def vtapG(v):
                    dtap(v, 2688, KGB)
                    for i4 in range(4):
                        v.tensor_copy(out=DBGT[:, 2704 + i4:2705 + i4], in_=GIV[i4][:, 0:1])
                        v.tensor_copy(out=HENDW[:, 2640:3152], in_=HENDW[:, 2640:3152])
                        v.tensor_copy(out=DBGT[:, 2708 + i4:2709 + i4], in_=GSW[i4][:, 0:1])
                        v.tensor_copy(out=HENDW[:, 2640:3152], in_=HENDW[:, 2640:3152])
                b.ve(vtapG)

            QT = arv(0, 16 * 512)
            KH = [arv(16, 512), arv(17, 512)]
            VTA = [arv(18, 4 * 2048), arv(34, 4 * 2048)]
            PBUF = arv(50, 2048)
            MSKB = arv(54, 2048)
            OAC = arv(68, 16 * 512, F32)
            CM = HENDW[:, 784:1296]
            CMX = HENDW[:, 1296:1808]
            QSQ2 = HENDW[:, 1808:2320].bitcast(BF16)
            RSH = HENDW[:, 1808:2320]
            CNB16 = HENDW[:, 2320:2336]
            CC = HENDW[:, 2336:2340]
            CT1 = HENDW[:, 2640:3152]
            AMO = HENDW[:, 3152:3664]
            SOB = HENDW[:, 784:1296]
            SOIW = HENDW[:, 1296:1808]
            RSI = HENDW[:, 0:512]
            TSCM = []
            AVM = []

            for qb in range(2):
                NKB = 8 * (qb + 1)

                def ldq4(sy, qb=qb):
                    v = b.dma(sy, QT.rearrange("p (t c) -> p t c", t=16),
                              g["qd"][:, qb * 512:qb * 512 + 512]
                              .rearrange("(t p) c -> p t c", p=128), "ld")
                    sy.wait_ge(b.sem("ld"), v)
                    sy.nop().then_inc(b.sem("sy"), 1)
                b.sy(ldq4)
                b.s["sy"][1] += 1
                vq4 = b.val("sy")

                te_q = b.val("te")
                for h in range(H):
                    def sqsq(sc, h=h, vq4=vq4, te_q=te_q):
                        if h == 0:
                            sc.wait_ge(b.sem("sy"), vq4)
                        if h >= 2:
                            sc.wait_ge(b.sem("te"), te_q + h - 1)
                        i = sc.activation(out=QSQ2[:, (h % 2) * 512:(h % 2) * 512 + 512],
                                          in_=QT[:, h * 512:h * 512 + 512], func=AF.Square)
                        b.mark(i, "se")
                    b.se(sqsq)
                    vsq = b.val("se")

                    def tqn(t, h=h, vsq=vsq):
                        t.wait_ge(b.sem("se"), vsq)
                        i = t.matmul(PS_A[0:16, 0:512], EH16[:, h * 16:h * 16 + 16],
                                     QSQ2[:, (h % 2) * 512:(h % 2) * 512 + 512],
                                     start=(h == 0), stop=(h == H - 1))
                        b.mark(i, "te")
                    b.te(tqn)
                vtqn = b.val("te")

                def vcm(v, vtqn=vtqn):
                    v.wait_ge(b.sem("te"), vtqn)
                    v.tensor_copy(out=CM[0:16, :], in_=PS_A[0:16, 0:512])
                    for s in range(8):
                        kb_ = KGB[0:16, s:s + 1].broadcast_to((16, 512))
                        if s == 0:
                            v.tensor_tensor(out=CMX[0:16, :], in0=CM[0:16, :],
                                            in1=kb_, op=ALU.mult)
                        else:
                            v.tensor_tensor(out=CT1[0:16, :], in0=CM[0:16, :],
                                            in1=kb_, op=ALU.mult)
                            v.tensor_tensor(out=CMX[0:16, :], in0=CMX[0:16, :],
                                            in1=CT1[0:16, :], op=ALU.max)
                    i = v.reduce_max(out=CC[0:16, 0:1], in_=CMX[0:16, :],
                                     axis=mybir.AxisListType.X)
                    b.mark(i, "ve")
                b.ve(vcm)
                vcmm = b.val("ve")

                def scm(sc, vcmm=vcmm):
                    sc.wait_ge(b.sem("ve"), vcmm)
                    i = sc.activation(out=CC[0:16, 1:2], in_=CC[0:16, 0:1], func=AF.Sqrt)
                    b.mark(i, "se")
                b.se(scm)
                vscm = b.val("se")

                def vcneg(v, vscm=vscm):
                    v.wait_ge(b.sem("se"), vscm)
                    i = v.tensor_scalar(out=CC[0:16, 2:3], in0=CC[0:16, 1:2],
                                        scalar1=float(-1.015625 / np.sqrt(128.0)),
                                        scalar2=None, op0=ALU.mult)
                    b.mark(i, "ve")
                b.ve(vcneg)
                vcnb = b.val("ve")

                def syct(sy, vcnb=vcnb):
                    sy.wait_ge(b.sem("ve"), vcnb)
                    vs = b.dma(sy, g["mkb"][0:1, 16:32].rearrange("a (bb c) -> (a bb) c", c=1),
                               CC[0:16, 2:3], "st")
                    sy.wait_ge(b.sem("st"), vs)
                    v2 = b.dma(sy, CNB16, g["mkb"][0:1, 16:32].partition_broadcast(128), "ld")
                    sy.wait_ge(b.sem("ld"), v2)
                    sy.nop().then_inc(b.sem("sy"), 1)
                b.sy(syct)
                b.s["sy"][1] += 1
                b.barrier()

                kb_av = []
                for kb in range(NKB):
                    m, r = kb // 8, kb % 8
                    partial = (m == qb)

                    def ldv(ge, m=m, r=r, kb=kb, kb_av=kb_av):
                        if kb >= 2:
                            ge.wait_ge(b.sem("ve"), kb_av[kb - 2])
                        for s2 in range(4):
                            rowb = r * 1024 + 512 * m + 128 * s2
                            b.dma(ge, VTA[kb % 2][:, s2 * 2048:s2 * 2048 + 2048],
                                  g["gvo"][rowb:rowb + 128, :], "vld")
                    b.ge(ldv)
                    vldv = b.val("vld")

                    if partial:
                        def ldmsk(ge, r=r):
                            ge.wait_ge(b.sem("te"), b.val("te"))
                            b.dma(ge, MSKB.rearrange("p (j c) -> p j c", j=4),
                                  g["masks"][r * 4:r * 4 + 4].rearrange("j p c -> p j c"),
                                  "wld")
                        b.ge(ldmsk)
                    vmk = b.val("wld")

                    for h in range(H):
                        def ldk(sy, h=h, m=m, r=r, n=len(TSCM)):
                            if n >= 2:
                                sy.wait_ge(b.sem("te"), TSCM[n - 2])
                            rb = r * 2048 + h * 128
                            b.dma(sy, KH[h % 2], g["gko"][rb:rb + 128,
                                                          m * 512:m * 512 + 512], "ld")
                        b.sy(ldk)
                        vldk = b.val("ld")

                        def tscT(t, h=h, partial=partial, vldk=vldk, vldv=vldv, vmk=vmk):
                            t.wait_ge(b.sem("ld"), vldk)
                            if h == 0:
                                t.wait_ge(b.sem("vld"), vldv)
                                if partial:
                                    t.wait_ge(b.sem("wld"), vmk)
                            t.wait_ge(b.sem("se"), b.val("se"))
                            for s2 in range(4):
                                i = t.matmul(PS_A[:, s2 * 512:s2 * 512 + 512],
                                             KH[h % 2][:, s2 * 128:s2 * 128 + 128],
                                             QT[:, h * 512:h * 512 + 512],
                                             start=True, stop=(not partial))
                                if partial:
                                    i = t.matmul(PS_A[:, s2 * 512:s2 * 512 + 512], IDENT[:],
                                                 MSKB[:, s2 * 512:s2 * 512 + 512],
                                                 start=False, stop=True)
                            TSCM.append(b.mark(i, "te"))
                        b.te(tscT)
                        vts = b.val("te")

                        def sexpT(sc, h=h, vts=vts):
                            sc.wait_ge(b.sem("te"), vts)
                            i = sc.activation(out=PBUF[:], in_=PS_A[:, 0:2048],
                                              func=AF.Exp,
                                              scale=float(1.0 / np.sqrt(128.0)),
                                              bias=CNB16[:, h:h + 1])
                            b.mark(i, "se")
                        b.se(sexpT)
                        vex = b.val("se")

                        def trsav(t, h=h, kb=kb, vex=vex, n=len(AVM)):
                            t.wait_ge(b.sem("se"), vex)
                            if n >= 2:
                                t.wait_ge(b.sem("ve"), AVM[n - 2])
                            ph2 = (h % 2) * 512
                            for s2 in range(4):
                                t.matmul(PS_C[0:16, ph2:ph2 + 512],
                                         EH16[:, h * 16:h * 16 + 16],
                                         PBUF[:, s2 * 512:s2 * 512 + 512],
                                         start=(s2 == 0), stop=(s2 == 3))
                            for s2 in range(4):
                                i = t.matmul(PS_B[:, ph2:ph2 + 512],
                                             VTA[kb % 2][:, s2 * 2048 + h * 128:s2 * 2048 + h * 128 + 128],
                                             PBUF[:, s2 * 512:s2 * 512 + 512],
                                             start=(s2 == 0), stop=(s2 == 3))
                            b.mark(i, "te")
                        b.te(trsav)
                        vta2 = b.val("te")

                        def vacT(v, h=h, kb=kb, vta2=vta2):
                            v.wait_ge(b.sem("te"), vta2)
                            ph2 = (h % 2) * 512
                            if kb == 0:
                                v.tensor_copy(out=OAC[:, h * 512:h * 512 + 512],
                                              in_=PS_B[:, ph2:ph2 + 512])
                            else:
                                v.tensor_tensor(out=OAC[:, h * 512:h * 512 + 512],
                                                in0=OAC[:, h * 512:h * 512 + 512],
                                                in1=PS_B[:, ph2:ph2 + 512], op=ALU.add)
                            if kb == 0 and h == 0:
                                i = v.tensor_copy(out=RSH[0:16, :], in_=PS_C[0:16, ph2:ph2 + 512])
                            else:
                                i = v.tensor_tensor(out=RSH[0:16, :], in0=RSH[0:16, :],
                                                    in1=PS_C[0:16, ph2:ph2 + 512], op=ALU.add)
                            AVM.append(b.mark(i, "ve"))
                        b.ve(vacT)
                    kb_av.append(b.val("ve"))

                def vrsfT(v):
                    v.reciprocal(out=CT1[0:16, :], in_=RSH[0:16, :])
                    i = v.tensor_copy(out=CM[0:16, :], in_=CM[0:16, :])
                    b.mark(i, "ve")
                b.ve(vrsfT)
                vrsm = b.val("ve")

                def syrs(sy, vrsm=vrsm):
                    sy.wait_ge(b.sem("ve"), vrsm)
                    vs = b.dma(sy, g["rsb"], CT1[0:16, :], "st")
                    sy.wait_ge(b.sem("st"), vs)
                    sy.nop().then_inc(b.sem("sy"), 1)
                b.sy(syrs)
                b.s["sy"][1] += 1
                vrsy = b.val("sy")

                for h in range(H):
                    def ldrsi(sy, h=h, vrsy=vrsy):
                        if h == 0:
                            sy.wait_ge(b.sem("sy"), vrsy)
                        sy.wait_ge(b.sem("ve"), b.val("ve"))
                        v2 = b.dma(sy, RSI, g["rsb"][h:h + 1, :].partition_broadcast(128), "ld")
                        sy.wait_ge(b.sem("ld"), v2)
                        sy.nop().then_inc(b.sem("sy"), 1)
                    b.sy(ldrsi)
                    b.s["sy"][1] += 1
                    vrl = b.val("sy")

                    def vnrm(v, h=h, vrl=vrl, qb=qb):
                        v.wait_ge(b.sem("sy"), vrl)
                        i = v.tensor_tensor(out=OAC[:, h * 512:h * 512 + 512],
                                            in0=OAC[:, h * 512:h * 512 + 512],
                                            in1=RSI, op=ALU.mult)
                        if qb == 0 and h == 0:
                            dtap(v, 2048, OAC[:, 0:512])
                        b.mark(i, "ve")
                    b.ve(vnrm)
                vnd = b.val("ve")

                HIQ = arv(16, 512)
                LOQ = arv(17, 512)
                THQ = arv(18, 128)
                TLQ = arv(19, 128)
                FRQ = arv(20, 128, F32)

                AMN4 = HENDW[:, 784:1296]

                def vamax4(v, vnd=vnd):
                    v.wait_ge(b.sem("ve"), vnd)
                    v.tensor_copy(out=AMO, in_=OAC[:, 0:512])
                    v.tensor_copy(out=AMN4, in_=OAC[:, 0:512])
                    for h in range(1, H):
                        v.tensor_tensor(out=AMO, in0=AMO,
                                        in1=OAC[:, h * 512:h * 512 + 512], op=ALU.max)
                        v.tensor_tensor(out=AMN4, in0=AMN4,
                                        in1=OAC[:, h * 512:h * 512 + 512], op=ALU.min)
                    v.tensor_scalar(out=AMN4, in0=AMN4, scalar1=-1.0, scalar2=None, op0=ALU.mult)
                    v.tensor_tensor(out=AMO, in0=AMO, in1=AMN4, op=ALU.max)
                    for stp in (64, 32):
                        v.tensor_copy(out=CT1[0:stp, :], in_=AMO[stp:2 * stp, :])
                        v.tensor_tensor(out=AMO[0:stp, :], in0=AMO[0:stp, :],
                                        in1=CT1[0:stp, :], op=ALU.max)
                    v.tensor_copy(out=HIQ[0:32, :], in_=AMO[0:32, :])
                    v.tensor_copy(out=CT1[0:32, :], in_=HIQ[0:32, :])
                    v.tensor_tensor(out=CT1[0:32, :], in0=AMO[0:32, :],
                                    in1=CT1[0:32, :], op=ALU.subtract)
                    i = v.tensor_copy(out=LOQ[0:32, :], in_=CT1[0:32, :])
                    b.mark(i, "ve")
                b.ve(vamax4)
                vam4 = b.val("ve")

                def sytr4(sy, vam4=vam4):
                    sy.wait_ge(b.sem("ve"), vam4)
                    for ci in range(4):
                        b.dmat(sy, THQ[:, ci * 32:ci * 32 + 32],
                               HIQ[0:32, ci * 128:ci * 128 + 128], "tp")
                        b.dmat(sy, TLQ[:, ci * 32:ci * 32 + 32],
                               LOQ[0:32, ci * 128:ci * 128 + 128], "tp")
                    sy.wait_ge(b.sem("tp"), b.val("tp"))
                    sy.nop().then_inc(b.sem("sy"), 1)
                b.sy(sytr4)
                b.s["sy"][1] += 1
                vtr4 = b.val("sy")

                def vfr4(v, vtr4=vtr4, qb=qb):
                    v.wait_ge(b.sem("sy"), vtr4)
                    v.tensor_tensor(out=FRQ, in0=THQ[:, 0:128], in1=TLQ[:, 0:128], op=ALU.add)
                    v.tensor_scalar(out=FRQ, in0=FRQ, scalar1=1e-5,
                                    scalar2=None, op0=ALU.max)
                    i = v.reduce_max(out=CT1[:, 0:4],
                                     in_=FRQ.rearrange("p (ci c) -> p ci c", ci=4),
                                     axis=mybir.AxisListType.X)
                    if qb == 0:
                        dtap(v, 2720, FRQ[:, 0:32])
                    b.mark(i, "ve")
                b.ve(vfr4)
                vfr4m = b.val("ve")

                def syso(sy, qb=qb, vfr4m=vfr4m):
                    sy.wait_ge(b.sem("ve"), vfr4m)
                    vs = None
                    for ci in range(4):
                        cX = qb * 512 + ci * 128
                        vs = b.dma(sy, g["sxrow"][2:3, cX:cX + 128]
                                   .rearrange("a (bb c) -> (a bb) c", c=1),
                                   CT1[:, ci:ci + 1], "st")
                    sy.wait_ge(b.sem("st"), vs)
                    v3 = b.dma(sy, SOB,
                               g["sxrow"][2:3, qb * 512:qb * 512 + 512].partition_broadcast(128), "ld")
                    sy.wait_ge(b.sem("ld"), v3)
                    sy.nop().then_inc(b.sem("sy"), 1)
                b.sy(syso)
                b.s["sy"][1] += 1
                vso = b.val("sy")

                def vsoi(v, vso=vso, qb=qb):
                    v.wait_ge(b.sem("sy"), vso)
                    v.tensor_scalar(out=SOIW, in0=SOB, scalar1=1.0 / 127.0,
                                    scalar2=None, op0=ALU.mult)
                    v.reciprocal(out=SOB, in_=SOIW)
                    gb = GIV[3][:, 0:1].broadcast_to((128, 512))
                    i = v.tensor_tensor(out=SOIW, in0=SOIW, in1=gb, op=ALU.mult)
                    if qb == 0:
                        dtap(v, 2712, SOB[:, 0:4])
                        dtap(v, 2716, SOIW[:, 0:4])
                    b.mark(i, "ve")
                b.ve(vsoi)

                for h in range(H):
                    def vxi2(v, h=h, qb=qb):
                        v.tensor_tensor(out=CT1, in0=OAC[:, h * 512:h * 512 + 512],
                                        in1=SOB, op=ALU.mult)
                        v.tensor_scalar(out=CT1, in0=CT1, scalar1=MAGIC, scalar2=MAGIC,
                                        op0=ALU.add, op1=ALU.subtract)
                        i = v.tensor_scalar(out=QT[:, h * 512:h * 512 + 512], in0=CT1,
                                            scalar1=-128.0, scalar2=127.0,
                                            op0=ALU.max, op1=ALU.min)
                        if qb == 0 and h == 0:
                            dtap(v, 2752, QT[:, 0:64])
                        b.mark(i, "ve")
                    b.ve(vxi2)

                wov = g["wot"].rearrange("(j p) c -> p j c", p=128)
                for ob in range(16):
                    def glo(ge, ob=ob):
                        ge.wait_ge(b.sem("ve"), b.val("ve"))
                        b.dma(ge, WF4[ob % 2][:].rearrange("p (j c) -> p j c", c=128),
                              wov[:, :, ob * 128:ob * 128 + 128], "wld")
                    b.ge(glo)
                    vlo = b.val("wld")

                    def vqo(v, ob=ob, vlo=vlo):
                        v.wait_ge(b.sem("wld"), vlo)
                        if ob >= 1:
                            v.wait_ge(b.sem("te"), b.val("te"))
                        wf = WF4[ob % 2]
                        gb = GSW[3][:, 0:1].broadcast_to((128, 2048))
                        v.tensor_tensor(out=wf[:], in0=wf[:], in1=gb, op=ALU.mult)
                        v.tensor_scalar(out=wf[:], in0=wf[:], scalar1=MAGIC, scalar2=MAGIC,
                                        op0=ALU.add, op1=ALU.subtract)
                        i = v.tensor_scalar(out=WQ4[:], in0=wf[:], scalar1=-1.0, scalar2=1.0,
                                            op0=ALU.max, op1=ALU.min)
                        b.mark(i, "ve")
                    b.ve(vqo)
                    vqom = b.val("ve")

                    def tout(t, ob=ob, vqom=vqom):
                        t.wait_ge(b.sem("ve"), vqom)
                        for i_ in range(NT):
                            i = t.matmul(PS_B[:, (ob % 2) * 512:(ob % 2) * 512 + 512],
                                         WQ4[:, i_ * 128:i_ * 128 + 128],
                                         QT[:, i_ * 512:i_ * 512 + 512],
                                         start=(i_ == 0), stop=(i_ == NT - 1))
                        b.mark(i, "te")
                    b.te(tout)
                    vto = b.val("te")

                    def vevo(v, ob=ob, vto=vto):
                        v.wait_ge(b.sem("te"), vto)
                        v.wait_ge(b.sem("st"), b.val("st"))
                        i = v.tensor_tensor(out=CT1,
                                            in0=PS_B[:, (ob % 2) * 512:(ob % 2) * 512 + 512],
                                            in1=SOIW, op=ALU.mult)
                        b.mark(i, "ve")
                    b.ve(vevo)
                    vevom = b.val("ve")

                    def syo(sy, ob=ob, qb=qb, vevom=vevom):
                        sy.wait_ge(b.sem("ve"), vevom)
                        b.dma(sy, g["ddacc"][ob * 128:ob * 128 + 128, qb * 512:qb * 512 + 512],
                              CT1, "st")
                    b.sy(syo)
                b.barrier()

            h_a4, v_a4 = b.bump("ge")

            def gacw4(ge):
                ge.wait_ge(b.sem("st"), b.val("st"))
                YB4 = arv(0, 16 * TC)
                b.dma(ge, YB4.rearrange("p (t c) -> p t c", t=NT),
                      g["ddacc"].rearrange("(t p) c -> p t c", p=128), "wld")
                ge.wait_ge(b.sem("wld"), b.val("wld"))
                ge.nop().then_inc(h_a4, 1)
            b.ge(gacw4)
            YB4 = arv(0, 16 * TC)

            def yb4_t(tj, half):
                o = tj * TC + half * 512
                return YB4[0:128, o:o + 512]

            for en in ("vector", "scalar", "tensor"):
                def wv4b(eng, en=en):
                    eng.wait_ge(h_a4, v_a4)
                getattr(b.block, en)(wv4b)

            for half in range(2):
                mrep, rrep, vrr = ln_stats(lambda tj, h=half: yb4_t(tj, h), half)

                def vres4(v, half=half, mrep=mrep, rrep=rrep, vrr=vrr):
                    v.wait_ge(b.sem("ve"), vrr)
                    tmp = TMPS[:, half * 512:half * 512 + 512]
                    for tj in range(NT):
                        xsl = X[:, tj * TC + half * 512:tj * TC + half * 512 + 512]
                        v.tensor_tensor(out=tmp, in0=yb4_t(tj, half), in1=mrep, op=ALU.subtract)
                        v.tensor_tensor(out=tmp, in0=tmp, in1=rrep, op=ALU.mult)
                        i = v.tensor_tensor(out=xsl, in0=xsl, in1=tmp, op=ALU.add)
                    b.mark(i, "ve")
                b.ve(vres4)
            b.barrier()

        if PHASES >= 5:
            XL5 = arv(0, 16 * TC)

            def xl5_t(tj, half):
                o = tj * TC + half * 512
                return XL5[0:128, o:o + 512]

            for half in range(2):
                def vcast8(v, half=half):
                    for tj in range(NT):
                        i = v.tensor_copy(out=xl5_t(tj, half),
                                          in_=X[:, tj * TC + half * 512:tj * TC + half * 512 + 512])
                    b.mark(i, "ve")
                b.ve(vcast8)
                mrep, rrep, vrr = ln_stats(lambda tj, h=half: xl5_t(tj, h), half)

                def vnorm8(v, half=half, mrep=mrep, rrep=rrep, vrr=vrr):
                    v.wait_ge(b.sem("ve"), vrr)
                    tmp = TMPS[:, half * 512:half * 512 + 512]
                    for tj in range(NT):
                        xsl = X[:, tj * TC + half * 512:tj * TC + half * 512 + 512]
                        v.tensor_tensor(out=tmp, in0=xsl, in1=mrep, op=ALU.subtract)
                        i = v.tensor_tensor(out=xl5_t(tj, half), in0=tmp, in1=rrep, op=ALU.mult)
                    b.mark(i, "ve")
                b.ve(vnorm8)
            b.barrier()

            WG5 = [arv(32, 2048), arv(36, 2048)]
            WU5 = [arv(40, 2048), arv(44, 2048)]
            SG5 = arv(48, 512, F32)
            TS5 = arv(50, 512, F32)
            HID5 = arv(52, 16 * 1024)
            WD5 = [arv(84, 2048), arv(88, 2048)]
            ESB5 = arv(92, 1024, F32)
            wgv = g["wgt"].rearrange("(j p) c -> p j c", p=128)
            wuv = g["wut"].rearrange("(j p) c -> p j c", p=128)
            wdv = g["wdnt"].rearrange("(j p) c -> p j c", p=128)
            tg_marks = []
            for ch in range(4):
                for oi in range(16):
                    ob = ch * 16 + oi

                    def gl8(ge, ob=ob, n=len(tg_marks)):
                        if n >= 2:
                            ge.wait_ge(b.sem("te"), tg_marks[n - 2])
                        b.dma(ge, WG5[ob % 2][:].rearrange("p (j c) -> p j c", c=128),
                              wgv[:, :, ob * 128:ob * 128 + 128], "wld")
                        b.dma(ge, WU5[ob % 2][:].rearrange("p (j c) -> p j c", c=128),
                              wuv[:, :, ob * 128:ob * 128 + 128], "wld")
                    b.ge(gl8)
                    vl8 = b.val("wld")

                    def tgu(t, ob=ob, vl8=vl8):
                        t.wait_ge(b.sem("wld"), vl8)
                        t.wait_ge(b.sem("se"), b.val("se"))
                        t.wait_ge(b.sem("ve"), b.val("ve"))
                        for half in range(2):
                            for i_ in range(NT):
                                t.matmul(PS_A[:, half * 512:half * 512 + 512],
                                         WG5[ob % 2][:, i_ * 128:i_ * 128 + 128],
                                         xl5_t(i_, half), start=(i_ == 0), stop=(i_ == NT - 1))
                            for i_ in range(NT):
                                i = t.matmul(PS_A[:, 1024 + half * 512:1024 + half * 512 + 512],
                                             WU5[ob % 2][:, i_ * 128:i_ * 128 + 128],
                                             xl5_t(i_, half), start=(i_ == 0), stop=(i_ == NT - 1))
                        tg_marks.append(b.mark(i, "te"))
                    b.te(tgu)
                    vtg = b.val("te")

                    for half in range(2):
                        def ssig(sc, vtg=vtg, half=half):
                            if half == 0:
                                sc.wait_ge(b.sem("te"), vtg)
                            if half == 1:
                                sc.wait_ge(b.sem("ve"), b.val("ve"))
                            i = sc.activation(out=SG5, in_=PS_A[:, half * 512:half * 512 + 512],
                                              func=AF.Sigmoid)
                            b.mark(i, "se")
                        b.se(ssig)
                        vsg8 = b.val("se")

                        def vhid8(v, oi=oi, half=half, vsg8=vsg8):
                            v.wait_ge(b.sem("se"), vsg8)
                            v.tensor_tensor(out=TS5, in0=PS_A[:, half * 512:half * 512 + 512],
                                            in1=SG5, op=ALU.mult)
                            i = v.tensor_tensor(
                                out=HID5[:, oi * 1024 + half * 512:oi * 1024 + half * 512 + 512],
                                in0=PS_A[:, 1024 + half * 512:1024 + half * 512 + 512],
                                in1=TS5, op=ALU.mult)
                            b.mark(i, "ve")
                        b.ve(vhid8)

                for dt in range(16):
                    def gl8d(ge, ch=ch, dt=dt):
                        ge.wait_ge(b.sem("te"), b.val("te"))
                        b.dma(ge, WD5[dt % 2][:].rearrange("p (j c) -> p j c", c=128),
                              wdv[:, ch * 16:ch * 16 + 16, dt * 128:dt * 128 + 128], "wld")
                    b.ge(gl8d)
                    vl8d = b.val("wld")

                    def tdn8(t, dt=dt, vl8d=vl8d):
                        t.wait_ge(b.sem("wld"), vl8d)
                        t.wait_ge(b.sem("ve"), b.val("ve"))
                        for half in range(2):
                            for ii in range(16):
                                i = t.matmul(PS_B[:, half * 512:half * 512 + 512],
                                             WD5[dt % 2][:, ii * 128:ii * 128 + 128],
                                             HID5[:, ii * 1024 + half * 512:ii * 1024 + half * 512 + 512],
                                             start=(ii == 0), stop=(ii == 15))
                        b.mark(i, "te")
                    b.te(tdn8)
                    vtd8 = b.val("te")

                    def vev8(v, vtd8=vtd8):
                        v.wait_ge(b.sem("te"), vtd8)
                        v.wait_ge(b.sem("ac"), b.val("ac"))
                        i = v.tensor_copy(out=ESB5, in_=PS_B[:, 0:1024])
                        b.mark(i, "ve")
                    b.ve(vev8)
                    vem8 = b.val("ve")

                    def gac8(ge, ch=ch, dt=dt, vem8=vem8):
                        ge.wait_ge(b.sem("ve"), vem8)
                        if ch == 0:
                            b.dma(ge, g["ddacc"][dt * 128:dt * 128 + 128, :], ESB5, "ac")
                        else:
                            h2, _ = b.bump("ac", 16)
                            ge.dma_start(out=g["ddacc"][dt * 128:dt * 128 + 128, :], in_=ESB5,
                                         accum_op=ALU.add).then_inc(h2, 16)
                    b.ge(gac8)

                def gacw8(ge):
                    ge.wait_ge(b.sem("ac"), b.val("ac"))
                b.ge(gacw8)
                b.barrier(("vector", "tensor", "scalar", "gpsimd"))

            h_a5, v_a5 = b.bump("ge")

            def gacw5(ge):
                ge.wait_ge(b.sem("ac"), b.val("ac"))
                YB5 = arv(0, 16 * TC)
                b.dma(ge, YB5.rearrange("p (t c) -> p t c", t=NT),
                      g["ddacc"].rearrange("(t p) c -> p t c", p=128), "wld")
                ge.wait_ge(b.sem("wld"), b.val("wld"))
                ge.nop().then_inc(h_a5, 1)
            b.ge(gacw5)
            YB5 = arv(0, 16 * TC)

            def yb5_t(tj, half):
                o = tj * TC + half * 512
                return YB5[0:128, o:o + 512]

            for en in ("vector", "scalar", "tensor"):
                def wv5b(eng, en=en):
                    eng.wait_ge(h_a5, v_a5)
                getattr(b.block, en)(wv5b)

            for half in range(2):
                mrep, rrep, vrr = ln_stats(lambda tj, h=half: yb5_t(tj, h), half)

                def vres5(v, half=half, mrep=mrep, rrep=rrep, vrr=vrr):
                    v.wait_ge(b.sem("ve"), vrr)
                    tmp = TMPS[:, half * 512:half * 512 + 512]
                    for tj in range(NT):
                        xsl = X[:, tj * TC + half * 512:tj * TC + half * 512 + 512]
                        v.tensor_tensor(out=tmp, in0=yb5_t(tj, half), in1=mrep, op=ALU.subtract)
                        v.tensor_tensor(out=tmp, in0=tmp, in1=rrep, op=ALU.mult)
                        i = v.tensor_tensor(out=xsl, in0=xsl, in1=tmp, op=ALU.add)
                    b.mark(i, "ve")
                b.ve(vres5)
            b.barrier()
        def pout(sy):
            sy.wait_ge(b.sem("ve"), b.val("ve"))
            v = b.dma(sy, g["yt"].rearrange("(t p) c -> p t c", p=128),
                      X[:].rearrange("p (t c) -> p t c", t=NT), "st")
            if DEBUG:
                v = b.dma(sy, g["dbg"], DBGT[:], "st")
            sy.wait_ge(b.sem("st"), v)
        b.sy(pout)

    return nc


_NC_CACHE = {}


def kernel(**inputs):
    x = np.asarray(inputs["L0_attn_x"][0])
    xT = np.ascontiguousarray(x.T)
    eye = np.eye(128, dtype=np.float32)
    ones = np.ones((128, 128), dtype=np.float32)

    consts = np.zeros((128, 8), dtype=np.float32)
    for ci, cv in enumerate([EPS, 1.0 / np.sqrt(128.0), 1.0, 0.0]):
        consts[:, ci] = np.float32(cv)
    shared = {
        "wx": np.ascontiguousarray(np.asarray(inputs["L0_attn_W_xproj"]).T),
        "ones": ones, "ident": eye, "consts": consts,
    }
    if PHASES >= 3:
        shared["w1t"] = np.ascontiguousarray(np.asarray(inputs["L0_ffn_w1"]).T)
        shared["w2t"] = np.ascontiguousarray(np.asarray(inputs["L0_ffn_w2"]).T)
        shared["w3t"] = np.ascontiguousarray(np.asarray(inputs["L0_ffn_w3"]).T)
    if PHASES >= 4:
        shared["wqt"] = np.ascontiguousarray(np.asarray(inputs["L1_attn_w_q"]).T)
        shared["wkt"] = np.ascontiguousarray(np.asarray(inputs["L1_attn_w_k"]).T)
        shared["wvt"] = np.ascontiguousarray(np.asarray(inputs["L1_attn_w_v"]).T)
        shared["wot"] = np.ascontiguousarray(np.asarray(inputs["L1_attn_w_o"]).T)
        eh = np.zeros((128, 256), dtype=np.float32)
        for hh in range(16):
            eh[:, hh * 16 + hh] = 1.0
        shared["eh16"] = eh
    if PHASES >= 5:
        shared["wgt"] = np.ascontiguousarray(np.asarray(inputs["L1_ffn_w_gate"]).T)
        shared["wut"] = np.ascontiguousarray(np.asarray(inputs["L1_ffn_w_up"]).T)
        shared["wdnt"] = np.ascontiguousarray(np.asarray(inputs["L1_ffn_w_down"]).T)
    wdtT = np.ascontiguousarray(np.asarray(inputs["L0_attn_W_dt"]).T)

    in_maps = []
    idxs = []
    for c in range(NCORES):
        ix = tok_idx(c)
        idxs.append(ix)
        m = dict(shared)
        m["xt"] = np.ascontiguousarray(xT[:, ix])
        m["xd"] = np.ascontiguousarray(xT[DSL * c:DSL * c + DSL, :])
        m["wdt"] = np.ascontiguousarray(wdtT[:, DSL * c:DSL * c + DSL])
        m["bdt"] = np.ascontiguousarray(
            np.asarray(inputs["L0_attn_b_dt"])[DSL * c:DSL * c + DSL].reshape(2, 128).T)
        m["asl"] = np.ascontiguousarray(
            np.asarray(inputs["L0_attn_A"])[DSL * c:DSL * c + DSL].reshape(2, 128, NST)
            .transpose(1, 0, 2).reshape(128, 2 * NST))
        m["dsl"] = np.ascontiguousarray(
            np.asarray(inputs["L0_attn_D"])[DSL * c:DSL * c + DSL].reshape(2, 128).T)
        if PHASES >= 4:
            mk = np.full((8, 4, 128, 512), NEG, dtype=np.float32)
            o = np.arange(64)
            diagT = np.where(o[:, None] <= o[None, :], 0.0, NEG).astype(np.float32)
            for r in range(8):
                for s2 in range(4):
                    for kk2 in range(2):
                        sbl = s2 * 2 + kk2
                        rows = slice(kk2 * 64, kk2 * 64 + 64)
                        mk[r, s2, rows, 64 * (sbl + 1):] = 0.0
                        if r < c:
                            mk[r, s2, rows, 64 * sbl:64 * sbl + 64] = 0.0
                        elif r == c:
                            mk[r, s2, rows, 64 * sbl:64 * sbl + 64] = diagT
            m["masks"] = np.ascontiguousarray(mk.reshape(32, 128, 512))
        in_maps.append(m)

    if PHASES not in _NC_CACHE:
        _NC_CACHE[PHASES] = build_program()
    nc = _NC_CACHE[PHASES]
    res = run_bass_kernel_spmd(nc, in_maps, core_ids=list(range(NCORES)))

    out = np.zeros((S, D), dtype=np.float32)
    for c in range(NCORES):
        out[idxs[c], :] = res.results[c]["yt"].T
    return out[None]


# version 04 (final)
# speedup vs baseline: 1.6411x; 1.0704x over previous
import numpy as np
import ml_dtypes
from contextlib import ExitStack
import concourse.bass as bass
import concourse.mybir as mybir
from concourse.bass_utils import run_bass_kernel_spmd
from concourse.aot_env import allow_ici_all_to_all

F32 = mybir.dt.float32
BF16 = mybir.dt.bfloat16
I32 = mybir.dt.int32
U16 = mybir.dt.uint16
AF = mybir.ActivationFunctionType
ALU = mybir.AluOpType
MAGIC = float(np.float32(1.5 * 2 ** 23))
NEG = -1e30

NCORES = 8
S = 8192
TC = 1024
D = 2048
NT = 16
DSL = 256
NST = 16
H = 16
DH = 128
HFF0 = 5632
HFF1 = 8192
EPS = 1e-5

PHASES = 5
DEBUG = False
BF = ml_dtypes.bfloat16
KBb = 512


def tok_idx(c):
    return np.concatenate([np.arange(64 * (8 * jj + c), 64 * (8 * jj + c) + 64)
                           for jj in range(16)])


class Build:
    def __init__(self, nc, ctx):
        self.nc = nc
        self.ctx = ctx
        self.block = ctx.enter_context(nc.Block())
        self.s = {}

    def sem(self, name):
        if name not in self.s:
            self.s[name] = [self.ctx.enter_context(self.nc.semaphore(name)), 0]
        return self.s[name][0]

    def val(self, name):
        self.sem(name)
        return self.s[name][1]

    def bump(self, name, k=1):
        h = self.sem(name)
        self.s[name][1] += k
        return h, self.s[name][1]

    def dma(self, eng, out, in_, q):
        h, v = self.bump(q, 16)
        eng.dma_start(out=out, in_=in_).then_inc(h, 16)
        return v

    def dmat(self, eng, out, in_, q):
        h, v = self.bump(q, 16)
        eng.dma_start_transpose(out=out, in_=in_).then_inc(h, 16)
        return v

    def mark(self, instr, eng):
        h, v = self.bump(eng, 1)
        instr.then_inc(h, 1)
        return v

    def te(self, fn):
        self.block.tensor(fn)

    def ve(self, fn):
        self.block.vector(fn)

    def se(self, fn):
        self.block.scalar(fn)

    def ge(self, fn):
        self.block.gpsimd(fn)

    def sy(self, fn):
        self.block.sync(fn)

    def barrier(self, names=("tensor", "vector", "scalar", "gpsimd", "sync")):
        vals = {n: self.val("bar_" + n) + 1 for n in names}
        for n in names:
            def mk(eng, me=n):
                eng.nop().then_inc(self.sem("bar_" + me), 1)
                for o in names:
                    if o != me:
                        eng.wait_ge(self.sem("bar_" + o), vals[o])
            getattr(self.block, n)(mk)
        for n in names:
            self.s["bar_" + n][1] += 1


def build_program():
    nc = bass.Bass("TRN2", target_bir_lowering=False, debug=False)
    g = {}

    def din(name, shape, dtype=F32):
        g[name] = nc.dram_tensor(name, shape, dtype, kind="ExternalInput").ap()

    def dout(name, shape, dtype=F32):
        g[name] = nc.dram_tensor(name, shape, dtype, kind="ExternalOutput").ap()

    def dscr(name, shape, dtype):
        g[name] = nc.dram_tensor(name, shape, dtype).ap()

    din("xt", [D, TC])
    din("xd", [DSL, S])
    din("wx", [D, 160])
    din("wdt", [128, DSL])
    din("bdt", [128, 2])
    din("asl", [128, 2 * NST])
    din("dsl", [128, 2])
    din("ones", [128, 128])
    din("ident", [128, 128])
    din("consts", [128, 8])
    if PHASES >= 3:
        din("w1t", [D, HFF0])
        din("w2t", [D, HFF0])
        din("w3t", [HFF0, D])
    if PHASES >= 4:
        din("wqt", [D, D])
        din("wkt", [D, D])
        din("wvt", [D, D])
        din("wot", [D, D])
        din("masks", [32, 128, 512])
        din("eh16", [128, 256])
    if PHASES >= 5:
        din("wgt", [D, HFF1])
        din("wut", [D, HFF1])
        din("wdnt", [HFF1, D])
    dout("yt", [D, TC])
    if DEBUG:
        dout("dbg", [128, 3072])

    dscr("g1i", [TC, 256], BF16)
    dscr("g1o", [S, 256], BF16)
    dscr("g1si", [2, TC], F32)
    dscr("g1so", [2 * NCORES, TC], F32)
    dscr("bcc", [32, S], BF16)
    dscr("stc", [2, S], F32)
    dscr("g2i", [S, DSL], BF16)
    dscr("g2o", [S, DSL], BF16)
    if PHASES >= 3:
        dscr("ddacc", [D, TC], F32)
    if PHASES >= 4:
        dscr("gki", [D, TC], BF16)
        dscr("gko", [NCORES * D, TC], BF16)
        dscr("gvi", [TC, D], BF16)
        dscr("gvo", [S, D], BF16)
        dscr("g5i", [1, 128], F32)
        dscr("g5o", [NCORES, 128], F32)
        dscr("mkb", [1, 32], F32)
        dscr("rsb", [16, 512], F32)
        dscr("sxrow", [4, TC], F32)
        dscr("qd", [D, TC], BF16)

    ctx = ExitStack()
    with ctx:
        b = Build(nc, ctx)
        E = ctx.enter_context

        X = E(nc.sbuf_tensor("X", [128, NT * TC], F32))
        AR = E(nc.sbuf_tensor("AR", [128, 100 * KBb], BF16, is_transpose_dest=True))
        WX = E(nc.sbuf_tensor("WX", [128, NT * 160], BF16))
        WDT = E(nc.sbuf_tensor("WDT", [128, DSL], BF16))
        ONES = E(nc.sbuf_tensor("ONES", [128, 128], BF16))
        IDENT = E(nc.sbuf_tensor("IDENT", [128, 128], BF16))
        BDT = E(nc.sbuf_tensor("BDT", [128, 2], F32))
        ASL = E(nc.sbuf_tensor("ASL", [128, 2 * NST], F32))
        DS = E(nc.sbuf_tensor("DS", [128, 2], F32))
        HENDW = E(nc.sbuf_tensor("HENDW", [128, 32 * 128], F32))
        HBQN = E(nc.sbuf_tensor("HBQN", [128, 2056], F32))
        CONSTS = E(nc.sbuf_tensor("CONSTS", [128, 8], F32))
        CONST_VALS = [EPS, 1.0 / np.sqrt(128.0), 1.0, 0.0, -1.0]
        for ci, cv in enumerate(CONST_VALS):
            nc.const_aps.register(F32, float(cv), CONSTS[:, ci:ci + 1])
            if float(np.float32(cv)) != float(cv):
                nc.const_aps.register(F32, float(np.float32(cv)), CONSTS[:, ci:ci + 1])
        STG = E(nc.sbuf_tensor("STG", [128, 256], BF16, is_transpose_dest=True))
        EH16 = E(nc.sbuf_tensor("EH16", [128, 256], BF16))
        GAM = None
        DBGT = E(nc.sbuf_tensor("DBGT", [128, 3072], F32)) if DEBUG else None

        def dtap(v, col, src):
            if DEBUG:
                v.tensor_copy(out=DBGT[:, col:col + src.shape[-1]], in_=src)
        PS_A = E(nc.psum_tensor("PS_A", [128, 2048], F32))
        PS_B = E(nc.psum_tensor("PS_B", [128, 1024], F32))
        PS_C = E(nc.psum_tensor("PS_C", [128, 1024], F32))

        def arv(off_kb, nelem, dtype=BF16, p0=0, p1=128):
            i0 = off_kb * KBb
            if dtype == BF16:
                return AR[p0:p1, i0:i0 + nelem]
            nbf = nelem * (4 if dtype in (F32, I32) else 2) // 2
            return AR[p0:p1, i0:i0 + nbf].bitcast(dtype)

        XLN = arv(0, 16 * TC)
        MREP = arv(80, TC, F32)
        RREP = arv(84, TC, F32)
        TMPS = arv(88, TC, F32)
        SQT = [arv(92, 512), arv(93, 512)]
        CONA = arv(94, 1024)
        CONB = arv(96, 1024)
        GB = [arv(98, 256), arv(98 + 1, 256)][0:1] + [AR[:, 98 * KBb + 256:98 * KBb + 512]]

        def xln_t(tj, half):
            o = tj * TC + half * 512
            return XLN[:, o:o + 512] if False else XLN[0:128, o:o + 512]

        def p0(sy):
            b.dma(sy, X[:].rearrange("p (t c) -> p t c", t=NT),
                  g["xt"].rearrange("(t p) c -> p t c", p=128), "ld")
            b.dma(sy, BDT[:], g["bdt"], "ld")
            b.dma(sy, DS[:], g["dsl"], "ld")
            b.dma(sy, ASL[:], g["asl"], "ld")
            b.dma(sy, CONSTS[:], g["consts"], "ld")
            sy.wait_ge(b.sem("ld"), b.val("ld"))
        b.sy(p0)

        def p0g(ge):
            b.dma(ge, WX[:].rearrange("p (t c) -> p t c", t=NT),
                  g["wx"].rearrange("(t p) c -> p t c", p=128), "wld")
            b.dma(ge, WDT[:], g["wdt"], "wld")
            b.dma(ge, ONES[:], g["ones"], "wld")
            b.dma(ge, IDENT[:], g["ident"], "wld")
            if PHASES >= 4:
                b.dma(ge, EH16[:], g["eh16"], "wld")
            ge.wait_ge(b.sem("wld"), b.val("wld"))
        b.ge(p0g)
        b.barrier()

        def ln_stats(tiles_bf, half):
            mrep = MREP[:, half * 512:half * 512 + 512]
            rrep = RREP[:, half * 512:half * 512 + 512]
            tmp = TMPS[:, half * 512:half * 512 + 512]
            se_base = b.val("se")
            te_base = b.val("te")
            ve_guard = b.val("ve")
            for tj in range(NT):
                def sq(sc, tj=tj):
                    if tj == 0:
                        sc.wait_ge(b.sem("ve"), ve_guard)
                    if tj >= 2:
                        sc.wait_ge(b.sem("te"), te_base + tj - 1)
                    i = sc.activation(out=SQT[tj % 2], in_=tiles_bf(tj), func=AF.Square)
                    b.mark(i, "se")
                b.se(sq)

                def mm(t, tj=tj):
                    if tj == 0:
                        t.wait_ge(b.sem("ve"), ve_guard)
                    t.wait_ge(b.sem("se"), se_base + tj + 1)
                    t.matmul(PS_B[:, 0:512], ONES[:], tiles_bf(tj),
                             start=(tj == 0), stop=(tj == NT - 1))
                    i = t.matmul(PS_B[:, 512:1024], ONES[:], SQT[tj % 2],
                                 start=(tj == 0), stop=(tj == NT - 1))
                    b.mark(i, "te")
                b.te(mm)
            vte = b.val("te")
            hold = {}

            def vstat(v):
                v.wait_ge(b.sem("te"), vte)
                v.tensor_scalar(out=mrep, in0=PS_B[:, 0:512], scalar1=1.0 / D,
                                scalar2=None, op0=ALU.mult)
                v.tensor_scalar(out=tmp, in0=PS_B[:, 512:1024], scalar1=1.0 / D,
                                scalar2=None, op0=ALU.mult)
                v.tensor_tensor(out=rrep, in0=mrep, in1=mrep, op=ALU.mult)
                i = v.tensor_tensor(out=tmp, in0=tmp, in1=rrep, op=ALU.subtract)
                hold["v1"] = b.mark(i, "ve")
            b.ve(vstat)

            def sstat(sc):
                sc.wait_ge(b.sem("ve"), hold["v1"])
                i = sc.activation(out=tmp, in_=tmp, func=AF.Sqrt, bias=EPS)
                hold["s1"] = b.mark(i, "se")
            b.se(sstat)

            def vrec(v):
                v.wait_ge(b.sem("se"), hold["s1"])
                i = v.reciprocal(out=rrep, in_=tmp)
                hold["v2"] = b.mark(i, "ve")
            b.ve(vrec)
            return mrep, rrep, hold["v2"]

        ln1_stats = []
        for half in range(2):
            def vcast(v, half=half):
                for tj in range(NT):
                    i = v.tensor_copy(out=xln_t(tj, half),
                                      in_=X[:, tj * TC + half * 512:tj * TC + half * 512 + 512])
                b.mark(i, "ve")
            b.ve(vcast)
            mrep, rrep, vrr = ln_stats(lambda tj, h=half: xln_t(tj, h), half)

            def vnorm(v, half=half, mrep=mrep, rrep=rrep, vrr=vrr):
                v.wait_ge(b.sem("ve"), vrr)
                tmp = TMPS[:, half * 512:half * 512 + 512]
                for tj in range(NT):
                    xsl = X[:, tj * TC + half * 512:tj * TC + half * 512 + 512]
                    v.tensor_tensor(out=tmp, in0=xsl, in1=mrep, op=ALU.subtract)
                    i = v.tensor_tensor(out=xln_t(tj, half), in0=tmp, in1=rrep, op=ALU.mult)
                b.mark(i, "ve")
            b.ve(vnorm)
            ln1_stats.append((mrep, rrep))

        def vtap1(v):
            pass
        b.ve(vtap1)
        b.barrier()

        for half in range(2):
            ve_guard = b.val("ve")

            def temm(t, half=half, ve_guard=ve_guard):
                t.wait_ge(b.sem("ve"), ve_guard)
                for tj in range(NT):
                    t.matmul(PS_A[:, 0:512], WX[:, tj * 160:tj * 160 + 128],
                             xln_t(tj, half), start=(tj == 0), stop=(tj == NT - 1))
                    i = t.matmul(PS_C[0:32, 0:512], WX[:, tj * 160 + 128:tj * 160 + 160],
                                 xln_t(tj, half), start=(tj == 0), stop=(tj == NT - 1))
                b.mark(i, "te")
            b.te(temm)
            vte = b.val("te")
            mrep, rrep = ln1_stats[half]

            def vevac(v, half=half, vte=vte):
                v.wait_ge(b.sem("te"), vte)
                v.tensor_copy(out=CONA[:, half * 512:half * 512 + 512], in_=PS_A[:, 0:512])
                i = v.tensor_copy(out=CONB[0:32, half * 512:half * 512 + 512], in_=PS_C[0:32, 0:512])
                b.mark(i, "ve")
            b.ve(vevac)

            def systat(sy, half=half, mrep=mrep, rrep=rrep):
                sy.wait_ge(b.sem("ve"), b.val("ve"))
                b.dma(sy, g["g1si"][0:1, half * 512:half * 512 + 512], mrep[0:1, :], "st")
                b.dma(sy, g["g1si"][1:2, half * 512:half * 512 + 512], rrep[0:1, :], "st")
            b.sy(systat)
        vcon = b.val("ve")

        def syt(sy):
            sy.wait_ge(b.sem("ve"), vcon)
            for tb in range(8):
                b.dmat(sy, STG[:, 0:128], CONA[:, tb * 128:tb * 128 + 128], "tp")
                vt = b.dmat(sy, STG[:, 128:256], CONB[:, tb * 128:tb * 128 + 128], "tp")
                sy.wait_ge(b.sem("tp"), vt)
                vs = b.dma(sy, g["g1i"][tb * 128:tb * 128 + 128, :], STG[:], "st")
                sy.wait_ge(b.sem("st"), vs)
        b.sy(syt)

        vsy_h, vsy = b.bump("sy")

        def symark(sy):
            sy.nop().then_inc(vsy_h, 1)
        b.sy(symark)

        def gcc1b(ge):
            ge.wait_ge(vsy_h, vsy)
            i = ge.collective_compute(
                "AllGather", ALU.bypass, replica_groups=[list(range(NCORES))],
                ins=[g["g1i"].opt()], outs=[g["g1o"].opt()])
            h, _ = b.bump("cc", 1)
            i.then_inc(h, 1)
            i = ge.collective_compute(
                "AllGather", ALU.bypass, replica_groups=[list(range(NCORES))],
                ins=[g["g1si"].opt()], outs=[g["g1so"].opt()])
            h, _ = b.bump("cc", 1)
            i.then_inc(h, 1)
            ge.wait_ge(b.sem("cc"), b.val("cc"))
            stv = g["stc"].rearrange("n (i1 r i0) -> n r i1 i0", r=8, i0=64)
            for r in range(NCORES):
                sv = g["g1so"].rearrange("n (i1 i0) -> n i1 i0", i0=64)
                b.dma(ge, stv[0:1, r], sv[2 * r:2 * r + 1], "wld")
                b.dma(ge, stv[1:2, r], sv[2 * r + 1:2 * r + 2], "wld")
            ge.wait_ge(b.sem("wld"), b.val("wld"))
        b.ge(gcc1b)
        vcc1 = b.val("cc")
        vstc = b.val("wld")

        DTFM = arv(32, 8192)
        BCFM = arv(48, 8192)

        def p2b(sy):
            sy.wait_ge(b.sem("cc"), vcc1)
            for blk in range(64):
                gb = GB[blk % 2]
                if blk >= 2:
                    sy.wait_ge(b.sem("tp"), b.val("tp"))
                vl = b.dma(sy, gb, g["g1o"][blk * 128:blk * 128 + 128, :], "ld")
                sy.wait_ge(b.sem("ld"), vl)
                b.dmat(sy, DTFM[:, blk * 128:blk * 128 + 128], gb[:, 0:128], "tp")
                b.dmat(sy, BCFM[:, blk * 128:blk * 128 + 128], gb[:, 128:256], "tp")
            sy.wait_ge(b.sem("tp"), b.val("tp"))
            bco = g["bcc"].rearrange("n (i1 r i0) -> n r i1 i0", r=8, i0=64)
            bcs = BCFM.rearrange("p (r i1 i0) -> p r i1 i0", r=8, i0=64)
            for r in range(8):
                b.dma(sy, bco[:, r], bcs[0:32, r], "st")
            sy.wait_ge(b.sem("st"), b.val("st"))
            sy.nop().then_inc(b.sem("sy"), 1)
        b.sy(p2b)
        b.s["sy"][1] += 1
        vsy2 = b.val("sy")

        b.barrier()

        QW = 2048
        NQ = 4
        SD = 2052
        DELTAP = arv(0, 2 * NQ * SD)

        def dchunk(tj, q):
            o = (tj * NQ + q) * SD
            return DELTAP[:, o:o + 2050]

        def vpfx(v):
            for tj in range(2):
                for q in range(NQ):
                    o = (tj * NQ + q) * SD
                    i = v.memset(DELTAP[:, o:o + 2], 30000.0)
            b.mark(i, "ve")
        b.ve(vpfx)

        dtv = DTFM.rearrange("p (r i1 i0) -> p r i1 i0", r=8, i0=64)
        se_base = b.val("se")
        ve_base_c = b.val("ve")
        for k in range(32):
            tj2, ch = divmod(k, 16)

            def dmm(t, k=k, tj2=tj2, ch=ch):
                if k >= 2:
                    t.wait_ge(b.sem("se"), se_base + k - 1)
                    t.wait_ge(b.sem("ve"), ve_base_c + k - 1)
                i = t.matmul(PS_B[:, (k % 2) * 512:(k % 2) * 512 + 512],
                             WDT[:, tj2 * 128:tj2 * 128 + 128], dtv[:, :, ch, :],
                             start=True, stop=True)
                b.mark(i, "te")
            b.te(dmm)
            vte = b.val("te")

            def dact(sc, k=k, tj2=tj2, ch=ch, vte=vte):
                sc.wait_ge(b.sem("te"), vte)
                if k >= 2:
                    sc.wait_ge(b.sem("ve"), ve_base_c + k - 1)
                ps = PS_B[:, (k % 2) * 512:(k % 2) * 512 + 512]
                spt = MREP[:, (k % 2) * 512:(k % 2) * 512 + 512]
                sc.activation(out=spt, in_=ps, func=AF.Abs, bias=BDT[:, tj2:tj2 + 1])
                sc.activation(out=spt, in_=spt, func=AF.Exp, scale=-1.0)
                i = sc.activation(out=spt, in_=spt, func=AF.Ln, bias=1.0)
                b.mark(i, "se")
            b.se(dact)
            vse_sp = b.val("se")

            def dve_sp(v, k=k, tj2=tj2, ch=ch, vse_sp=vse_sp):
                v.wait_ge(b.sem("se"), vse_sp)
                ps = PS_B[:, (k % 2) * 512:(k % 2) * 512 + 512]
                spt = MREP[:, (k % 2) * 512:(k % 2) * 512 + 512]
                spt2 = RREP[:, (k % 2) * 512:(k % 2) * 512 + 512]
                v.tensor_scalar(out=spt2, in0=ps, scalar1=BDT[:, tj2:tj2 + 1],
                                scalar2=0.0, op0=ALU.add, op1=ALU.max)
                dout_ = dchunk(tj2, ch // 4)
                off = 2 + (ch % 4) * 512
                i = v.tensor_tensor(out=dout_[:, off:off + 512], in0=spt,
                                    in1=spt2, op=ALU.add)
                b.mark(i, "ve")
            b.ve(dve_sp)

        b.barrier()

        XSQ = arv(33, QW)
        UQP = arv(37, 2050)
        YQ = PS_A[:, 0:QW]
        ADQ = [arv(50, 2050), arv(55, 2050)]
        BXQ = arv(60, 2050)
        XDQ = arv(50, QW, F32)
        HBQ = HBQN[:, 0:2050]
        BMBP = [arv(66, 2050), arv(71, 2050)]
        CBB = [arv(76, QW), arv(80, QW)]
        CSQ = arv(84, 2052, F32)
        ZD = arv(93, 2050)
        MEANB = arv(84, QW, F32)
        RSTDB = arv(66, QW, F32)
        PQ = [arv(33, QW), ZD[:, 2:2050]]

        def vhz(v):
            v.memset(HENDW[:], 0.0)
            v.memset(UQP[:, 0:2], 0.0)
            i = v.memset(ZD[:, 0:2], 0.0)
            b.mark(i, "ve")
        b.ve(vhz)
        b.barrier(("vector", "sync", "scalar"))

        for q in range(NQ):
            for tj in range(2):
                def ldq(sy, q=q, tj=tj):
                    b.dma(sy, XDQ, g["xd"][tj * 128:tj * 128 + 128, q * QW:q * QW + QW], "ld")
                    b.dma(sy, MEANB, g["stc"][0:1, q * QW:q * QW + QW].partition_broadcast(128), "ld")
                    b.dma(sy, RSTDB, g["stc"][1:2, q * QW:q * QW + QW].partition_broadcast(128), "ld")
                b.sy(ldq)
                vld = b.val("ld")

                def vxs(v, tj=tj, q=q, vld=vld):
                    v.wait_ge(b.sem("ld"), vld)
                    v.tensor_tensor(out=XSQ, in0=XDQ, in1=MEANB, op=ALU.subtract)
                    v.tensor_tensor(out=XSQ, in0=XSQ, in1=RSTDB, op=ALU.mult)
                    v.tensor_scalar(out=YQ, in0=XSQ, scalar1=DS[:, tj:tj + 1],
                                    scalar2=None, op0=ALU.mult)
                    v.tensor_tensor(out=UQP[:, 2:2050], in0=dchunk(tj, q)[:, 2:2050],
                                    in1=XSQ, op=ALU.mult)
                    v.tensor_copy(out=ZD[:, 2:2050], in_=dchunk(tj, q)[:, 2:2050])
                    i = v.tensor_tensor_scan(out=CSQ[:, 0:2050], data0=ZD, data1=ZD,
                                             initial=0.0, op0=ALU.add, op1=ALU.bypass)
                    b.mark(i, "ve")
                b.ve(vxs)
                ve_q0 = b.val("ve")

                for n in range(NST):
                    pb = n % 2
                    slot = tj * NST + n

                    def ldbc(sy, q=q, n=n, pb=pb, ve_q0=ve_q0):
                        if n < 2:
                            sy.wait_ge(b.sem("ve"), ve_q0)
                        else:
                            sy.wait_ge(b.sem("ve"), ve_q0 + n - 1)
                        b.dma(sy, BMBP[pb][:, 2:2050],
                              g["bcc"][n:n + 1, q * QW:q * QW + QW].partition_broadcast(128), "bc%d" % pb)
                        b.dma(sy, CBB[pb],
                              g["bcc"][16 + n:17 + n, q * QW:q * QW + QW].partition_broadcast(128), "bc%d" % pb)
                    b.sy(ldbc)
                    vbc = b.val("bc%d" % pb)

                    def sad(sc, tj=tj, n=n, q=q, pb=pb, ve_q0=ve_q0):
                        if n == 0:
                            sc.wait_ge(b.sem("ve"), ve_q0)
                        if n >= 2:
                            sc.wait_ge(b.sem("ve"), ve_q0 + n - 1)
                        sc.activation(out=ADQ[pb], in_=dchunk(tj, q),
                                      func=AF.Exp, scale=ASL[:, tj * NST + n:tj * NST + n + 1])
                        i = sc.activation(out=PQ[pb], in_=CSQ[:, 2:2050],
                                          func=AF.Exp, scale=ASL[:, tj * NST + n:tj * NST + n + 1])
                        b.mark(i, "se")
                    b.se(sad)
                    vse = b.val("se")

                    def vscan(v, tj=tj, n=n, pb=pb, vbc=vbc, vse=vse, q=q, slot=slot):
                        v.wait_ge(b.sem("bc%d" % pb), vbc)
                        v.wait_ge(b.sem("se"), vse)
                        v.tensor_tensor(out=BXQ, in0=UQP, in1=BMBP[pb], op=ALU.mult)
                        v.tensor_tensor_scan(out=HBQ, data0=ADQ[pb], data1=BXQ,
                                             initial=0.0, op0=ALU.mult, op1=ALU.add)
                        hsb = HENDW[:, slot * 128 + 127:slot * 128 + 128].broadcast_to((128, QW))
                        v.tensor_tensor(out=BXQ[:, 2:2050], in0=PQ[pb], in1=hsb, op=ALU.mult)
                        v.tensor_tensor(out=HBQ[:, 2:2050], in0=HBQ[:, 2:2050],
                                        in1=BXQ[:, 2:2050], op=ALU.add)
                        v.tensor_copy(out=HENDW[:, slot * 128:slot * 128 + 128],
                                      in_=HBQ[:, 2050 - 128:2050])
                        v.tensor_tensor(out=HBQ[:, 2:2050], in0=HBQ[:, 2:2050], in1=CBB[pb], op=ALU.mult)
                        i = v.tensor_tensor(out=YQ, in0=YQ, in1=HBQ[:, 2:2050], op=ALU.add)
                        b.mark(i, "ve")
                    b.ve(vscan)

                def vyc(v, q=q, tj=tj):
                    i = v.tensor_copy(out=XSQ, in_=YQ)
                    b.mark(i, "ve")
                b.ve(vyc)
                vym = b.val("ve")

                def syy(sy, q=q, tj=tj, vym=vym):
                    sy.wait_ge(b.sem("ve"), vym)
                    for blk in range(QW // 128):
                        sy.wait_ge(b.sem("st"), b.val("st"))
                        vtt = b.dmat(sy, STG[:, (blk % 2) * 128:(blk % 2) * 128 + 128],
                                     XSQ[:, blk * 128:blk * 128 + 128], "tp")
                        sy.wait_ge(b.sem("tp"), vtt)
                        t128 = q * (QW // 128) + blk
                        for k2 in range(2):
                            g64 = (2 * t128 + k2) % 8
                            jj = (2 * t128 + k2) // 8
                            b.dma(sy, g["g2i"][g64 * 1024 + jj * 64:g64 * 1024 + jj * 64 + 64,
                                               tj * 128:tj * 128 + 128],
                                  STG[k2 * 64:k2 * 64 + 64, (blk % 2) * 128:(blk % 2) * 128 + 128], "st")
                b.sy(syy)
                b.barrier(("vector", "sync", "scalar"))

        h_sy, v_sy = b.bump("sy")

        def symark2(sy):
            sy.wait_ge(b.sem("st"), b.val("st"))
            sy.nop().then_inc(h_sy, 1)
        b.sy(symark2)

        def gcc2b(ge):
            ge.wait_ge(h_sy, v_sy)
            with allow_ici_all_to_all():
                i = ge.collective_compute(
                    "AllToAll", ALU.bypass, replica_groups=[list(range(NCORES))],
                    ins=[g["g2i"].opt()], outs=[g["g2o"].opt()])
            h, _ = b.bump("cc", 1)
            i.then_inc(h, 1)
        b.ge(gcc2b)
        vcc2 = b.val("cc")

        YF = arv(32, 16 * TC)

        def p2e(sy):
            sy.wait_ge(b.sem("cc"), vcc2)
            for k in range(64):
                r, tb = divmod(k, 8)
                gb = GB[k % 2]
                if k >= 2:
                    sy.wait_ge(b.sem("tp"), b.val("tp"))
                vl = b.dma(sy, gb, g["g2o"][r * 1024 + tb * 128:r * 1024 + tb * 128 + 128, :], "ld")
                sy.wait_ge(b.sem("ld"), vl)
                for dd in range(2):
                    gt = 2 * r + dd
                    b.dmat(sy, YF[:, gt * TC + tb * 128:gt * TC + tb * 128 + 128],
                           gb[:, dd * 128:dd * 128 + 128], "tp")
            sy.wait_ge(b.sem("tp"), b.val("tp"))
            sy.nop().then_inc(b.sem("sy"), 1)
        b.sy(p2e)
        b.s["sy"][1] += 1
        vsy3 = b.val("sy")

        def yf_t(tj, half):
            o = tj * TC + half * 512
            return YF[0:128, o:o + 512]


        for half in range(2):
            if half == 0:
                def wv(v):
                    v.wait_ge(b.sem("sy"), vsy3)
                b.ve(wv)

                def ws(sc):
                    sc.wait_ge(b.sem("sy"), vsy3)
                b.se(ws)

                def wt(t):
                    t.wait_ge(b.sem("sy"), vsy3)
                b.te(wt)
            mrep, rrep, vrr = ln_stats(lambda tj, h=half: yf_t(tj, h), half)

            def vres(v, half=half, mrep=mrep, rrep=rrep, vrr=vrr):
                v.wait_ge(b.sem("ve"), vrr)
                tmp = TMPS[:, half * 512:half * 512 + 512]
                for tj in range(NT):
                    xsl = X[:, tj * TC + half * 512:tj * TC + half * 512 + 512]
                    v.tensor_tensor(out=tmp, in0=yf_t(tj, half), in1=mrep, op=ALU.subtract)
                    v.tensor_tensor(out=tmp, in0=tmp, in1=rrep, op=ALU.mult)
                    i = v.tensor_tensor(out=xsl, in0=xsl, in1=tmp, op=ALU.add)
                b.mark(i, "ve")
            b.ve(vres)
        b.barrier()

        if PHASES >= 3:
            WFA = [arv(0, 5632, F32), arv(23, 5632, F32)]
            ABSB = arv(46, 5632)
            EVB = arv(57, 128)
            GSC = [arv(58, 128, F32), arv(59, 128, F32), arv(60, 128, F32)]
            wspec = [("w1t", D, HFF0), ("w2t", D, HFF0), ("w3t", HFF0, D)]
            for wi, (wn, wrows, wcols) in enumerate(wspec):
                nj = wrows // 128
                ncb = wcols // 128
                se_b = b.val("se")
                te_b = b.val("te")
                wv = g[wn].rearrange("(j p) c -> p j c", p=128)
                loadv = []
                for cb in range(ncb):
                    def gl(ge, cb=cb, nj=nj, wv=wv, se_b=se_b):
                        if cb >= 2:
                            ge.wait_ge(b.sem("se"), se_b + cb - 1)
                        v = b.dma(ge, WFA[cb % 2][:, 0:nj * 128].rearrange("p (j c) -> p j c", c=128),
                                  wv[:, :, cb * 128:cb * 128 + 128], "wld")
                        loadv.append(v)
                    b.ge(gl)

                    def sab(sc, cb=cb, nj=nj, te_b=te_b):
                        sc.wait_ge(b.sem("wld"), loadv[cb])
                        if cb >= 1:
                            sc.wait_ge(b.sem("te"), te_b + cb)
                        i = sc.activation(out=ABSB[:, 0:nj * 128], in_=WFA[cb % 2][:, 0:nj * 128],
                                          func=AF.Abs)
                        b.mark(i, "se")
                    b.se(sab)
                    vse = b.val("se")

                    def tmm(t, cb=cb, nj=nj, vse=vse, ncb=ncb):
                        t.wait_ge(b.sem("se"), vse)
                        for j in range(nj):
                            i = t.matmul(PS_B[:, 0:128], ABSB[:, j * 128:j * 128 + 128],
                                         ONES[:, 0:128], start=(cb == 0 and j == 0),
                                         stop=(cb == ncb - 1 and j == nj - 1))
                        b.mark(i, "te")
                    b.te(tmm)
                vte = b.val("te")

                def vev(v, vte=vte):
                    v.wait_ge(b.sem("te"), vte)
                    i = v.tensor_copy(out=EVB, in_=PS_B[:, 0:128])
                    b.mark(i, "ve")
                b.ve(vev)
                vv = b.val("ve")

                def tm2(t, vv=vv):
                    t.wait_ge(b.sem("ve"), vv)
                    i = t.matmul(PS_C[:, 0:128], ONES[:, 0:128], EVB, start=True, stop=True)
                    b.mark(i, "te")
                b.te(tm2)
                vte2 = b.val("te")

                def vg(v, wi=wi, vte2=vte2, numel=wrows * wcols):
                    v.wait_ge(b.sem("te"), vte2)
                    v.tensor_scalar(out=GSC[wi], in0=PS_C[:, 0:128], scalar1=1.0 / numel,
                                    scalar2=1e-8, op0=ALU.mult, op1=ALU.add)
                    i = v.reciprocal(out=GSC[wi], in_=GSC[wi])
                    b.mark(i, "ve")
                    dtap(v, 128 * wi, GSC[wi])
                b.ve(vg)
            b.barrier()

            def xln3_t(tj, half):
                o = tj * TC + half * 512
                return XLN[0:128, o:o + 512]
            for half in range(2):
                def vcast3(v, half=half):
                    for tj in range(NT):
                        i = v.tensor_copy(out=xln3_t(tj, half),
                                          in_=X[:, tj * TC + half * 512:tj * TC + half * 512 + 512])
                    b.mark(i, "ve")
                b.ve(vcast3)
                mrep, rrep, vrr = ln_stats(lambda tj, h=half: xln3_t(tj, h), half)

                def vnorm3(v, half=half, mrep=mrep, rrep=rrep, vrr=vrr):
                    v.wait_ge(b.sem("ve"), vrr)
                    tmp = TMPS[:, half * 512:half * 512 + 512]
                    for tj in range(NT):
                        xsl = X[:, tj * TC + half * 512:tj * TC + half * 512 + 512]
                        v.tensor_tensor(out=tmp, in0=xsl, in1=mrep, op=ALU.subtract)
                        i = v.tensor_tensor(out=xln3_t(tj, half), in0=tmp, in1=rrep, op=ALU.mult)
                    b.mark(i, "ve")
                b.ve(vnorm3)
            b.barrier()

            WF1 = [arv(33, 2048, F32), arv(41, 2048, F32)]
            WF2 = [arv(49, 2048, F32), arv(61, 2048, F32)]
            WQT = arv(69, 2048)
            WQT2 = arv(73, 2048)
            SG = arv(77, 512)
            HIDC = arv(78, 11 * 1024)
            w1v = g["w1t"].rearrange("(j p) c -> p j c", p=128)
            w2v = g["w2t"].rearrange("(j p) c -> p j c", p=128)
            w3v = g["w3t"].rearrange("(j p) c -> p j c", p=128)
            VQ2M = []
            for ch in range(4):
                for oi in range(11):
                    ob = ch * 11 + oi
                    def gl2(ge, ob=ob, oi=oi, n=len(VQ2M)):
                        if oi >= 2:
                            ge.wait_ge(b.sem("ve"), VQ2M[n - 2])
                        elif ob >= 2:
                            ge.wait_ge(b.sem("ve"), b.val("ve"))
                        b.dma(ge, WF1[ob % 2][:].rearrange("p (j c) -> p j c", c=128),
                              w1v[:, :, ob * 128:ob * 128 + 128], "wld")
                        b.dma(ge, WF2[ob % 2][:].rearrange("p (j c) -> p j c", c=128),
                              w2v[:, :, ob * 128:ob * 128 + 128], "wld")
                    b.ge(gl2)
                    vl2 = b.val("wld")

                    def vq(v, ob=ob, vl2=vl2):
                        v.wait_ge(b.sem("wld"), vl2)
                        if ob >= 1:
                            v.wait_ge(b.sem("te"), b.val("te"))
                        for wf, gsc, wqt in ((WF1[ob % 2], GSC[0], WQT), (WF2[ob % 2], GSC[1], WQT2)):
                            gb = gsc[:, 0:1].broadcast_to((128, 2048))
                            v.tensor_tensor(out=wf[:], in0=wf[:], in1=gb, op=ALU.mult)
                            v.tensor_scalar(out=wf[:], in0=wf[:], scalar1=MAGIC, scalar2=MAGIC,
                                            op0=ALU.add, op1=ALU.subtract)
                            i = v.tensor_scalar(out=wqt[:], in0=wf[:], scalar1=-1.0, scalar2=1.0,
                                                op0=ALU.max, op1=ALU.min)
                        VQ2M.append(b.mark(i, "ve"))
                    b.ve(vq)
                    vqm = b.val("ve")

                    def tffn(t, ob=ob, vqm=vqm):
                        t.wait_ge(b.sem("ve"), vqm)
                        t.wait_ge(b.sem("se"), b.val("se"))
                        for half in range(2):
                            for i_ in range(NT):
                                t.matmul(PS_A[:, half * 512:half * 512 + 512],
                                         WQT[:, i_ * 128:i_ * 128 + 128],
                                         xln3_t(i_, half), start=(i_ == 0), stop=(i_ == NT - 1))
                            for i_ in range(NT):
                                i = t.matmul(PS_A[:, 1024 + half * 512:1024 + half * 512 + 512],
                                             WQT2[:, i_ * 128:i_ * 128 + 128],
                                             xln3_t(i_, half), start=(i_ == 0), stop=(i_ == NT - 1))
                        b.mark(i, "te")
                    b.te(tffn)
                    vtf = b.val("te")

                    for half in range(2):
                        def sgev(sc, vtf=vtf, half=half):
                            if half == 0:
                                sc.wait_ge(b.sem("te"), vtf)
                            if half == 1:
                                sc.wait_ge(b.sem("ve"), b.val("ve"))
                            i = sc.activation(out=SG[:, 0:512], in_=PS_A[:, half * 512:half * 512 + 512],
                                              func=AF.Sigmoid)
                            b.mark(i, "se")
                        b.se(sgev)
                        vsg = b.val("se")

                        def vhid(v, oi=oi, half=half, vsg=vsg):
                            v.wait_ge(b.sem("se"), vsg)
                            i2 = v.tensor_tensor(out=HIDC[:, oi * 1024 + half * 512:oi * 1024 + half * 512 + 512],
                                                 in0=PS_A[:, 1024 + half * 512:1024 + half * 512 + 512],
                                                 in1=SG[:, 0:512], op=ALU.mult)
                            b.mark(i2, "ve")
                        b.ve(vhid)
                WF3 = [arv(33, 1408, F32), arv(39, 1408, F32)]
                WQT3 = arv(45, 1408)
                ESB = arv(49, 1024, F32)
                ESB3 = [arv(69, 1024, F32), arv(73, 1024, F32)]
                VQ3M = []
                ACV3 = []
                for dt in range(16):
                    def gl3(ge, ch=ch, dt=dt, n=len(VQ3M)):
                        if dt >= 2:
                            ge.wait_ge(b.sem("ve"), VQ3M[n - 2])
                        else:
                            ge.wait_ge(b.sem("ve"), b.val("ve"))
                        b.dma(ge, WF3[dt % 2][:].rearrange("p (j c) -> p j c", c=128),
                              w3v[:, 11 * ch:11 * ch + 11, dt * 128:dt * 128 + 128], "wld")
                    b.ge(gl3)
                    vl3 = b.val("wld")

                    def vq3(v, dt=dt, vl3=vl3):
                        v.wait_ge(b.sem("wld"), vl3)
                        if dt >= 1:
                            v.wait_ge(b.sem("te"), b.val("te"))
                        wf = WF3[dt % 2]
                        gb = GSC[2][:, 0:1].broadcast_to((128, 1408))
                        v.tensor_tensor(out=wf[:], in0=wf[:], in1=gb, op=ALU.mult)
                        v.tensor_scalar(out=wf[:], in0=wf[:], scalar1=MAGIC, scalar2=MAGIC,
                                        op0=ALU.add, op1=ALU.subtract)
                        i = v.tensor_scalar(out=WQT3[:], in0=wf[:], scalar1=-1.0, scalar2=1.0,
                                            op0=ALU.max, op1=ALU.min)
                        VQ3M.append(b.mark(i, "ve"))
                    b.ve(vq3)
                    vq3m = b.val("ve")

                    def tdn(t, dt=dt, vq3m=vq3m):
                        t.wait_ge(b.sem("ve"), vq3m)
                        for half in range(2):
                            for ii in range(11):
                                i = t.matmul(PS_B[:, half * 512:half * 512 + 512],
                                             WQT3[:, ii * 128:ii * 128 + 128],
                                             HIDC[:, ii * 1024 + half * 512:ii * 1024 + half * 512 + 512],
                                             start=(ii == 0), stop=(ii == 10))
                        b.mark(i, "te")
                    b.te(tdn)
                    vtd = b.val("te")

                    def vev3(v, dt=dt, vtd=vtd, n=len(ACV3)):
                        v.wait_ge(b.sem("te"), vtd)
                        if n >= 2:
                            v.wait_ge(b.sem("ac"), ACV3[n - 2])
                        i = v.tensor_copy(out=ESB3[dt % 2], in_=PS_B[:, 0:1024])
                        b.mark(i, "ve")
                    b.ve(vev3)
                    vem = b.val("ve")

                    def gac(ge, ch=ch, dt=dt, vem=vem):
                        ge.wait_ge(b.sem("ve"), vem)
                        if ch == 0:
                            ACV3.append(b.dma(ge, g["ddacc"][dt * 128:dt * 128 + 128, :],
                                              ESB3[dt % 2], "ac"))
                        else:
                            h2, v2 = b.bump("ac", 16)
                            ge.dma_start(out=g["ddacc"][dt * 128:dt * 128 + 128, :],
                                         in_=ESB3[dt % 2],
                                         accum_op=ALU.add).then_inc(h2, 16)
                            ACV3.append(v2)
                    b.ge(gac)

                def gacw0(ge):
                    ge.wait_ge(b.sem("ac"), b.val("ac"))
                b.ge(gacw0)
                if ch == 0:
                    def vtapH(v):
                        dtap(v, 384, HIDC[:, 0:512])
                    b.ve(vtapH)
                b.barrier(("vector", "tensor", "scalar", "gpsimd"))

            h_ac, v_ac = b.bump("ge")

            def gacw(ge):
                ge.wait_ge(b.sem("ac"), b.val("ac"))
                YB3 = arv(0, 16 * TC)
                b.dma(ge, YB3.rearrange("p (t c) -> p t c", t=NT),
                      g["ddacc"].rearrange("(t p) c -> p t c", p=128), "wld")
                ge.wait_ge(b.sem("wld"), b.val("wld"))
                ge.nop().then_inc(h_ac, 1)
            b.ge(gacw)
            YB3 = arv(0, 16 * TC)

            def yb3_t(tj, half):
                o = tj * TC + half * 512
                return YB3[0:128, o:o + 512]

            for en in ("vector", "scalar", "tensor"):
                def wv3(eng, en=en):
                    eng.wait_ge(h_ac, v_ac)
                getattr(b.block, en)(wv3)

            def vtapY(v):
                dtap(v, 896, YB3[0:128, 0:512])
            b.ve(vtapY)

            for half in range(2):
                mrep, rrep, vrr = ln_stats(lambda tj, h=half: yb3_t(tj, h), half)

                def vres3(v, half=half, mrep=mrep, rrep=rrep, vrr=vrr):
                    v.wait_ge(b.sem("ve"), vrr)
                    tmp = TMPS[:, half * 512:half * 512 + 512]
                    for tj in range(NT):
                        xsl = X[:, tj * TC + half * 512:tj * TC + half * 512 + 512]
                        v.tensor_tensor(out=tmp, in0=yb3_t(tj, half), in1=mrep, op=ALU.subtract)
                        v.tensor_tensor(out=tmp, in0=tmp, in1=rrep, op=ALU.mult)
                        i = v.tensor_tensor(out=xsl, in0=xsl, in1=tmp, op=ALU.add)
                    b.mark(i, "ve")
                b.ve(vres3)
            b.barrier()

        if PHASES >= 4:
            WFA4 = [arv(0, 2048, F32), arv(8, 2048, F32)]
            ABSB4 = arv(16, 2048)
            EVB4 = arv(20, 128)
            EVL4 = arv(20, 256)[:, 128:256]
            ACC4 = HBQN[:, 0:128]
            ACC4B = HBQN[:, 128:256]
            GSW = [HBQN[:, 1024 + i4 * 128:1024 + i4 * 128 + 128] for i4 in range(4)]
            GIV = [HBQN[:, 1536 + i4 * 128:1536 + i4 * 128 + 128] for i4 in range(4)]
            for wi, wn in enumerate(["wqt", "wkt", "wvt", "wot"]):
                se_b = b.val("se")
                te_b = b.val("te")
                wv4 = g[wn].rearrange("(j p) c -> p j c", p=128)
                loadv4 = []
                for cb in range(16):
                    def gl4(ge, cb=cb, wv4=wv4, se_b=se_b):
                        if cb >= 2:
                            ge.wait_ge(b.sem("se"), se_b + cb - 1)
                        else:
                            ge.wait_ge(b.sem("se"), se_b)
                        v = b.dma(ge, WFA4[cb % 2][:].rearrange("p (j c) -> p j c", c=128),
                                  wv4[:, :, cb * 128:cb * 128 + 128], "wld")
                        loadv4.append(v)
                    b.ge(gl4)

                    def sab4(sc, cb=cb, te_b=te_b):
                        sc.wait_ge(b.sem("wld"), loadv4[cb])
                        if cb >= 1:
                            sc.wait_ge(b.sem("te"), te_b + cb)
                        else:
                            sc.wait_ge(b.sem("te"), te_b)
                        i = sc.activation(out=ABSB4[:], in_=WFA4[cb % 2][:], func=AF.Abs)
                        b.mark(i, "se")
                    b.se(sab4)
                    vse4 = b.val("se")

                    def tmm4(t, cb=cb, vse4=vse4):
                        t.wait_ge(b.sem("se"), vse4)
                        t.wait_ge(b.sem("ve"), b.val("ve"))
                        for j in range(16):
                            i = t.matmul(PS_B[:, 0:128], ABSB4[:, j * 128:j * 128 + 128],
                                         ONES[:, 0:128], start=(j == 0), stop=(j == 15))
                        b.mark(i, "te")
                    b.te(tmm4)
                    vtc4 = b.val("te")

                    def vac4(v, cb=cb, vtc4=vtc4):
                        v.wait_ge(b.sem("te"), vtc4)
                        if cb == 0:
                            i = v.tensor_copy(out=ACC4, in_=PS_B[:, 0:128])
                        else:
                            i = v.tensor_tensor(out=ACC4, in0=ACC4,
                                                in1=PS_B[:, 0:128], op=ALU.add)
                        b.mark(i, "ve")
                    b.ve(vac4)

                def vhl4(v):
                    v.tensor_copy(out=EVB4, in_=ACC4)
                    v.tensor_copy(out=ACC4B, in_=EVB4)
                    v.tensor_tensor(out=ACC4B, in0=ACC4, in1=ACC4B, op=ALU.subtract)
                    i = v.tensor_copy(out=EVL4, in_=ACC4B)
                    b.mark(i, "ve")
                b.ve(vhl4)
                vv4 = b.val("ve")

                def tm24(t, vv4=vv4):
                    t.wait_ge(b.sem("ve"), vv4)
                    t.matmul(PS_C[:, 0:128], ONES[:, 0:128], EVB4, start=True, stop=False)
                    i = t.matmul(PS_C[:, 0:128], ONES[:, 0:128], EVL4, start=False, stop=True)
                    b.mark(i, "te")
                b.te(tm24)
                vte24 = b.val("te")

                def vg4(v, wi=wi, vte24=vte24):
                    v.wait_ge(b.sem("te"), vte24)
                    v.tensor_scalar(out=GIV[wi], in0=PS_C[:, 0:128], scalar1=1.0 / (D * D),
                                    scalar2=1e-5, op0=ALU.mult, op1=ALU.max)
                    i = v.reciprocal(out=GSW[wi], in_=GIV[wi])
                    b.mark(i, "ve")
                b.ve(vg4)
                b.barrier(("vector", "scalar", "tensor", "gpsimd"))
            b.barrier()

            def xcast5_t(tj, half):
                o = tj * TC + half * 512
                return XLN[0:128, o:o + 512]
            ln5 = []
            for half in range(2):
                def vcast5(v, half=half):
                    for tj in range(NT):
                        i = v.tensor_copy(out=xcast5_t(tj, half),
                                          in_=X[:, tj * TC + half * 512:tj * TC + half * 512 + 512])
                    b.mark(i, "ve")
                b.ve(vcast5)
                mrep, rrep, vrr = ln_stats(lambda tj, h=half: xcast5_t(tj, h), half)
                ln5.append((mrep, rrep, vrr))
                AMX = arv(48, 512, F32)
                TT5 = arv(50, 512, F32)
                HI5 = arv(54, 512)
                LO5 = arv(55, 512)
                TH5 = arv(56, 128)
                TL5 = arv(57, 128)
                FR5 = arv(58, 128, F32)

                AMN = arv(52, 512, F32)

                def vamax(v, half=half, mrep=mrep, rrep=rrep, vrr=vrr):
                    v.wait_ge(b.sem("ve"), vrr)
                    for tj in range(NT):
                        xsl = X[:, tj * TC + half * 512:tj * TC + half * 512 + 512]
                        v.tensor_tensor(out=TT5, in0=xsl, in1=mrep, op=ALU.subtract)
                        if tj == 0:
                            v.tensor_copy(out=AMX, in_=TT5)
                            v.tensor_copy(out=AMN, in_=TT5)
                        else:
                            v.tensor_tensor(out=AMX, in0=AMX, in1=TT5, op=ALU.max)
                            v.tensor_tensor(out=AMN, in0=AMN, in1=TT5, op=ALU.min)
                    v.tensor_scalar(out=AMN, in0=AMN, scalar1=-1.0, scalar2=None, op0=ALU.mult)
                    v.tensor_tensor(out=AMX, in0=AMX, in1=AMN, op=ALU.max)
                    v.tensor_tensor(out=AMX, in0=AMX, in1=rrep, op=ALU.mult)
                    for stp in (64, 32):
                        v.tensor_copy(out=TT5[0:stp, :], in_=AMX[stp:2 * stp, :])
                        v.tensor_tensor(out=AMX[0:stp, :], in0=AMX[0:stp, :],
                                        in1=TT5[0:stp, :], op=ALU.max)
                    v.tensor_copy(out=HI5[0:32, :], in_=AMX[0:32, :])
                    v.tensor_copy(out=TT5[0:32, :], in_=HI5[0:32, :])
                    v.tensor_tensor(out=TT5[0:32, :], in0=AMX[0:32, :],
                                    in1=TT5[0:32, :], op=ALU.subtract)
                    i = v.tensor_copy(out=LO5[0:32, :], in_=TT5[0:32, :])
                    b.mark(i, "ve")
                b.ve(vamax)
                vam = b.val("ve")

                def sytr5(sy, vam=vam):
                    sy.wait_ge(b.sem("ve"), vam)
                    for ci in range(4):
                        b.dmat(sy, TH5[:, ci * 32:ci * 32 + 32],
                               HI5[0:32, ci * 128:ci * 128 + 128], "tp")
                        b.dmat(sy, TL5[:, ci * 32:ci * 32 + 32],
                               LO5[0:32, ci * 128:ci * 128 + 128], "tp")
                    sy.wait_ge(b.sem("tp"), b.val("tp"))
                    sy.nop().then_inc(b.sem("sy"), 1)
                b.sy(sytr5)
                b.s["sy"][1] += 1
                vtr5 = b.val("sy")

                def vfr5(v, vtr5=vtr5, half=half):
                    v.wait_ge(b.sem("sy"), vtr5)
                    v.tensor_tensor(out=FR5, in0=TH5[:, 0:128], in1=TL5[:, 0:128], op=ALU.add)
                    v.tensor_scalar(out=FR5, in0=FR5, scalar1=1e-5,
                                    scalar2=None, op0=ALU.max)
                    i = v.reduce_max(out=TT5[:, 0:4],
                                     in_=FR5.rearrange("p (ci c) -> p ci c", ci=4),
                                     axis=mybir.AxisListType.X)
                    if half == 0:
                        dtap(v, 2944, FR5)
                    b.mark(i, "ve")
                b.ve(vfr5)
                vfr5m = b.val("ve")

                def sysx(sy, half=half, vfr5m=vfr5m):
                    sy.wait_ge(b.sem("ve"), vfr5m)
                    vs = None
                    for ci in range(4):
                        cX = half * 512 + ci * 128
                        vs = b.dma(sy, g["sxrow"][0:1, cX:cX + 128]
                                   .rearrange("a (bb c) -> (a bb) c", c=1),
                                   TT5[:, ci:ci + 1], "st")
                    sy.wait_ge(b.sem("st"), vs)
                b.sy(sysx)
            b.barrier(("vector", "sync", "scalar"))

            SXB = arv(44, TC, F32)
            SXIW = [arv(32, TC, F32), arv(36, TC, F32), arv(40, TC, F32)]

            def ldsx(sy):
                v = b.dma(sy, SXB, g["sxrow"][0:1, :].partition_broadcast(128), "ld")
                sy.wait_ge(b.sem("ld"), v)
                sy.nop().then_inc(b.sem("sy"), 1)
            b.sy(ldsx)
            b.s["sy"][1] += 1
            vsx = b.val("sy")

            def vsxc(v):
                v.wait_ge(b.sem("sy"), vsx)
                v.tensor_scalar(out=SXIW[0], in0=SXB, scalar1=1.0 / 127.0,
                                scalar2=None, op0=ALU.mult)
                v.reciprocal(out=SXB, in_=SXIW[0])
                i = v.tensor_copy(out=AMX, in_=AMX)
                dtap(v, 2816, SXB[:, 0:128])
                b.mark(i, "ve")
            b.ve(vsxc)

            for half in range(2):
                mrep, rrep, vrr = ln5[half]

                def vxint(v, half=half, mrep=mrep, rrep=rrep):
                    TT5b = arv(59, 512, F32)
                    v.tensor_tensor(out=TT5b, in0=rrep, in1=SXB[:, half * 512:half * 512 + 512], op=ALU.mult)
                    for tj in range(NT):
                        xsl = X[:, tj * TC + half * 512:tj * TC + half * 512 + 512]
                        TT5c = arv(61, 512, F32)
                        v.tensor_tensor(out=TT5c, in0=xsl, in1=mrep, op=ALU.subtract)
                        v.tensor_tensor(out=TT5c, in0=TT5c, in1=TT5b, op=ALU.mult)
                        v.tensor_scalar(out=TT5c, in0=TT5c, scalar1=MAGIC, scalar2=MAGIC,
                                        op0=ALU.add, op1=ALU.subtract)
                        i = v.tensor_scalar(out=xcast5_t(tj, half), in0=TT5c, scalar1=-128.0,
                                            scalar2=127.0, op0=ALU.max, op1=ALU.min)
                    if half == 0:
                        dtap(v, 0, xcast5_t(0, 0))
                    b.mark(i, "ve")
                b.ve(vxint)
            b.barrier()

            def vsxiw(v):
                gb1 = GIV[1][:, 0:1].broadcast_to((128, TC))
                gb2 = GIV[2][:, 0:1].broadcast_to((128, TC))
                gb0 = GIV[0][:, 0:1].broadcast_to((128, TC))
                v.tensor_tensor(out=SXIW[1], in0=SXIW[0], in1=gb1, op=ALU.mult)
                v.tensor_tensor(out=SXIW[2], in0=SXIW[0], in1=gb2, op=ALU.mult)
                i = v.tensor_tensor(out=SXIW[0], in0=SXIW[0], in1=gb0, op=ALU.mult)
                b.mark(i, "ve")
            b.ve(vsxiw)
            b.barrier(("vector", "sync", "gpsimd", "tensor", "scalar"))

            WF4 = [arv(45, 2048, F32), arv(53, 2048, F32)]
            WQ4 = arv(61, 2048)
            QKV = arv(65, 1024)
            KSQ = arv(67, 1024)
            KN2 = arv(69, 128, F32)
            STG4 = STG
            wviews = [g["wqt"].rearrange("(j p) c -> p j c", p=128),
                      g["wkt"].rearrange("(j p) c -> p j c", p=128),
                      g["wvt"].rearrange("(j p) c -> p j c", p=128)]
            VQ5M = []
            for wi in range(3):
                for ob in range(16):
                    def gl5(ge, wi=wi, ob=ob, n=len(VQ5M)):
                        if n >= 2:
                            ge.wait_ge(b.sem("ve"), VQ5M[n - 2])
                        else:
                            ge.wait_ge(b.sem("ve"), b.val("ve"))
                        b.dma(ge, WF4[ob % 2][:].rearrange("p (j c) -> p j c", c=128),
                              wviews[wi][:, :, ob * 128:ob * 128 + 128], "wld")
                    b.ge(gl5)
                    vl5 = b.val("wld")

                    def vq5(v, wi=wi, ob=ob, vl5=vl5):
                        v.wait_ge(b.sem("wld"), vl5)
                        if not (wi == 0 and ob == 0):
                            v.wait_ge(b.sem("te"), b.val("te"))
                        wf = WF4[ob % 2]
                        gb = GSW[wi][:, 0:1].broadcast_to((128, 2048))
                        v.tensor_tensor(out=wf[:], in0=wf[:], in1=gb, op=ALU.mult)
                        v.tensor_scalar(out=wf[:], in0=wf[:], scalar1=MAGIC, scalar2=MAGIC,
                                        op0=ALU.add, op1=ALU.subtract)
                        i = v.tensor_scalar(out=WQ4[:], in0=wf[:], scalar1=-1.0, scalar2=1.0,
                                            op0=ALU.max, op1=ALU.min)
                        VQ5M.append(b.mark(i, "ve"))
                    b.ve(vq5)
                    vqm5 = b.val("ve")

                    def tqkv(t, wi=wi, ob=ob, vqm5=vqm5):
                        t.wait_ge(b.sem("ve"), vqm5)
                        t.wait_ge(b.sem("se"), b.val("se"))
                        for half in range(2):
                            for i_ in range(NT):
                                i = t.matmul(PS_B[:, half * 512:half * 512 + 512],
                                             WQ4[:, i_ * 128:i_ * 128 + 128],
                                             xcast5_t(i_, half), start=(i_ == 0), stop=(i_ == NT - 1))
                        b.mark(i, "te")
                    b.te(tqkv)
                    vtq = b.val("te")

                    def vevq(v, wi=wi, ob=ob, vtq=vtq):
                        v.wait_ge(b.sem("te"), vtq)
                        v.wait_ge(b.sem("st"), b.val("st"))
                        v.wait_ge(b.sem("tp"), b.val("tp"))
                        i = v.tensor_tensor(out=QKV[:, 0:1024],
                                            in0=PS_B[:, 0:1024], in1=SXIW[wi], op=ALU.mult)
                        if wi == 0 and ob == 0:
                            dtap(v, 512, QKV[:, 0:512])
                        b.mark(i, "ve")
                    b.ve(vevq)
                    vem5 = b.val("ve")

                    if wi == 0:
                        def syq(sy, ob=ob, vem5=vem5):
                            sy.wait_ge(b.sem("ve"), vem5)
                            b.dma(sy, g["qd"][ob * 128:ob * 128 + 128, :], QKV[:, 0:1024], "st")
                        b.sy(syq)
                    elif wi == 1:
                        def syk(sy, ob=ob, vem5=vem5):
                            sy.wait_ge(b.sem("ve"), vem5)
                            b.dma(sy, g["gki"][ob * 128:ob * 128 + 128, :], QKV[:, 0:1024], "st")
                        b.sy(syk)
                    else:
                        def syv(sy, ob=ob, vem5=vem5):
                            sy.wait_ge(b.sem("ve"), vem5)
                            for tb in range(8):
                                sy.wait_ge(b.sem("st"), b.val("st"))
                                vtt = b.dmat(sy, STG4[:, (tb % 2) * 128:(tb % 2) * 128 + 128],
                                             QKV[:, tb * 128:tb * 128 + 128], "tp")
                                sy.wait_ge(b.sem("tp"), vtt)
                                b.dma(sy, g["gvi"][tb * 128:tb * 128 + 128,
                                                   ob * 128:ob * 128 + 128],
                                      STG4[:, (tb % 2) * 128:(tb % 2) * 128 + 128], "st")
                        b.sy(syv)
                    if wi == 1:
                        def skn(sc, ob=ob, vem5=vem5):
                            sc.wait_ge(b.sem("ve"), vem5)
                            i = sc.activation(out=KSQ[:, 0:1024], in_=QKV[:, 0:1024], func=AF.Square)
                            b.mark(i, "se")
                        b.se(skn)
                        vkn = b.val("se")

                        def tkn(t, vkn=vkn, ob=ob):
                            t.wait_ge(b.sem("se"), vkn)
                            t.matmul(PS_C[:, 512:1024], ONES[:], KSQ[:, 0:512], start=True, stop=True)
                            i = t.matmul(PS_C[:, 512:1024], ONES[:], KSQ[:, 512:1024],
                                         start=False, stop=True)
                            b.mark(i, "te")
                        b.te(tkn)
                        vkt = b.val("te")

                        def vkn2(v, ob=ob, vkt=vkt):
                            v.wait_ge(b.sem("te"), vkt)
                            KW = arv(70, 512, F32)
                            v.tensor_copy(out=KW[:, 0:512], in_=PS_C[:, 512:1024])
                            for w2 in (256, 128, 64, 32, 16, 8):
                                v.tensor_tensor(out=KW[:, 0:w2], in0=KW[:, 0:w2],
                                                in1=KW[:, w2:2 * w2], op=ALU.max)
                            i = v.tensor_copy(out=KN2[:, ob * 8:ob * 8 + 8], in_=KW[:, 0:8])
                            b.mark(i, "ve")
                        b.ve(vkn2)
            vknd = b.val("ve")

            def sykn(sy, vknd=vknd):
                sy.wait_ge(b.sem("ve"), vknd)
                vs = b.dma(sy, g["g5i"], KN2[0:1, :], "st")
                sy.wait_ge(b.sem("st"), vs)
                sy.nop().then_inc(b.sem("sy"), 1)
            b.sy(sykn)
            b.s["sy"][1] += 1
            vstg = b.val("sy")

            def gcc4(ge, vstg=vstg):
                ge.wait_ge(b.sem("sy"), vstg)
                for nmi, nmo in (("gki", "gko"), ("gvi", "gvo"), ("g5i", "g5o")):
                    i = ge.collective_compute(
                        "AllGather", ALU.bypass, replica_groups=[list(range(NCORES))],
                        ins=[g[nmi].opt()], outs=[g[nmo].opt()])
                    h2, _ = b.bump("cc", 1)
                    i.then_inc(h2, 1)
            b.ge(gcc4)
            vcc4 = b.val("cc")

            KNR = HENDW[:, 656:784]
            KGB = HENDW[:, 640:656]

            def ldknr(sy, vcc4=vcc4):
                sy.wait_ge(b.sem("cc"), vcc4)
                v = b.dma(sy, KNR[0:8, 0:128], g["g5o"], "ld")
                sy.wait_ge(b.sem("ld"), v)
                sy.nop().then_inc(b.sem("sy"), 1)
            b.sy(ldknr)
            b.s["sy"][1] += 1
            vknl = b.val("sy")

            KNRED = HENDW[:, 576:577]

            def vknf(v, vknl=vknl):
                v.wait_ge(b.sem("sy"), vknl)
                i = v.reduce_max(out=KNRED[0:8, 0:1], in_=KNR[0:8, 0:128],
                                 axis=mybir.AxisListType.X)
                b.mark(i, "ve")
            b.ve(vknf)
            vknm = b.val("ve")

            def syknb(sy, vknm=vknm):
                sy.wait_ge(b.sem("ve"), vknm)
                vs = b.dma(sy, g["mkb"][0:1, 0:8].rearrange("a (bb c) -> (a bb) c", c=1),
                           KNRED[0:8, 0:1], "st")
                sy.wait_ge(b.sem("st"), vs)
                v2 = b.dma(sy, KGB[:, 0:8],
                           g["mkb"][0:1, 0:8].partition_broadcast(128), "ld")
                sy.wait_ge(b.sem("ld"), v2)
                sy.nop().then_inc(b.sem("sy"), 1)
            b.sy(syknb)
            b.s["sy"][1] += 1
            b.barrier()
            if DEBUG:
                def vtapG(v):
                    dtap(v, 2688, KGB)
                    for i4 in range(4):
                        v.tensor_copy(out=DBGT[:, 2704 + i4:2705 + i4], in_=GIV[i4][:, 0:1])
                        v.tensor_copy(out=HENDW[:, 2640:3152], in_=HENDW[:, 2640:3152])
                        v.tensor_copy(out=DBGT[:, 2708 + i4:2709 + i4], in_=GSW[i4][:, 0:1])
                        v.tensor_copy(out=HENDW[:, 2640:3152], in_=HENDW[:, 2640:3152])
                b.ve(vtapG)

            QT = arv(0, 16 * 512)
            KH = [arv(16, 512), arv(17, 512)]
            VTA = [arv(18, 4 * 2048), arv(34, 4 * 2048)]
            PBUF = arv(50, 2048)
            MSKB = arv(54, 2048)
            OAC = arv(68, 16 * 512, F32)
            CM = HENDW[:, 784:1296]
            CMX = HENDW[:, 1296:1808]
            QSQ2 = HENDW[:, 1808:2320].bitcast(BF16)
            RSH = HENDW[:, 1808:2320]
            CNB16 = HENDW[:, 2320:2336]
            CC = HENDW[:, 2336:2340]
            CT1 = HENDW[:, 2640:3152]
            AMO = HENDW[:, 3152:3664]
            SOB = HENDW[:, 784:1296]
            SOIW = HENDW[:, 1296:1808]
            RSI = HENDW[:, 0:512]
            TSCM = []
            AVM = []

            for qb in range(2):
                NKB = 8 * (qb + 1)

                def ldq4(sy, qb=qb):
                    v = b.dma(sy, QT.rearrange("p (t c) -> p t c", t=16),
                              g["qd"][:, qb * 512:qb * 512 + 512]
                              .rearrange("(t p) c -> p t c", p=128), "ld")
                    sy.wait_ge(b.sem("ld"), v)
                    sy.nop().then_inc(b.sem("sy"), 1)
                b.sy(ldq4)
                b.s["sy"][1] += 1
                vq4 = b.val("sy")

                te_q = b.val("te")
                for h in range(H):
                    def sqsq(sc, h=h, vq4=vq4, te_q=te_q):
                        if h == 0:
                            sc.wait_ge(b.sem("sy"), vq4)
                        if h >= 2:
                            sc.wait_ge(b.sem("te"), te_q + h - 1)
                        i = sc.activation(out=QSQ2[:, (h % 2) * 512:(h % 2) * 512 + 512],
                                          in_=QT[:, h * 512:h * 512 + 512], func=AF.Square)
                        b.mark(i, "se")
                    b.se(sqsq)
                    vsq = b.val("se")

                    def tqn(t, h=h, vsq=vsq):
                        t.wait_ge(b.sem("se"), vsq)
                        i = t.matmul(PS_A[0:16, 0:512], EH16[:, h * 16:h * 16 + 16],
                                     QSQ2[:, (h % 2) * 512:(h % 2) * 512 + 512],
                                     start=(h == 0), stop=(h == H - 1))
                        b.mark(i, "te")
                    b.te(tqn)
                vtqn = b.val("te")

                def vcm(v, vtqn=vtqn):
                    v.wait_ge(b.sem("te"), vtqn)
                    v.tensor_copy(out=CM[0:16, :], in_=PS_A[0:16, 0:512])
                    for s in range(8):
                        kb_ = KGB[0:16, s:s + 1].broadcast_to((16, 512))
                        if s == 0:
                            v.tensor_tensor(out=CMX[0:16, :], in0=CM[0:16, :],
                                            in1=kb_, op=ALU.mult)
                        else:
                            v.tensor_tensor(out=CT1[0:16, :], in0=CM[0:16, :],
                                            in1=kb_, op=ALU.mult)
                            v.tensor_tensor(out=CMX[0:16, :], in0=CMX[0:16, :],
                                            in1=CT1[0:16, :], op=ALU.max)
                    i = v.reduce_max(out=CC[0:16, 0:1], in_=CMX[0:16, :],
                                     axis=mybir.AxisListType.X)
                    b.mark(i, "ve")
                b.ve(vcm)
                vcmm = b.val("ve")

                def scm(sc, vcmm=vcmm):
                    sc.wait_ge(b.sem("ve"), vcmm)
                    i = sc.activation(out=CC[0:16, 1:2], in_=CC[0:16, 0:1], func=AF.Sqrt)
                    b.mark(i, "se")
                b.se(scm)
                vscm = b.val("se")

                def vcneg(v, vscm=vscm):
                    v.wait_ge(b.sem("se"), vscm)
                    i = v.tensor_scalar(out=CC[0:16, 2:3], in0=CC[0:16, 1:2],
                                        scalar1=float(-1.015625 / np.sqrt(128.0)),
                                        scalar2=None, op0=ALU.mult)
                    b.mark(i, "ve")
                b.ve(vcneg)
                vcnb = b.val("ve")

                def syct(sy, vcnb=vcnb):
                    sy.wait_ge(b.sem("ve"), vcnb)
                    vs = b.dma(sy, g["mkb"][0:1, 16:32].rearrange("a (bb c) -> (a bb) c", c=1),
                               CC[0:16, 2:3], "st")
                    sy.wait_ge(b.sem("st"), vs)
                    v2 = b.dma(sy, CNB16, g["mkb"][0:1, 16:32].partition_broadcast(128), "ld")
                    sy.wait_ge(b.sem("ld"), v2)
                    sy.nop().then_inc(b.sem("sy"), 1)
                b.sy(syct)
                b.s["sy"][1] += 1
                b.barrier()

                kb_av = []
                for kb in range(NKB):
                    m, r = kb // 8, kb % 8
                    partial = (m == qb)

                    def ldv(ge, m=m, r=r, kb=kb, kb_av=kb_av):
                        if kb >= 2:
                            ge.wait_ge(b.sem("ve"), kb_av[kb - 2])
                        for s2 in range(4):
                            rowb = r * 1024 + 512 * m + 128 * s2
                            b.dma(ge, VTA[kb % 2][:, s2 * 2048:s2 * 2048 + 2048],
                                  g["gvo"][rowb:rowb + 128, :], "vld")
                    b.ge(ldv)
                    vldv = b.val("vld")

                    if partial:
                        def ldmsk(ge, r=r):
                            ge.wait_ge(b.sem("te"), b.val("te"))
                            b.dma(ge, MSKB.rearrange("p (j c) -> p j c", j=4),
                                  g["masks"][r * 4:r * 4 + 4].rearrange("j p c -> p j c"),
                                  "wld")
                        b.ge(ldmsk)
                    vmk = b.val("wld")

                    for h in range(H):
                        def ldk(sy, h=h, m=m, r=r, n=len(TSCM)):
                            if n >= 2:
                                sy.wait_ge(b.sem("te"), TSCM[n - 2])
                            rb = r * 2048 + h * 128
                            b.dma(sy, KH[h % 2], g["gko"][rb:rb + 128,
                                                          m * 512:m * 512 + 512], "ld")
                        b.sy(ldk)
                        vldk = b.val("ld")

                        def tscT(t, h=h, partial=partial, vldk=vldk, vldv=vldv, vmk=vmk):
                            t.wait_ge(b.sem("ld"), vldk)
                            if h == 0:
                                t.wait_ge(b.sem("vld"), vldv)
                                if partial:
                                    t.wait_ge(b.sem("wld"), vmk)
                            t.wait_ge(b.sem("se"), b.val("se"))
                            for s2 in range(4):
                                i = t.matmul(PS_A[:, s2 * 512:s2 * 512 + 512],
                                             KH[h % 2][:, s2 * 128:s2 * 128 + 128],
                                             QT[:, h * 512:h * 512 + 512],
                                             start=True, stop=(not partial))
                                if partial:
                                    i = t.matmul(PS_A[:, s2 * 512:s2 * 512 + 512], IDENT[:],
                                                 MSKB[:, s2 * 512:s2 * 512 + 512],
                                                 start=False, stop=True)
                            TSCM.append(b.mark(i, "te"))
                        b.te(tscT)
                        vts = b.val("te")

                        def sexpT(sc, h=h, vts=vts):
                            sc.wait_ge(b.sem("te"), vts)
                            i = sc.activation(out=PBUF[:], in_=PS_A[:, 0:2048],
                                              func=AF.Exp,
                                              scale=float(1.0 / np.sqrt(128.0)),
                                              bias=CNB16[:, h:h + 1])
                            b.mark(i, "se")
                        b.se(sexpT)
                        vex = b.val("se")

                        def trsav(t, h=h, kb=kb, vex=vex, n=len(AVM)):
                            t.wait_ge(b.sem("se"), vex)
                            if n >= 2:
                                t.wait_ge(b.sem("ve"), AVM[n - 2])
                            ph2 = (h % 2) * 512
                            for s2 in range(4):
                                t.matmul(PS_C[0:16, ph2:ph2 + 512],
                                         EH16[:, h * 16:h * 16 + 16],
                                         PBUF[:, s2 * 512:s2 * 512 + 512],
                                         start=(s2 == 0), stop=(s2 == 3))
                            for s2 in range(4):
                                i = t.matmul(PS_B[:, ph2:ph2 + 512],
                                             VTA[kb % 2][:, s2 * 2048 + h * 128:s2 * 2048 + h * 128 + 128],
                                             PBUF[:, s2 * 512:s2 * 512 + 512],
                                             start=(s2 == 0), stop=(s2 == 3))
                            b.mark(i, "te")
                        b.te(trsav)
                        vta2 = b.val("te")

                        def vacT(v, h=h, kb=kb, vta2=vta2):
                            v.wait_ge(b.sem("te"), vta2)
                            ph2 = (h % 2) * 512
                            if kb == 0:
                                v.tensor_copy(out=OAC[:, h * 512:h * 512 + 512],
                                              in_=PS_B[:, ph2:ph2 + 512])
                            else:
                                v.tensor_tensor(out=OAC[:, h * 512:h * 512 + 512],
                                                in0=OAC[:, h * 512:h * 512 + 512],
                                                in1=PS_B[:, ph2:ph2 + 512], op=ALU.add)
                            if kb == 0 and h == 0:
                                i = v.tensor_copy(out=RSH[0:16, :], in_=PS_C[0:16, ph2:ph2 + 512])
                            else:
                                i = v.tensor_tensor(out=RSH[0:16, :], in0=RSH[0:16, :],
                                                    in1=PS_C[0:16, ph2:ph2 + 512], op=ALU.add)
                            AVM.append(b.mark(i, "ve"))
                        b.ve(vacT)
                    kb_av.append(b.val("ve"))

                def vrsfT(v):
                    v.reciprocal(out=CT1[0:16, :], in_=RSH[0:16, :])
                    i = v.tensor_copy(out=CM[0:16, :], in_=CM[0:16, :])
                    b.mark(i, "ve")
                b.ve(vrsfT)
                vrsm = b.val("ve")

                def syrs(sy, vrsm=vrsm):
                    sy.wait_ge(b.sem("ve"), vrsm)
                    vs = b.dma(sy, g["rsb"], CT1[0:16, :], "st")
                    sy.wait_ge(b.sem("st"), vs)
                    sy.nop().then_inc(b.sem("sy"), 1)
                b.sy(syrs)
                b.s["sy"][1] += 1
                vrsy = b.val("sy")

                for h in range(H):
                    def ldrsi(sy, h=h, vrsy=vrsy):
                        if h == 0:
                            sy.wait_ge(b.sem("sy"), vrsy)
                        sy.wait_ge(b.sem("ve"), b.val("ve"))
                        v2 = b.dma(sy, RSI, g["rsb"][h:h + 1, :].partition_broadcast(128), "ld")
                        sy.wait_ge(b.sem("ld"), v2)
                        sy.nop().then_inc(b.sem("sy"), 1)
                    b.sy(ldrsi)
                    b.s["sy"][1] += 1
                    vrl = b.val("sy")

                    def vnrm(v, h=h, vrl=vrl, qb=qb):
                        v.wait_ge(b.sem("sy"), vrl)
                        i = v.tensor_tensor(out=OAC[:, h * 512:h * 512 + 512],
                                            in0=OAC[:, h * 512:h * 512 + 512],
                                            in1=RSI, op=ALU.mult)
                        if qb == 0 and h == 0:
                            dtap(v, 2048, OAC[:, 0:512])
                        b.mark(i, "ve")
                    b.ve(vnrm)
                vnd = b.val("ve")

                HIQ = arv(16, 512)
                LOQ = arv(17, 512)
                THQ = arv(18, 128)
                TLQ = arv(19, 128)
                FRQ = arv(20, 128, F32)

                AMN4 = HENDW[:, 784:1296]

                def vamax4(v, vnd=vnd):
                    v.wait_ge(b.sem("ve"), vnd)
                    v.tensor_copy(out=AMO, in_=OAC[:, 0:512])
                    v.tensor_copy(out=AMN4, in_=OAC[:, 0:512])
                    for h in range(1, H):
                        v.tensor_tensor(out=AMO, in0=AMO,
                                        in1=OAC[:, h * 512:h * 512 + 512], op=ALU.max)
                        v.tensor_tensor(out=AMN4, in0=AMN4,
                                        in1=OAC[:, h * 512:h * 512 + 512], op=ALU.min)
                    v.tensor_scalar(out=AMN4, in0=AMN4, scalar1=-1.0, scalar2=None, op0=ALU.mult)
                    v.tensor_tensor(out=AMO, in0=AMO, in1=AMN4, op=ALU.max)
                    for stp in (64, 32):
                        v.tensor_copy(out=CT1[0:stp, :], in_=AMO[stp:2 * stp, :])
                        v.tensor_tensor(out=AMO[0:stp, :], in0=AMO[0:stp, :],
                                        in1=CT1[0:stp, :], op=ALU.max)
                    v.tensor_copy(out=HIQ[0:32, :], in_=AMO[0:32, :])
                    v.tensor_copy(out=CT1[0:32, :], in_=HIQ[0:32, :])
                    v.tensor_tensor(out=CT1[0:32, :], in0=AMO[0:32, :],
                                    in1=CT1[0:32, :], op=ALU.subtract)
                    i = v.tensor_copy(out=LOQ[0:32, :], in_=CT1[0:32, :])
                    b.mark(i, "ve")
                b.ve(vamax4)
                vam4 = b.val("ve")

                def sytr4(sy, vam4=vam4):
                    sy.wait_ge(b.sem("ve"), vam4)
                    for ci in range(4):
                        b.dmat(sy, THQ[:, ci * 32:ci * 32 + 32],
                               HIQ[0:32, ci * 128:ci * 128 + 128], "tp")
                        b.dmat(sy, TLQ[:, ci * 32:ci * 32 + 32],
                               LOQ[0:32, ci * 128:ci * 128 + 128], "tp")
                    sy.wait_ge(b.sem("tp"), b.val("tp"))
                    sy.nop().then_inc(b.sem("sy"), 1)
                b.sy(sytr4)
                b.s["sy"][1] += 1
                vtr4 = b.val("sy")

                def vfr4(v, vtr4=vtr4, qb=qb):
                    v.wait_ge(b.sem("sy"), vtr4)
                    v.tensor_tensor(out=FRQ, in0=THQ[:, 0:128], in1=TLQ[:, 0:128], op=ALU.add)
                    v.tensor_scalar(out=FRQ, in0=FRQ, scalar1=1e-5,
                                    scalar2=None, op0=ALU.max)
                    i = v.reduce_max(out=CT1[:, 0:4],
                                     in_=FRQ.rearrange("p (ci c) -> p ci c", ci=4),
                                     axis=mybir.AxisListType.X)
                    if qb == 0:
                        dtap(v, 2720, FRQ[:, 0:32])
                    b.mark(i, "ve")
                b.ve(vfr4)
                vfr4m = b.val("ve")

                def syso(sy, qb=qb, vfr4m=vfr4m):
                    sy.wait_ge(b.sem("ve"), vfr4m)
                    vs = None
                    for ci in range(4):
                        cX = qb * 512 + ci * 128
                        vs = b.dma(sy, g["sxrow"][2:3, cX:cX + 128]
                                   .rearrange("a (bb c) -> (a bb) c", c=1),
                                   CT1[:, ci:ci + 1], "st")
                    sy.wait_ge(b.sem("st"), vs)
                    v3 = b.dma(sy, SOB,
                               g["sxrow"][2:3, qb * 512:qb * 512 + 512].partition_broadcast(128), "ld")
                    sy.wait_ge(b.sem("ld"), v3)
                    sy.nop().then_inc(b.sem("sy"), 1)
                b.sy(syso)
                b.s["sy"][1] += 1
                vso = b.val("sy")

                def vsoi(v, vso=vso, qb=qb):
                    v.wait_ge(b.sem("sy"), vso)
                    v.tensor_scalar(out=SOIW, in0=SOB, scalar1=1.0 / 127.0,
                                    scalar2=None, op0=ALU.mult)
                    v.reciprocal(out=SOB, in_=SOIW)
                    gb = GIV[3][:, 0:1].broadcast_to((128, 512))
                    i = v.tensor_tensor(out=SOIW, in0=SOIW, in1=gb, op=ALU.mult)
                    if qb == 0:
                        dtap(v, 2712, SOB[:, 0:4])
                        dtap(v, 2716, SOIW[:, 0:4])
                    b.mark(i, "ve")
                b.ve(vsoi)

                for h in range(H):
                    def vxi2(v, h=h, qb=qb):
                        v.tensor_tensor(out=CT1, in0=OAC[:, h * 512:h * 512 + 512],
                                        in1=SOB, op=ALU.mult)
                        v.tensor_scalar(out=CT1, in0=CT1, scalar1=MAGIC, scalar2=MAGIC,
                                        op0=ALU.add, op1=ALU.subtract)
                        i = v.tensor_scalar(out=QT[:, h * 512:h * 512 + 512], in0=CT1,
                                            scalar1=-128.0, scalar2=127.0,
                                            op0=ALU.max, op1=ALU.min)
                        if qb == 0 and h == 0:
                            dtap(v, 2752, QT[:, 0:64])
                        b.mark(i, "ve")
                    b.ve(vxi2)

                wov = g["wot"].rearrange("(j p) c -> p j c", p=128)
                VQOM = []
                for ob in range(16):
                    def glo(ge, ob=ob, n=len(VQOM)):
                        if n >= 2:
                            ge.wait_ge(b.sem("ve"), VQOM[n - 2])
                        else:
                            ge.wait_ge(b.sem("ve"), b.val("ve"))
                        b.dma(ge, WF4[ob % 2][:].rearrange("p (j c) -> p j c", c=128),
                              wov[:, :, ob * 128:ob * 128 + 128], "wld")
                    b.ge(glo)
                    vlo = b.val("wld")

                    def vqo(v, ob=ob, vlo=vlo):
                        v.wait_ge(b.sem("wld"), vlo)
                        if ob >= 1:
                            v.wait_ge(b.sem("te"), b.val("te"))
                        wf = WF4[ob % 2]
                        gb = GSW[3][:, 0:1].broadcast_to((128, 2048))
                        v.tensor_tensor(out=wf[:], in0=wf[:], in1=gb, op=ALU.mult)
                        v.tensor_scalar(out=wf[:], in0=wf[:], scalar1=MAGIC, scalar2=MAGIC,
                                        op0=ALU.add, op1=ALU.subtract)
                        i = v.tensor_scalar(out=WQ4[:], in0=wf[:], scalar1=-1.0, scalar2=1.0,
                                            op0=ALU.max, op1=ALU.min)
                        VQOM.append(b.mark(i, "ve"))
                    b.ve(vqo)
                    vqom = b.val("ve")

                    def tout(t, ob=ob, vqom=vqom):
                        t.wait_ge(b.sem("ve"), vqom)
                        for i_ in range(NT):
                            i = t.matmul(PS_B[:, (ob % 2) * 512:(ob % 2) * 512 + 512],
                                         WQ4[:, i_ * 128:i_ * 128 + 128],
                                         QT[:, i_ * 512:i_ * 512 + 512],
                                         start=(i_ == 0), stop=(i_ == NT - 1))
                        b.mark(i, "te")
                    b.te(tout)
                    vto = b.val("te")

                    def vevo(v, ob=ob, vto=vto):
                        v.wait_ge(b.sem("te"), vto)
                        v.wait_ge(b.sem("st"), b.val("st"))
                        i = v.tensor_tensor(out=CT1,
                                            in0=PS_B[:, (ob % 2) * 512:(ob % 2) * 512 + 512],
                                            in1=SOIW, op=ALU.mult)
                        b.mark(i, "ve")
                    b.ve(vevo)
                    vevom = b.val("ve")

                    def syo(sy, ob=ob, qb=qb, vevom=vevom):
                        sy.wait_ge(b.sem("ve"), vevom)
                        b.dma(sy, g["ddacc"][ob * 128:ob * 128 + 128, qb * 512:qb * 512 + 512],
                              CT1, "st")
                    b.sy(syo)
                b.barrier()

            h_a4, v_a4 = b.bump("ge")

            def gacw4(ge):
                ge.wait_ge(b.sem("st"), b.val("st"))
                YB4 = arv(0, 16 * TC)
                b.dma(ge, YB4.rearrange("p (t c) -> p t c", t=NT),
                      g["ddacc"].rearrange("(t p) c -> p t c", p=128), "wld")
                ge.wait_ge(b.sem("wld"), b.val("wld"))
                ge.nop().then_inc(h_a4, 1)
            b.ge(gacw4)
            YB4 = arv(0, 16 * TC)

            def yb4_t(tj, half):
                o = tj * TC + half * 512
                return YB4[0:128, o:o + 512]

            for en in ("vector", "scalar", "tensor"):
                def wv4b(eng, en=en):
                    eng.wait_ge(h_a4, v_a4)
                getattr(b.block, en)(wv4b)

            for half in range(2):
                mrep, rrep, vrr = ln_stats(lambda tj, h=half: yb4_t(tj, h), half)

                def vres4(v, half=half, mrep=mrep, rrep=rrep, vrr=vrr):
                    v.wait_ge(b.sem("ve"), vrr)
                    tmp = TMPS[:, half * 512:half * 512 + 512]
                    for tj in range(NT):
                        xsl = X[:, tj * TC + half * 512:tj * TC + half * 512 + 512]
                        v.tensor_tensor(out=tmp, in0=yb4_t(tj, half), in1=mrep, op=ALU.subtract)
                        v.tensor_tensor(out=tmp, in0=tmp, in1=rrep, op=ALU.mult)
                        i = v.tensor_tensor(out=xsl, in0=xsl, in1=tmp, op=ALU.add)
                    b.mark(i, "ve")
                b.ve(vres4)
            b.barrier()

        if PHASES >= 5:
            XL5 = arv(0, 16 * TC)

            def xl5_t(tj, half):
                o = tj * TC + half * 512
                return XL5[0:128, o:o + 512]

            for half in range(2):
                def vcast8(v, half=half):
                    for tj in range(NT):
                        i = v.tensor_copy(out=xl5_t(tj, half),
                                          in_=X[:, tj * TC + half * 512:tj * TC + half * 512 + 512])
                    b.mark(i, "ve")
                b.ve(vcast8)
                mrep, rrep, vrr = ln_stats(lambda tj, h=half: xl5_t(tj, h), half)

                def vnorm8(v, half=half, mrep=mrep, rrep=rrep, vrr=vrr):
                    v.wait_ge(b.sem("ve"), vrr)
                    tmp = TMPS[:, half * 512:half * 512 + 512]
                    for tj in range(NT):
                        xsl = X[:, tj * TC + half * 512:tj * TC + half * 512 + 512]
                        v.tensor_tensor(out=tmp, in0=xsl, in1=mrep, op=ALU.subtract)
                        i = v.tensor_tensor(out=xl5_t(tj, half), in0=tmp, in1=rrep, op=ALU.mult)
                    b.mark(i, "ve")
                b.ve(vnorm8)
            b.barrier()

            WG5 = [arv(32, 2048), arv(36, 2048)]
            WU5 = [arv(40, 2048), arv(44, 2048)]
            SG5 = arv(48, 512, F32)
            TS5 = arv(50, 512, F32)
            HID5 = arv(52, 16 * 1024)
            WD5 = [arv(84, 2048), arv(88, 2048)]
            ESB5 = [arv(92, 1024, F32), arv(96, 1024, F32)]
            wgv = g["wgt"].rearrange("(j p) c -> p j c", p=128)
            wuv = g["wut"].rearrange("(j p) c -> p j c", p=128)
            wdv = g["wdnt"].rearrange("(j p) c -> p j c", p=128)
            tg_marks = []
            for ch in range(4):
                for oi in range(16):
                    ob = ch * 16 + oi

                    def gl8(ge, ob=ob, n=len(tg_marks)):
                        if n >= 2:
                            ge.wait_ge(b.sem("te"), tg_marks[n - 2])
                        b.dma(ge, WG5[ob % 2][:].rearrange("p (j c) -> p j c", c=128),
                              wgv[:, :, ob * 128:ob * 128 + 128], "wld")
                        b.dma(ge, WU5[ob % 2][:].rearrange("p (j c) -> p j c", c=128),
                              wuv[:, :, ob * 128:ob * 128 + 128], "wld")
                    b.ge(gl8)
                    vl8 = b.val("wld")

                    def tgu(t, ob=ob, vl8=vl8):
                        t.wait_ge(b.sem("wld"), vl8)
                        t.wait_ge(b.sem("se"), b.val("se"))
                        t.wait_ge(b.sem("ve"), b.val("ve"))
                        for half in range(2):
                            for i_ in range(NT):
                                t.matmul(PS_A[:, half * 512:half * 512 + 512],
                                         WG5[ob % 2][:, i_ * 128:i_ * 128 + 128],
                                         xl5_t(i_, half), start=(i_ == 0), stop=(i_ == NT - 1))
                            for i_ in range(NT):
                                i = t.matmul(PS_A[:, 1024 + half * 512:1024 + half * 512 + 512],
                                             WU5[ob % 2][:, i_ * 128:i_ * 128 + 128],
                                             xl5_t(i_, half), start=(i_ == 0), stop=(i_ == NT - 1))
                        tg_marks.append(b.mark(i, "te"))
                    b.te(tgu)
                    vtg = b.val("te")

                    for half in range(2):
                        def ssig(sc, vtg=vtg, half=half):
                            if half == 0:
                                sc.wait_ge(b.sem("te"), vtg)
                            if half == 1:
                                sc.wait_ge(b.sem("ve"), b.val("ve"))
                            i = sc.activation(out=SG5, in_=PS_A[:, half * 512:half * 512 + 512],
                                              func=AF.Sigmoid)
                            b.mark(i, "se")
                        b.se(ssig)
                        vsg8 = b.val("se")

                        def vhid8(v, oi=oi, half=half, vsg8=vsg8):
                            v.wait_ge(b.sem("se"), vsg8)
                            v.tensor_tensor(out=TS5, in0=PS_A[:, half * 512:half * 512 + 512],
                                            in1=SG5, op=ALU.mult)
                            i = v.tensor_tensor(
                                out=HID5[:, oi * 1024 + half * 512:oi * 1024 + half * 512 + 512],
                                in0=PS_A[:, 1024 + half * 512:1024 + half * 512 + 512],
                                in1=TS5, op=ALU.mult)
                            b.mark(i, "ve")
                        b.ve(vhid8)

                TD8M = []
                ACV8 = []
                for dt in range(16):
                    def gl8d(ge, ch=ch, dt=dt, n=len(TD8M)):
                        if dt >= 2:
                            ge.wait_ge(b.sem("te"), TD8M[n - 2])
                        else:
                            ge.wait_ge(b.sem("te"), b.val("te"))
                        b.dma(ge, WD5[dt % 2][:].rearrange("p (j c) -> p j c", c=128),
                              wdv[:, ch * 16:ch * 16 + 16, dt * 128:dt * 128 + 128], "wld")
                    b.ge(gl8d)
                    vl8d = b.val("wld")

                    def tdn8(t, dt=dt, vl8d=vl8d):
                        t.wait_ge(b.sem("wld"), vl8d)
                        t.wait_ge(b.sem("ve"), b.val("ve"))
                        for half in range(2):
                            for ii in range(16):
                                i = t.matmul(PS_B[:, half * 512:half * 512 + 512],
                                             WD5[dt % 2][:, ii * 128:ii * 128 + 128],
                                             HID5[:, ii * 1024 + half * 512:ii * 1024 + half * 512 + 512],
                                             start=(ii == 0), stop=(ii == 15))
                        TD8M.append(b.mark(i, "te"))
                    b.te(tdn8)
                    vtd8 = b.val("te")

                    def vev8(v, vtd8=vtd8, dt=dt, n=len(ACV8)):
                        v.wait_ge(b.sem("te"), vtd8)
                        if n >= 2:
                            v.wait_ge(b.sem("ac"), ACV8[n - 2])
                        i = v.tensor_copy(out=ESB5[dt % 2], in_=PS_B[:, 0:1024])
                        b.mark(i, "ve")
                    b.ve(vev8)
                    vem8 = b.val("ve")

                    def gac8(ge, ch=ch, dt=dt, vem8=vem8):
                        ge.wait_ge(b.sem("ve"), vem8)
                        if ch == 0:
                            ACV8.append(b.dma(ge, g["ddacc"][dt * 128:dt * 128 + 128, :],
                                              ESB5[dt % 2], "ac"))
                        else:
                            h2, v2 = b.bump("ac", 16)
                            ge.dma_start(out=g["ddacc"][dt * 128:dt * 128 + 128, :],
                                         in_=ESB5[dt % 2],
                                         accum_op=ALU.add).then_inc(h2, 16)
                            ACV8.append(v2)
                    b.ge(gac8)

                def gacw8(ge):
                    ge.wait_ge(b.sem("ac"), b.val("ac"))
                b.ge(gacw8)
                b.barrier(("vector", "tensor", "scalar", "gpsimd"))

            h_a5, v_a5 = b.bump("ge")

            def gacw5(ge):
                ge.wait_ge(b.sem("ac"), b.val("ac"))
                YB5 = arv(0, 16 * TC)
                b.dma(ge, YB5.rearrange("p (t c) -> p t c", t=NT),
                      g["ddacc"].rearrange("(t p) c -> p t c", p=128), "wld")
                ge.wait_ge(b.sem("wld"), b.val("wld"))
                ge.nop().then_inc(h_a5, 1)
            b.ge(gacw5)
            YB5 = arv(0, 16 * TC)

            def yb5_t(tj, half):
                o = tj * TC + half * 512
                return YB5[0:128, o:o + 512]

            for en in ("vector", "scalar", "tensor"):
                def wv5b(eng, en=en):
                    eng.wait_ge(h_a5, v_a5)
                getattr(b.block, en)(wv5b)

            for half in range(2):
                mrep, rrep, vrr = ln_stats(lambda tj, h=half: yb5_t(tj, h), half)

                def vres5(v, half=half, mrep=mrep, rrep=rrep, vrr=vrr):
                    v.wait_ge(b.sem("ve"), vrr)
                    tmp = TMPS[:, half * 512:half * 512 + 512]
                    for tj in range(NT):
                        xsl = X[:, tj * TC + half * 512:tj * TC + half * 512 + 512]
                        v.tensor_tensor(out=tmp, in0=yb5_t(tj, half), in1=mrep, op=ALU.subtract)
                        v.tensor_tensor(out=tmp, in0=tmp, in1=rrep, op=ALU.mult)
                        i = v.tensor_tensor(out=xsl, in0=xsl, in1=tmp, op=ALU.add)
                    b.mark(i, "ve")
                b.ve(vres5)
            b.barrier()
        def pout(sy):
            sy.wait_ge(b.sem("ve"), b.val("ve"))
            v = b.dma(sy, g["yt"].rearrange("(t p) c -> p t c", p=128),
                      X[:].rearrange("p (t c) -> p t c", t=NT), "st")
            if DEBUG:
                v = b.dma(sy, g["dbg"], DBGT[:], "st")
            sy.wait_ge(b.sem("st"), v)
        b.sy(pout)

    return nc


_NC_CACHE = {}


def kernel(**inputs):
    x = np.asarray(inputs["L0_attn_x"][0])
    xT = np.ascontiguousarray(x.T)
    eye = np.eye(128, dtype=np.float32)
    ones = np.ones((128, 128), dtype=np.float32)

    consts = np.zeros((128, 8), dtype=np.float32)
    for ci, cv in enumerate([EPS, 1.0 / np.sqrt(128.0), 1.0, 0.0]):
        consts[:, ci] = np.float32(cv)
    shared = {
        "wx": np.ascontiguousarray(np.asarray(inputs["L0_attn_W_xproj"]).T),
        "ones": ones, "ident": eye, "consts": consts,
    }
    if PHASES >= 3:
        shared["w1t"] = np.ascontiguousarray(np.asarray(inputs["L0_ffn_w1"]).T)
        shared["w2t"] = np.ascontiguousarray(np.asarray(inputs["L0_ffn_w2"]).T)
        shared["w3t"] = np.ascontiguousarray(np.asarray(inputs["L0_ffn_w3"]).T)
    if PHASES >= 4:
        shared["wqt"] = np.ascontiguousarray(np.asarray(inputs["L1_attn_w_q"]).T)
        shared["wkt"] = np.ascontiguousarray(np.asarray(inputs["L1_attn_w_k"]).T)
        shared["wvt"] = np.ascontiguousarray(np.asarray(inputs["L1_attn_w_v"]).T)
        shared["wot"] = np.ascontiguousarray(np.asarray(inputs["L1_attn_w_o"]).T)
        eh = np.zeros((128, 256), dtype=np.float32)
        for hh in range(16):
            eh[:, hh * 16 + hh] = 1.0
        shared["eh16"] = eh
    if PHASES >= 5:
        shared["wgt"] = np.ascontiguousarray(np.asarray(inputs["L1_ffn_w_gate"]).T)
        shared["wut"] = np.ascontiguousarray(np.asarray(inputs["L1_ffn_w_up"]).T)
        shared["wdnt"] = np.ascontiguousarray(np.asarray(inputs["L1_ffn_w_down"]).T)
    wdtT = np.ascontiguousarray(np.asarray(inputs["L0_attn_W_dt"]).T)

    in_maps = []
    idxs = []
    for c in range(NCORES):
        ix = tok_idx(c)
        idxs.append(ix)
        m = dict(shared)
        m["xt"] = np.ascontiguousarray(xT[:, ix])
        m["xd"] = np.ascontiguousarray(xT[DSL * c:DSL * c + DSL, :])
        m["wdt"] = np.ascontiguousarray(wdtT[:, DSL * c:DSL * c + DSL])
        m["bdt"] = np.ascontiguousarray(
            np.asarray(inputs["L0_attn_b_dt"])[DSL * c:DSL * c + DSL].reshape(2, 128).T)
        m["asl"] = np.ascontiguousarray(
            np.asarray(inputs["L0_attn_A"])[DSL * c:DSL * c + DSL].reshape(2, 128, NST)
            .transpose(1, 0, 2).reshape(128, 2 * NST))
        m["dsl"] = np.ascontiguousarray(
            np.asarray(inputs["L0_attn_D"])[DSL * c:DSL * c + DSL].reshape(2, 128).T)
        if PHASES >= 4:
            mk = np.full((8, 4, 128, 512), NEG, dtype=np.float32)
            o = np.arange(64)
            diagT = np.where(o[:, None] <= o[None, :], 0.0, NEG).astype(np.float32)
            for r in range(8):
                for s2 in range(4):
                    for kk2 in range(2):
                        sbl = s2 * 2 + kk2
                        rows = slice(kk2 * 64, kk2 * 64 + 64)
                        mk[r, s2, rows, 64 * (sbl + 1):] = 0.0
                        if r < c:
                            mk[r, s2, rows, 64 * sbl:64 * sbl + 64] = 0.0
                        elif r == c:
                            mk[r, s2, rows, 64 * sbl:64 * sbl + 64] = diagT
            m["masks"] = np.ascontiguousarray(mk.reshape(32, 128, 512))
        in_maps.append(m)

    if PHASES not in _NC_CACHE:
        _NC_CACHE[PHASES] = build_program()
    nc = _NC_CACHE[PHASES]
    res = run_bass_kernel_spmd(nc, in_maps, core_ids=list(range(NCORES)))

    out = np.zeros((S, D), dtype=np.float32)
    for c in range(NCORES):
        out[idxs[c], :] = res.results[c]["yt"].T
    return out[None]
